# version 10
# baseline (speedup 1.0000x reference)
import math
import numpy as np
import ml_dtypes
import concourse.bass as bass
import concourse.mybir as mybir
from contextlib import ExitStack
from concourse.bass_utils import run_bass_kernel_spmd

F32 = mybir.dt.float32
BF16 = mybir.dt.bfloat16
AF = mybir.ActivationFunctionType
ALU = mybir.AluOpType
AX = mybir.AxisListType
BF = ml_dtypes.bfloat16

D = 2048
INTER = 5632
B = 2
S = 4096
TPC = 1024
NT = 8
NH = 16
DKV1 = 512
HD = 128
NKC = 16
NIC = 44
MARGIN = 44.0

ENGINES = ("tensor", "vector", "scalar", "sync", "gpsimd")
PRECAST = True
PRECAST_CONSUME = True


def cdiv(a, b):
    return (a + b - 1) // b


class PB:

    def __init__(self):
        self.items = {e: [] for e in ENGINES}
        self.label_count = {}
        self.phase_deps = []

    def add(self, eng, emit, deps=(), dma=False, phased=True, sem=None):
        ds = [d for d in deps if d is not None]
        if phased:
            ds = ds + self.phase_deps
        if dma:
            assert sem is not None
            k = self.label_count.get(sem, 0)
            self.label_count[sem] = k + 1
            tokn = ("D", sem, k)
            if k > 0:
                ds.append(("D", sem, k - 1))
        else:
            tokn = ("E", eng, len(self.items[eng]))
        self.items[eng].append({"emit": emit, "deps": ds, "dma": dma,
                                "marked": dma, "sem": sem, "tok": tokn})
        return tokn

    def last(self, eng):
        for it in reversed(self.items[eng]):
            if not it["dma"]:
                return it["tok"]
        return None

    def barrier(self, engs=("tensor", "vector", "scalar")):
        self.phase_deps = [t for t in (self.last(e) for e in engs) if t is not None]

    def render(self, nc, block, sems, lsems):
        eidx = {}
        for e in ENGINES:
            for i, it in enumerate(self.items[e]):
                eidx[("E", e, i)] = (e, i)
        for e in ENGINES:
            for it in self.items[e]:
                for d in it["deps"]:
                    if d[0] == "E":
                        e2, i2 = eidx[d]
                        self.items[e2][i2]["marked"] = True
        for e in ("vector", "scalar"):
            for it in self.items[e]:
                if not it["dma"]:
                    it["marked"] = True
        semval = {}
        for e in ENGINES:
            v, vals = 0, []
            for it in self.items[e]:
                if it["marked"] and not it["dma"]:
                    v += 1
                vals.append(v)
            semval[e] = vals
        tot = {lbl: 16 * n for lbl, n in self.label_count.items()}

        def runner(ename):
            auto_serial = ename in ("vector", "scalar")
            def body(eng):
                hw = {}
                nprev = 0
                for it in self.items[ename]:
                    if auto_serial and not it["dma"] and nprev > 0:
                        if nprev > hw.get(ename, 0):
                            eng.wait_ge(sems[ename], nprev)
                            hw[ename] = nprev
                    for d in it["deps"]:
                        if d[0] == "E":
                            e2, i2 = eidx[d]
                            if e2 == ename:
                                continue
                            key, v = e2, semval[e2][i2]
                            sm = sems[e2]
                        else:
                            _, lbl, k = d
                            key = "D" + lbl
                            v = 16 * (k + 1)
                            sm = lsems[lbl]
                        if v > hw.get(key, 0):
                            eng.wait_ge(sm, v)
                            hw[key] = v
                    inst = it["emit"](eng)
                    if it["dma"]:
                        inst.then_inc(lsems[it["sem"]], 16)
                    elif it["marked"]:
                        assert inst is not None
                        inst.then_inc(sems[ename], 1)
                        nprev = semval[ename][self.items[ename].index(it)] if False else nprev + 1
            return body

        block.tensor(runner("tensor"))
        block.vector(runner("vector"))
        block.scalar(runner("scalar"))
        block.sync(runner("sync"))
        block.gpsimd(runner("gpsimd"))


class Leg:
    def __init__(self, leg, dbg=False):
        self.dbg = dbg
        self.leg = leg
        self.nc = bass.Bass("TRN2", target_bir_lowering=False,
                            dynamic_dma_scratch_size=24576)
        self.pb = PB()
        self.t = {}
        self.tok = {}
        self.m = {}
        self.off = {"f": 0, "b": 0}

    def res(self, space, name, cols, at=None, at_off=0):
        if at is not None:
            o = self.m[at][1] + at_off
        else:
            o = self.off[space]
            self.off[space] += cols
        self.m[name] = (space, o, cols)

    def V(self, name, p0=0, p1=128, c0=0, c1=None):
        sp, off, cols = self.m[name]
        if c1 is None:
            c1 = cols
        a = self.f32a if sp == "f" else self.bfa
        return a[p0:p1, off + c0:off + c1]

    def din(self, name, shape, dt=F32):
        self.t[name] = self.nc.dram_tensor(name, list(shape), dt, kind="ExternalInput").ap()

    def dout(self, name, shape, dt=F32):
        self.t[name] = self.nc.dram_tensor(name, list(shape), dt, kind="ExternalOutput").ap()


    def one_shot(self, name, emit, eng="sync"):
        lbl = "ld" if eng == "sync" else "ldg"
        self.tok[name] = self.pb.add(eng, emit, dma=True, phased=False, sem=lbl)

    def consts(self):
        V, t = self.V, self.t
        self.one_shot("onesr", lambda e: e.dma_start(out=V("onesr", 0, 1), in_=t["ones_row"]))
        self.one_shot("halfr", lambda e: e.dma_start(out=V("halfr", 0, 1), in_=t["half_row"]))
        self.one_shot("onesc", lambda e: e.dma_start(out=V("onesc"), in_=t["ones_col"]))
        self.one_shot("onesF", lambda e: e.dma_start(out=V("onesF"), in_=t["ones_full"]))
        self.one_shot("ident", lambda e: e.dma_start(out=V("ident"), in_=t["ident"]))
        self.one_shot("x", lambda e: e.dma_start(
            out=V("x").rearrange("p (t d) -> p t d", t=NT),
            in_=t["x_in"].rearrange("(t p) d -> p t d", p=128)))
        if self.leg in ("A", "B"):
            self.one_shot("gs", lambda e: e.dma_start(out=V("gsc"), in_=t["gs"]))
            self.one_shot("us", lambda e: e.dma_start(out=V("usc"), in_=t["us"]))
            self.one_shot("ds", lambda e: e.dma_start(out=V("ds_bc", 0, 1, 0, D), in_=t["ds_row"]))
        if self.leg == "A":
            self.one_shot("b_wk", lambda e: e.dma_start(out=V("bstat", c0=0, c1=NKC), in_=t["bk"]))
            self.one_shot("b_wq", lambda e: e.dma_start(out=V("bstat", c0=16, c1=16 + NKC), in_=t["bq"]))
        if self.leg == "B":
            self.one_shot("b_wk1", lambda e: e.dma_start(out=V("bstat", c0=0, c1=4), in_=t["bk1"]))
            self.one_shot("b_wq1", lambda e: e.dma_start(out=V("bstat", c0=16, c1=16 + NKC), in_=t["bq1"]))
        if self.leg in ("B", "C"):
            self.one_shot("QT_ld0", lambda e: e.dma_start(
                out=V("QT").rearrange("p (m q) -> p m q", m=NKC),
                in_=t["QT_in"].rearrange("(m p) q -> p m q", p=128)))
            self.one_shot("bo", lambda e: e.dma_start(
                out=V("browB", 0, 1, 0, D), in_=t["bo_row"]), eng="gpsimd")
        if self.leg == "C":
            self.one_shot("cm0", lambda e: e.dma_start(
                out=V("cm").rearrange("p (k q) -> p k q", k=32),
                in_=t["cmask"].rearrange("k p q -> p k q")))
            self.one_shot("ksel", lambda e: e.dma_start(
                out=V("kselb", 0, 1), in_=t["ksel"]))

    def load_row_bf(self, src, dst_c0, cols, deps=(), sem="rowb"):
        pb, V, t = self.pb, self.V, self.t
        return pb.add("gpsimd", lambda e: e.dma_start(
            out=V("browB", 0, 1, dst_c0, dst_c0 + cols), in_=t[src]),
            dma=True, phased=False, deps=list(deps), sem=sem)

    def bcast_hilo(self, src_name, src_p, src_c0, width, dst_name, dst_c0, psum, deps,
                   extra_mm_deps=()):
        pb, V = self.pb, self.V
        p0, p1 = src_p, src_p + 1
        src = lambda: V(src_name, p0, p1, src_c0, src_c0 + width)
        hi = pb.add("vector", lambda e: e.tensor_copy(
            V("hilo", p0, p1, 0, width), src()), deps=list(deps))
        lo1 = pb.add("vector", lambda e: e.tensor_tensor(
            V("hiloF", p0, p1, 0, width), src(), V("hilo", p0, p1, 0, width), ALU.subtract),
            deps=[hi])
        lo = pb.add("vector", lambda e: e.tensor_copy(
            V("hilo", p0, p1, width, 2 * width), V("hiloF", p0, p1, 0, width)), deps=[lo1])
        ones = (lambda: V("onesr", 0, 1)) if src_p == 0 else (lambda: V("onesF", p0, p1))
        mms = []
        for cc in range(0, width, 512):
            ce = min(width, cc + 512)
            pb.add("tensor", lambda e, cc=cc, ce=ce: e.matmul(
                psum[:, cc:ce], ones(), V("hilo", p0, p1, cc, ce),
                start=True, stop=False), deps=[hi, self.tok["onesr"], self.tok["onesF"]] + list(extra_mm_deps))
            mms.append(pb.add("tensor", lambda e, cc=cc, ce=ce: e.matmul(
                psum[:, cc:ce], ones(), V("hilo", p0, p1, width + cc, width + ce),
                start=False, stop=True), deps=[lo]))
        return pb.add("vector", lambda e: e.tensor_copy(
            V(dst_name, 0, 128, dst_c0, dst_c0 + width), psum[:, 0:width]), deps=mms)

    def emit_ln(self, tag):
        pb, V = self.pb, self.V
        tok = self.tok
        sq_toks, mean_toks = [], []
        for tt in range(NT):
            sq_toks.append(pb.add("scalar", lambda e, tt=tt: e.activation(
                V("sq"), V("x", c0=tt * D, c1=(tt + 1) * D), AF.Square,
                accum_out=V("stat", c0=8 + tt, c1=9 + tt)), deps=[tok["x"]]))
            mean_toks.append(pb.add("vector", lambda e, tt=tt: e.tensor_reduce(
                V("stat", c0=tt, c1=tt + 1), V("x", c0=tt * D, c1=(tt + 1) * D),
                AX.X, ALU.add), deps=[tok["x"]]))
        tmean = pb.add("vector", lambda e: e.tensor_scalar(
            V("stat", c0=16, c1=24), V("stat", c0=0, c1=8), -1.0 / D, None, ALU.mult),
            deps=mean_toks)
        tmsq = pb.add("vector", lambda e: e.tensor_tensor(
            V("stat", c0=24, c1=32), V("stat", c0=16, c1=24), V("stat", c0=16, c1=24), ALU.mult),
            deps=[tmean])
        ts2 = pb.add("vector", lambda e: e.tensor_scalar(
            V("stat", c0=32, c1=40), V("stat", c0=8, c1=16), 1.0 / D, None, ALU.mult),
            deps=sq_toks)
        tvar = pb.add("vector", lambda e: e.tensor_tensor(
            V("stat", c0=40, c1=48), V("stat", c0=32, c1=40), V("stat", c0=24, c1=32), ALU.subtract),
            deps=[tmsq, ts2])
        tsd = pb.add("scalar", lambda e: e.activation(
            V("stat", c0=48, c1=56), V("stat", c0=40, c1=48), AF.Sqrt, bias=np.float32(1e-5)), deps=[tvar])
        trstd = pb.add("vector", lambda e: e.reciprocal(
            V("stat", c0=56, c1=64), V("stat", c0=48, c1=56)), deps=[tsd])

        copy_toks = {}
        for tt in range(NT):
            deps = [trstd, tok.get(("stageW", 0)), tok.get(("stageW", 1))]
            if tt >= 1:
                deps.append(copy_toks[(tt - 1, 1)])
            xn_t = pb.add("vector", lambda e, tt=tt: e.tensor_scalar(
                V("xn"), V("x", c0=tt * D, c1=(tt + 1) * D),
                V("stat", c0=16 + tt, c1=17 + tt), V("stat", c0=56 + tt, c1=57 + tt),
                ALU.add, ALU.mult), deps=deps)
            for rnd in range(2):
                tps = []
                for jj in range(8):
                    j = rnd * 8 + jj
                    d2 = [xn_t, tok["ident"]]
                    if jj == 0:
                        prev = (tt, 0) if rnd == 1 else ((tt - 1, 1) if tt >= 1 else None)
                        if prev in copy_toks:
                            d2.append(copy_toks[prev])
                    tps.append(pb.add("tensor", lambda e, j=j, jj=jj: e.transpose(
                        self.ps_tb[:, jj * 128:(jj + 1) * 128],
                        V("xn", c0=j * 128, c1=(j + 1) * 128),
                        V("ident")), deps=d2))
                sp, xoff, _ = self.m["xnT"]
                a = self.f32a if sp == "f" else self.bfa
                def emc(e, tt=tt, rnd=rnd, xoff=xoff, a=a):
                    dst = a[:, xoff + rnd * 8 * TPC: xoff + (rnd + 1) * 8 * TPC]
                    dstv = dst.rearrange("p (j q) -> p j q", j=8)[:, :, tt * 128:(tt + 1) * 128]
                    return e.tensor_copy(dstv, self.ps_tb[:, :].rearrange("p (j q) -> p j q", j=8))
                copy_toks[(tt, rnd)] = pb.add("vector", emc, deps=[tps[-1]])
        tok["xnT_done"] = list(copy_toks.values())
        pb.barrier()

    def xnT(self, kc, q0, q1):
        return self.V("xnT", c0=kc * TPC + q0, c1=kc * TPC + q1)

    def emit_ffn0(self):
        pb, V, t, tok = self.pb, self.V, self.t, self.tok
        pre = self.leg == "B" and PRECAST and PRECAST_CONSUME
        gu_src = t["guB"] if pre else t["guA"]
        d_src = t["dB"] if pre else t["dA"]
        w_eng = "sync" if pre else "gpsimd"
        gs_t = tok["gs"]
        us_t = tok["us"]
        ds_ld = tok["ds"]
        ds_half = pb.add("vector", lambda e: e.tensor_scalar(
            V("ds_bc", 0, 1, 0, D), V("ds_bc", 0, 1, 0, D), 0.5, None, ALU.mult),
            deps=[ds_ld] + list(self.pb.phase_deps))
        dsbc = ds_half
        for qd in range(4):
            dsbc = self.bcast_hilo("ds_bc", 0, qd * 512, 512, "ds_bc2", qd * 512,
                                   self.ps_a, [ds_half, dsbc])

        hdn_tok = {}
        gu_dma = {}
        cons_gu = {}
        epi_done = {}
        dn_ev = {}
        gcons = {}
        def emit_gu_dma(half, ic):
            if ic >= NIC:
                return
            sl = ic % 2
            war = [cons_gu.get((half, ic - 2))]
            if half == 1 and ic < 2:
                war.append(cons_gu.get((0, NIC - 2 + ic)))
            if half == 0 and ic < 2:
                war = war + tok["xnT_done"] + [tok.get("wgu_gate")]
            if pre and ic < 2 and half == 0:
                war = war + (tok.get("precast") or [None])
            gu_dma[(half, ic)] = pb.add(w_eng, lambda e, ic=ic, sl=sl: e.dma_start(
                out=V("wgu", c0=sl * 2 * D, c1=(sl + 1) * 2 * D), in_=gu_src[ic]),
                deps=war, dma=True, phased=False, sem=f"gu{sl}")

        for half in range(2):
            q0, q1 = half * 512, (half + 1) * 512
            if half == 0:
                emit_gu_dma(0, 0)
                emit_gu_dma(0, 1)
            for ic in range(NIC):
                sl = ic % 2
                psl = ic % 2
                mg = mu = None
                d_first = [gu_dma[(half, ic)], epi_done.get((half, ic - 2)), dsbc] + tok["xnT_done"]
                if half == 1 and ic < 2:
                    d_first.append(epi_done.get((0, NIC - 2 + ic)))
                    d_first += [dn_ev.get((0, 3, tx)) for tx in range(4)]
                for kc in range(NKC):
                    mg = pb.add("tensor", lambda e, sl=sl, kc=kc, psl=psl, q0=q0, q1=q1: e.matmul(
                        self.ps_a[:, psl * 1024: psl * 1024 + 512],
                        V("wgu", c0=sl * 2 * D + kc * 128, c1=sl * 2 * D + (kc + 1) * 128),
                        self.xnT(kc, q0, q1),
                        start=(kc == 0), stop=(kc == NKC - 1)),
                        deps=(d_first if kc == 0 else []))
                for kc in range(NKC):
                    mu = pb.add("tensor", lambda e, sl=sl, kc=kc, psl=psl, q0=q0, q1=q1: e.matmul(
                        self.ps_a[:, psl * 1024 + 512: psl * 1024 + 1024],
                        V("wgu", c0=sl * 2 * D + D + kc * 128, c1=sl * 2 * D + D + (kc + 1) * 128),
                        self.xnT(kc, q0, q1),
                        start=(kc == 0), stop=(kc == NKC - 1)), deps=[])
                cons_gu[(half, ic)] = mu
                if ic + 2 < NIC:
                    emit_gu_dma(half, ic + 2)
                elif half == 0:
                    emit_gu_dma(1, ic + 2 - NIC)
                ty = pb.add("vector", lambda e, ic=ic, psl=psl: e.tensor_scalar(
                    V("scr0"), self.ps_a[:, psl * 1024: psl * 1024 + 512],
                    V("gsc", c0=ic, c1=ic + 1), None, ALU.mult), deps=[mg, gs_t])
                tsl = pb.add("scalar", lambda e: e.activation(V("sbf"), V("scr0"), AF.Silu),
                             deps=[ty, hdn_tok.get((half, ic - 1), hdn_tok.get((half - 1, NIC - 1)))])
                tu = pb.add("vector", lambda e, ic=ic, psl=psl: e.tensor_scalar(
                    V("scr1"), self.ps_a[:, psl * 1024 + 512: psl * 1024 + 1024],
                    V("usc", c0=ic, c1=ic + 1), None, ALU.mult), deps=[mu, us_t])
                hdn_tok[(half, ic)] = pb.add("vector", lambda e, ic=ic: e.tensor_tensor(
                    V("hdn", c0=ic * 512, c1=(ic + 1) * 512), V("sbf"), V("scr1"), ALU.mult),
                    deps=[tsl, tu])
                epi_done[(half, ic)] = hdn_tok[(half, ic)]
            hdeps = [hdn_tok[(half, ic)] for ic in range(NIC)]
            if self.dbg and half == 0:
                self.dout("d_hdn0", (128, NIC * 512), BF16)
                tk = pb.add("sync", lambda e: e.dma_start(
                    out=self.t["d_hdn0"], in_=V("hdn")),
                    deps=hdeps, dma=True, phased=False, sem="dbg")
                pb.add("sync", lambda e: None, deps=[tk], phased=False)
            groups = [self.ps_b[:, 0:512], self.ps_b[:, 512:1024],
                      self.ps_a[:, 0:512], self.ps_a[:, 512:1024]]
            dw_dma = {}

            def emit_dw_dma(half, seq):
                if seq >= 44:
                    return
                n2, j11 = seq // 11, seq % 11
                gseq = half * 44 + seq
                sl = gseq % 2
                war = [gcons.get(gseq - 2)]
                if gseq < 2:
                    war = war + tok["xnT_done"] + [tok.get("dwn_gate")]
                if pre and gseq < 2:
                    war = war + (tok.get("precast") or [None])
                dw_dma[(n2, j11)] = (sl, pb.add(w_eng, lambda e, n2=n2, j11=j11, sl=sl: e.dma_start(
                    out=V("dwn", c0=sl * 2048, c1=(sl + 1) * 2048).rearrange("p (g c) -> p g c", g=2),
                    in_=d_src[n2, 2 * j11:2 * j11 + 2].rearrange("g p c -> p g c")),
                    deps=war, dma=True, phased=False, sem=f"dw{sl}"))

            emit_dw_dma(half, 0)
            emit_dw_dma(half, 1)
            for n in range(4):
                last = {}
                for j11 in range(11):
                    gseq = half * 44 + n * 11 + j11
                    sl, dtok = dw_dma[(n, j11)]
                    for kc in range(4):
                        ic = j11 * 4 + kc
                        for t4 in range(4):
                            d0 = []
                            if ic == 0 and t4 == 0:
                                d0 = hdeps + [dtok] + [dn_ev.get((half, n - 1, tx)) for tx in range(4)]
                                if n == 0:
                                    d0 += [cons_gu[(half, NIC - 1)], dsbc]
                                    d0 += [dn_ev.get((half - 1, 3, tx)) for tx in range(4)]
                            elif kc == 0 and t4 == 0:
                                d0 = [dtok]
                            last[t4] = pb.add("tensor", lambda e, ic=ic, sl=sl, kc=kc, t4=t4, grp=groups[t4]: e.matmul(
                                grp,
                                V("hdn", c0=ic * 512 + t4 * 128, c1=ic * 512 + (t4 + 1) * 128),
                                V("dwn", c0=sl * 2048 + kc * 512, c1=sl * 2048 + (kc + 1) * 512),
                                start=(ic == 0), stop=(ic == NIC - 1)), deps=d0)
                    gcons[gseq] = last[3]
                    emit_dw_dma(half, n * 11 + j11 + 2)
                for t4 in range(4):
                    tt = half * 4 + t4
                    scr = "scr0" if t4 % 2 == 0 else "scr1"
                    tm = pb.add("vector", lambda e, n=n, scr=scr, grp=groups[t4]: e.tensor_tensor(
                        V(scr), grp, V("ds_bc2", c0=n * 512, c1=(n + 1) * 512), ALU.mult),
                        deps=[last[t4], dsbc, hdn_tok[(half, NIC - 1)]])
                    dn_ev[(half, n, t4)] = pb.add("vector", lambda e, tt=tt, n=n, scr=scr: e.tensor_tensor(
                        V("x", c0=tt * D + n * 512, c1=tt * D + (n + 1) * 512),
                        V(scr),
                        V("x", c0=tt * D + n * 512, c1=tt * D + (n + 1) * 512), ALU.add),
                        deps=[tm])
        tok["gu_last"] = cons_gu[(1, NIC - 1)]
        tok["dw_last"] = gcons[87]
        if self.dbg:
            self.dout("d_xh0", (128, NT * D), F32)
            tk = pb.add("sync", lambda e: e.dma_start(
                out=self.t["d_xh0"], in_=V("x")),
                deps=[dn_ev[(0, n2, t2)] for n2 in range(4) for t2 in range(4)],
                dma=True, phased=False, sem="dbg")
            pb.add("sync", lambda e: None, deps=[tk], phased=False)
        pb.barrier()

    def emit_proj_fm(self, w_name, b_name, nm, out_name, bias_c0, scale=None, rope=False):
        pb, V, t, tok = self.pb, self.V, self.t, self.tok
        bq = tok["b_" + w_name]
        if scale is not None:
            bq = pb.add("vector", lambda e: e.tensor_scalar(
                V("bstat", c0=bias_c0, c1=bias_c0 + nm),
                V("bstat", c0=bias_c0, c1=bias_c0 + nm), scale, None, ALU.mult),
                deps=[bq] + list(self.pb.phase_deps))
        wdma, cons, ev_tok, st_tok = {}, {}, {}, {}

        def emit_wdma(mm_):
            if mm_ >= nm:
                return
            sl = mm_ % 2
            war = [cons.get(mm_ - 2)]
            if mm_ < 2:
                war.append(tok.get("wproj_gate"))
            wdma[mm_] = pb.add("gpsimd", lambda e, m=mm_, sl=sl: e.dma_start(
                out=V("wproj", c0=sl * D, c1=(sl + 1) * D), in_=t[w_name][m]),
                deps=war, dma=True, phased=False, sem=f"wp{sl}")

        emit_wdma(0)
        emit_wdma(1)
        for m in range(nm):
            sl = m % 2
            psl = self.ps_b if m % 2 == 0 else self.ps_a
            mmt = None
            for qh in range(2):
                for kc in range(NKC):
                    d0 = ([wdma[m], ev_tok.get(m - 2)] + tok["xnT_done"]) if (qh == 0 and kc == 0) else []
                    mmt = pb.add("tensor", lambda e, m=m, sl=sl, kc=kc, qh=qh, psl=psl: e.matmul(
                        psl[:, qh * 512:(qh + 1) * 512],
                        V("wproj", c0=sl * D + kc * 128, c1=sl * D + (kc + 1) * 128),
                        self.xnT(kc, qh * 512, (qh + 1) * 512),
                        start=(kc == 0), stop=(kc == NKC - 1)), deps=d0)
            cons[m] = mmt
            emit_wdma(m + 2)
            ssl = m % 2
            sc = 1.0 if scale is None else scale
            slot_war = st_tok.get(m - 2, tok.get(("stageW", ssl)))
            if not rope:
                ev = pb.add("scalar", lambda e, m=m, psl=psl, ssl=ssl, sc=sc: e.activation(
                    V("stage", c0=ssl * D, c1=ssl * D + TPC), psl[:, 0:1024], AF.Identity,
                    bias=V("bstat", c0=bias_c0 + m, c1=bias_c0 + m + 1), scale=np.float32(sc)),
                    deps=[mmt, bq, slot_war])
            else:
                evf = pb.add("scalar", lambda e, m=m, psl=psl: e.activation(
                    V("qf"), psl[:, 0:1024], AF.Identity,
                    bias=V("bstat", c0=bias_c0 + m, c1=bias_c0 + m + 1), scale=np.float32(1.0)),
                    deps=[mmt, bq, slot_war, ev_tok.get(m - 1)])
                ev = self.emit_rope(evf, ssl, prev_ev=ev_tok.get(m - 1), slot_war=slot_war)
            ev_tok[m] = ev
            st_tok[m] = pb.add("sync", lambda e, m=m, ssl=ssl: e.dma_start(
                out=t[out_name][m * 128:(m + 1) * 128, :],
                in_=V("stage", c0=ssl * D, c1=ssl * D + TPC)),
                deps=[ev], dma=True, phased=False, sem=f"st{ssl}")
            tok[("stageW", ssl)] = st_tok[m]
        tok["fm_last"] = cons[nm - 1]
        for m in range(nm):
            self.pb.add("sync", lambda e: None, deps=[st_tok[m]], phased=False)
        pb.barrier()

    def emit_rope(self, evf, ssl, prev_ev=None, slot_war=None):
        pb, V = self.pb, self.V
        sw0 = pb.add("sync", lambda e: e.dma_start(
            out=V("qswap", 0, 64), in_=V("qf", 64, 128)), deps=[evf, prev_ev],
            dma=True, phased=False, sem="sw")
        sw1 = pb.add("sync", lambda e: e.dma_start(
            out=V("qswap", 64, 128), in_=V("qf", 0, 64)), deps=[evf, prev_ev],
            dma=True, phased=False, sem="sw")
        outs = []
        for hh in range(2):
            c0, c1 = hh * 512, (hh + 1) * 512
            ta = pb.add("vector", lambda e, c0=c0, c1=c1: e.tensor_tensor(
                V("scr0", 0, 64), V("qf", 0, 64, c0, c1), V("cs_c", 0, 64, c0, c1), ALU.mult),
                deps=[evf, self.tok["cs"], outs[-1] if outs else None])
            tb = pb.add("vector", lambda e, c0=c0, c1=c1: e.tensor_tensor(
                V("scr1", 0, 64), V("qswap", 0, 64, c0, c1), V("cs_s", 0, 64, c0, c1), ALU.mult),
                deps=[sw0, self.tok["cs"]])
            o1 = pb.add("vector", lambda e, c0=c0, c1=c1, ssl=ssl: e.tensor_tensor(
                V("stage", 0, 64, ssl * D + c0, ssl * D + c1),
                V("scr0", 0, 64), V("scr1", 0, 64), ALU.subtract), deps=[ta, tb, slot_war])
            tc = pb.add("vector", lambda e, c0=c0, c1=c1: e.tensor_tensor(
                V("scr0", 64, 128), V("qswap", 64, 128, c0, c1), V("cs_s", 64, 128, c0, c1), ALU.mult),
                deps=[sw1, o1])
            td = pb.add("vector", lambda e, c0=c0, c1=c1: e.tensor_tensor(
                V("scr1", 64, 128), V("qf", 64, 128, c0, c1), V("cs_c", 64, 128, c0, c1), ALU.mult),
                deps=[o1])
            o2 = pb.add("vector", lambda e, c0=c0, c1=c1, ssl=ssl: e.tensor_tensor(
                V("stage", 64, 128, ssl * D + c0, ssl * D + c1),
                V("scr0", 64, 128), V("scr1", 64, 128), ALU.add), deps=[tc, td])
            outs.append(o2)
        return outs[-1]

    def emit_proj_tm(self, w_name, wbuf, bias_tok, out_name, width):
        pb, V, t, tok = self.pb, self.V, self.t, self.tok
        nsl = width // 512
        cons_prev, ev, stt = {}, {}, {}
        for n in range(nsl):
            war = [cons_prev.get(n - 1)]
            if n == 0:
                war.append(tok.get("wvsl_gate"))
            wd = pb.add("gpsimd", lambda e, n=n: e.dma_start(
                out=V(wbuf, c0=0, c1=NKC * 512), in_=t[w_name][n]),
                deps=war, dma=True, phased=False, sem="wv0")
            for tt in range(NT):
                idx = n * NT + tt
                sl = idx % 2
                mmt = None
                prev_psum = ev.get(idx - 2)
                for kc in range(NKC):
                    d0 = ([wd, prev_psum] + tok["xnT_done"]) if kc == 0 else []
                    mmt = pb.add("tensor", lambda e, n=n, tt=tt, kc=kc: e.matmul(
                        self.ps_b[:, (tt % 2) * 512:(tt % 2) * 512 + 512],
                        self.xnT(kc, tt * 128, (tt + 1) * 128),
                        V(wbuf, c0=kc * 512, c1=(kc + 1) * 512),
                        start=(kc == 0), stop=False), deps=d0)
                mb = pb.add("tensor", lambda e, n=n, tt=tt: e.matmul(
                    self.ps_b[:, (tt % 2) * 512:(tt % 2) * 512 + 512],
                    V("onesr", 0, 1),
                    V("browB", 0, 1, n * 512, (n + 1) * 512),
                    start=False, stop=True), deps=[bias_tok, tok["onesr"]])
                cons_prev[n] = mb
                slot_war = stt.get(idx - 2, tok.get(("stageW", sl)))
                ev[idx] = pb.add("vector", lambda e, tt=tt, sl=sl: e.tensor_copy(
                    V("stage", c0=sl * 512, c1=(sl + 1) * 512),
                    self.ps_b[:, (tt % 2) * 512:(tt % 2) * 512 + 512]),
                    deps=[mb, slot_war])
                stt[idx] = pb.add("sync", lambda e, n=n, tt=tt, sl=sl: e.dma_start(
                    out=t[out_name][tt * 128:(tt + 1) * 128, n * 512:(n + 1) * 512],
                    in_=V("stage", c0=sl * 512, c1=(sl + 1) * 512)),
                    deps=[ev[idx]], dma=True, phased=False, sem=f"sv{sl}")
                tok[("stageW", sl)] = stt[idx]
        for k in stt:
            self.pb.add("sync", lambda e: None, deps=[stt[k]], phased=False)
        pb.barrier()

    def emit_out_proj(self, w_name, b_name):
        pb, V, t, tok = self.pb, self.V, self.t, self.tok
        pre2 = self.leg in ("B", "C") and PRECAST and PRECAST_CONSUME and w_name == "wo"
        wo_src = t["woB"] if pre2 else t[w_name]
        wo_eng = "sync" if pre2 else "gpsimd"
        brow_t = tok["bo"]
        cons_prev = {}
        opev = {}
        for n in range(4):
            war = [cons_prev.get(n - 1)]
            if n == 0:
                war.append(tok.get("wosl_gate"))
                if pre2:
                    war = war + (tok.get("precast_wo") or [None])
            wd = pb.add(wo_eng, lambda e, n=n: e.dma_start(
                out=V("wosl", c0=0, c1=NKC * 512), in_=wo_src[n]),
                deps=war, dma=True, phased=False, sem="wo0")
            for tt in range(NT):
                mmt = None
                pev = opev.get((n, tt - 2)) if tt >= 2 else opev.get((n - 1, NT - 2 + tt))
                for kc in range(NKC):
                    d0 = [wd, pev] if kc == 0 else []
                    mmt = pb.add("tensor", lambda e, n=n, tt=tt, kc=kc: e.matmul(
                        self.ps_b[:, (tt % 2) * 512:(tt % 2) * 512 + 512],
                        V("onT", c0=kc * TPC + tt * 128, c1=kc * TPC + (tt + 1) * 128),
                        V("wosl", c0=kc * 512, c1=(kc + 1) * 512),
                        start=(kc == 0), stop=False), deps=d0)
                mb = pb.add("tensor", lambda e, n=n, tt=tt: e.matmul(
                    self.ps_b[:, (tt % 2) * 512:(tt % 2) * 512 + 512],
                    V("onesr", 0, 1), V("browB", 0, 1, n * 512, (n + 1) * 512),
                    start=False, stop=True), deps=[brow_t, tok["onesr"]])
                cons_prev[n] = mb
                opev[(n, tt)] = pb.add("vector", lambda e, n=n, tt=tt: e.tensor_tensor(
                    V("x", c0=tt * D + n * 512, c1=tt * D + (n + 1) * 512),
                    self.ps_b[:, (tt % 2) * 512:(tt % 2) * 512 + 512],
                    V("x", c0=tt * D + n * 512, c1=tt * D + (n + 1) * 512), ALU.add),
                    deps=[mb])
        tok["bo_last"] = cons_prev[3]
        pb.barrier()

    def emit_ffn1(self):
        pb, V, t, tok = self.pb, self.V, self.t, self.tok
        brow_t = self.load_row_bf("b1_row", 0, D, deps=[tok.get("bo_last")])
        f1ev = {}
        psc_free = tok.get("psc_free_f1")
        for n in range(4):
            amms, sgn = [], []
            for kc in range(NKC):
                sl = kc % 2
                war = [sgn[kc - 2] if kc >= 2 else None]
                if kc < 2:
                    war += [f1ev.get((n - 1, NT - 1)), tok.get("w1f_gate")]
                wd = pb.add("gpsimd", lambda e, n=n, kc=kc, sl=sl: e.dma_start(
                    out=V("w1f", c0=sl * 512, c1=(sl + 1) * 512),
                    in_=t["w1"][n][:, kc * 512:(kc + 1) * 512]),
                    deps=war, dma=True, phased=False, sem=f"w1{sl}")
                ab = pb.add("scalar", lambda e, sl=sl: e.activation(
                    V("sbf"), V("w1f", c0=sl * 512, c1=(sl + 1) * 512), AF.Abs),
                    deps=[wd, amms[kc - 1] if kc >= 1 else tok.get("amms_last")])
                amms.append(pb.add("tensor", lambda e, kc=kc: e.matmul(
                    self.ps_c[0:1, 0:512], V("onesc"), V("sbf"),
                    start=(kc == 0), stop=(kc == NKC - 1)),
                    deps=([ab, tok["onesc"], psc_free] if kc == 0 else [ab])))
                sgn.append(pb.add("scalar", lambda e, kc=kc, sl=sl: e.activation(
                    V("w1sl", c0=kc * 512, c1=(kc + 1) * 512),
                    V("w1f", c0=sl * 512, c1=(sl + 1) * 512), AF.Sign), deps=[wd]))
            al = pb.add("vector", lambda e: e.tensor_scalar(
                V("rows", 0, 1, 0, 512), self.ps_c[0:1, 0:512], 0.5 / D, None, ALU.mult),
                deps=[amms[-1]])
            tok["amms_last"] = amms[-1]
            psc_free = al
            albc = self.bcast_hilo("rows", 0, 0, 512, "abc", 0, self.ps_a, [al])
            bw = []
            for kc in range(NKC):
                bw.append(pb.add("vector", lambda e, kc=kc: e.tensor_tensor(
                    V("w1sl", c0=kc * 512, c1=(kc + 1) * 512),
                    V("w1sl", c0=kc * 512, c1=(kc + 1) * 512),
                    V("abc", c0=0, c1=512), ALU.mult), deps=[albc, sgn[kc]]))
            for tt in range(NT):
                mmt = None
                pev = f1ev.get((n, tt - 2)) if tt >= 2 else f1ev.get((n - 1, NT - 2 + tt))
                for kc in range(NKC):
                    d0 = (bw + tok["xnT_done"] + [pev]) if kc == 0 else []
                    mmt = pb.add("tensor", lambda e, n=n, tt=tt, kc=kc: e.matmul(
                        self.ps_b[:, (tt % 2) * 512:(tt % 2) * 512 + 512],
                        self.xnT(kc, tt * 128, (tt + 1) * 128),
                        V("w1sl", c0=kc * 512, c1=(kc + 1) * 512),
                        start=(kc == 0), stop=False), deps=d0)
                mb = pb.add("tensor", lambda e, n=n, tt=tt: e.matmul(
                    self.ps_b[:, (tt % 2) * 512:(tt % 2) * 512 + 512],
                    V("halfr", 0, 1), V("browB", 0, 1, n * 512, (n + 1) * 512),
                    start=False, stop=True), deps=[brow_t, tok["halfr"]])
                f1ev[(n, tt)] = pb.add("vector", lambda e, n=n, tt=tt: e.tensor_tensor(
                    V("x", c0=tt * D + n * 512, c1=tt * D + (n + 1) * 512),
                    self.ps_b[:, (tt % 2) * 512:(tt % 2) * 512 + 512],
                    V("x", c0=tt * D + n * 512, c1=tt * D + (n + 1) * 512), ALU.add),
                    deps=[mb])
        tok["psc_free_f1"] = psc_free
        tok["b1_last"] = pb.last("tensor")
        pb.barrier()

    def emit_attention(self, layer):
        pb, V, t, tok = self.pb, self.V, self.t, self.tok
        causal = (layer == 1)
        dkv = DKV1 if causal else D
        post = (2048.0 ** -0.5) if causal else 1.0
        tok["QT_ld"] = tok["QT_ld0"]
        if causal:
            tok["cm_ld"] = tok["cm0"]
            ms0 = pb.add("vector", lambda e: e.memset(V("shiftr", 0, 128, 0, 1024), 0.0))
        if not causal:
            prevq = {}
            for h2 in range(NH):
                qsq = pb.add("vector", lambda e, h2=h2: e.tensor_tensor(
                    V("at0", c0=0, c1=1024), V("QT", c0=h2 * TPC, c1=(h2 + 1) * TPC),
                    V("QT", c0=h2 * TPC, c1=(h2 + 1) * TPC), ALU.mult),
                    deps=[tok["QT_ld"], prevq.get(h2 - 1)])
                mq2 = {}
                for qh in range(2):
                    pr = 0 if qh == 0 else 32
                    mq2[qh] = pb.add("tensor", lambda e, qh=qh, pr=pr: e.matmul(
                        self.ps_c[pr:pr + 1, 0:512], V("onesc"),
                        V("at0", c0=qh * 512, c1=(qh + 1) * 512),
                        start=True, stop=True),
                        deps=[qsq, tok["onesc"], prevq.get(h2 - 1)])
                for qh in range(2):
                    pr = 0 if qh == 0 else 32
                    prevq[h2] = pb.add("vector", lambda e, pr=pr, h2=h2: e.tensor_reduce(
                        V("stat", pr, pr + 1, 40 + h2, 41 + h2),
                        self.ps_c[pr:pr + 1, 0:512], AX.X, ALU.max),
                        deps=[mq2[0], mq2[1]])
            lnq_toks = []
            for pr in (0, 32):
                lnq_toks.append(pb.add("scalar", lambda e, pr=pr: e.activation(
                    V("stat", pr, pr + 1, 40, 56), V("stat", pr, pr + 1, 40, 56), AF.Ln),
                    deps=[prevq[NH - 1]]))
            tok["lnq_all"] = lnq_toks[-1]
            tok["maxq_done"] = prevq[NH - 1]
        cons_kv = {}
        kn_toks = {}

        if not causal:
            def make_prep(hh):
                if hh >= NH:
                    return []
                ksl2 = hh % 2
                st = kn_toks.setdefault(hh, {"bias": {}, "bias_mm": {}})
                steps = []

                def s_kd():
                    kd2 = []
                    for r2 in range(4):
                        kd2.append(pb.add("sync", lambda e, hh=hh, r2=r2, ksl2=ksl2: e.dma_start(
                            out=V("kth", c0=ksl2 * 4096 + r2 * 1024, c1=ksl2 * 4096 + (r2 + 1) * 1024),
                            in_=t["KT_full"][r2 * dkv + hh * 128: r2 * dkv + hh * 128 + 128, :]),
                            deps=[cons_kv.get(hh - 2)], dma=True, phased=False, sem=f"kt{ksl2}"))
                        kd2.append(pb.add("sync", lambda e, hh=hh, r2=r2, ksl2=ksl2: e.dma_start(
                            out=V("vth", c0=ksl2 * 4096 + r2 * 1024, c1=ksl2 * 4096 + (r2 + 1) * 1024)
                                .rearrange("p (j d) -> p j d", j=8),
                            in_=t["V_full"][r2 * TPC:(r2 + 1) * TPC, hh * 128:(hh + 1) * 128]
                                .rearrange("(j p) d -> p j d", p=128)),
                            deps=[cons_kv.get(hh - 2)], dma=True, phased=False, sem=f"vt{ksl2}"))
                    st["kd"] = kd2
                steps.append(s_kd)

                for ppp in range(8):
                    def s_kn(ppp=ppp):
                        prev_c = kn_toks.get(hh - 1, {})
                        ks = pb.add("vector", lambda e, ksl2=ksl2, ppp=ppp: e.tensor_tensor(
                            V("ksq"),
                            V("kth", c0=ksl2 * 4096 + ppp * 512, c1=ksl2 * 4096 + (ppp + 1) * 512),
                            V("kth", c0=ksl2 * 4096 + ppp * 512, c1=ksl2 * 4096 + (ppp + 1) * 512),
                            ALU.mult),
                            deps=st["kd"] + [st.get("red"),
                                             prev_c.get("last_m1") if ppp == 0 else None])
                        m1 = pb.add("tensor", lambda e: e.matmul(
                            self.ps_t[0:1, 0:512], V("onesc"), V("ksq"),
                            start=True, stop=True),
                            deps=[ks, tok["onesc"], st.get("red")] +
                                 ([prev_c["bias"].get(0), prev_c["bias"].get(1)]
                                  if ppp == 0 and prev_c.get("bias") else []))
                        st["red"] = pb.add("vector", lambda e, ppp=ppp: e.tensor_reduce(
                            V("stat", 0, 1, 24 + ppp, 25 + ppp),
                            self.ps_t[0:1, 0:512], AX.X, ALU.max), deps=[m1])
                        st.setdefault("reds", []).append(st["red"])
                        st["last_m1"] = m1
                    steps.append(s_kn)

                def s_lnk():
                    knm = pb.add("vector", lambda e: e.tensor_reduce(
                        V("stat", 0, 1, 32, 33), V("stat", 0, 1, 24, 32), AX.X, ALU.max),
                        deps=st["reds"])
                    lnk = pb.add("scalar", lambda e: e.activation(
                        V("stat", 0, 1, 33, 34), V("stat", 0, 1, 32, 33), AF.Ln), deps=[knm])
                    lnk_b = pb.add("vector", lambda e, hh=hh: e.tensor_copy(
                        V("bcol", 0, 1, 4 + hh % 2, 5 + hh % 2), V("stat", 0, 1, 33, 34)), deps=[lnk])
                    mmk = pb.add("tensor", lambda e, hh=hh: e.matmul(
                        self.ps_t[:, 16:17], V("onesr", 0, 1),
                        V("bcol", 0, 1, 4 + hh % 2, 5 + hh % 2),
                        start=True, stop=True), deps=[lnk_b, tok["onesr"]])
                    st["lnk_bc"] = pb.add("vector", lambda e: e.tensor_copy(
                        V("stat", c0=34, c1=35), self.ps_t[:, 16:17]), deps=[mmk])
                steps.append(s_lnk)

                for qh in range(2):
                    def s_bias(qh=qh):
                        pr = 0 if qh == 0 else 32
                        colc = (hh % 2) * 2 + qh
                        prev2 = kn_toks.get(hh - 2, {})
                        t1 = pb.add("vector", lambda e, pr=pr, hh=hh: e.tensor_scalar(
                            V("stat", pr, pr + 1, 37, 38), V("stat", pr, pr + 1, 40 + hh, 41 + hh),
                            V("stat", pr, pr + 1, 34, 35), None, ALU.add),
                            deps=[tok["lnq_all"], st["lnk_bc"]])
                        t2 = pb.add("scalar", lambda e, pr=pr: e.activation(
                            V("stat", pr, pr + 1, 38, 39), V("stat", pr, pr + 1, 37, 38),
                            AF.Exp, scale=np.float32(0.5)), deps=[t1])
                        t3 = pb.add("vector", lambda e, pr=pr: e.tensor_scalar(
                            V("stat", pr, pr + 1, 39, 40), V("stat", pr, pr + 1, 38, 39),
                            -1.0, MARGIN, ALU.mult, op1=ALU.add), deps=[t2])
                        cast = pb.add("vector", lambda e, pr=pr, colc=colc: e.tensor_copy(
                            V("bcol", pr, pr + 1, colc, colc + 1), V("stat", pr, pr + 1, 39, 40)),
                            deps=[t3, prev2.get("bias_mm", {}).get(qh)])
                        bmm = pb.add("tensor", lambda e, pr=pr, colc=colc: e.matmul(
                            self.ps_t[:, 20 + colc:21 + colc],
                            V("onesr", 0, 1) if pr == 0 else V("onesF", 32, 33),
                            V("bcol", pr, pr + 1, colc, colc + 1),
                            start=True, stop=True),
                            deps=[cast, tok["onesF"], tok["onesr"],
                                  prev2.get("bias", {}).get(qh)])
                        st["bias_mm"][qh] = bmm
                        st["bias"][qh] = pb.add("vector", lambda e, colc=colc: e.tensor_copy(
                            V("sbias", 0, 128, colc, colc + 1), self.ps_t[:, 20 + colc:21 + colc]),
                            deps=[bmm])
                    steps.append(s_bias)
                return steps

            for s in make_prep(0):
                s()

        for h in range(NH):
            head_gate = [tok.get(("rz_done", h - 1)), tok.get(("onT_ev", h - 1, 1))]
            if h == 0:
                head_gate.append(tok.get("maxq_done"))
            if causal:
                g = h // 4
                new_kv = (h % 4 == 0)
                ksl = g % 2
                if new_kv:
                    kd = []
                    for r in range(4):
                        kd.append(pb.add("sync", lambda e, g=g, r=r, ksl=ksl: e.dma_start(
                            out=V("kth", c0=ksl * 4096 + r * 1024, c1=ksl * 4096 + (r + 1) * 1024),
                            in_=t["KT_full"][r * dkv + g * 128: r * dkv + g * 128 + 128, :]),
                            deps=[cons_kv.get(g - 2)], dma=True, phased=False,
                            sem=f"kt{ksl}"))
                        kd.append(pb.add("sync", lambda e, g=g, r=r, ksl=ksl: e.dma_start(
                            out=V("vth", c0=ksl * 4096 + r * 1024, c1=ksl * 4096 + (r + 1) * 1024)
                                .rearrange("p (j d) -> p j d", j=8),
                            in_=t["V_full"][r * TPC:(r + 1) * TPC, g * 128:(g + 1) * 128]
                                .rearrange("(j p) d -> p j d", p=128)),
                            deps=[cons_kv.get(g - 2)], dma=True, phased=False,
                            sem=f"vt{ksl}"))
                    tok[("kv", g)] = kd
                    kn_parts = []
                    prev = None
                    nseg = 4
                    for ppp in range(8):
                        ks = pb.add("vector", lambda e, ksl=ksl, ppp=ppp: e.tensor_tensor(
                            V("ksq"),
                            V("kth", c0=ksl * 4096 + ppp * 512, c1=ksl * 4096 + (ppp + 1) * 512),
                            V("kth", c0=ksl * 4096 + ppp * 512, c1=ksl * 4096 + (ppp + 1) * 512),
                            ALU.mult), deps=kd + head_gate + ([prev] if prev else []))
                        m1 = pb.add("tensor", lambda e: e.matmul(
                            self.ps_c[0:1, 0:512], V("onesc"), V("ksq"),
                            start=True, stop=True),
                            deps=[ks, tok["onesc"]] + head_gate + ([kn_parts[-1]] if kn_parts else []))
                        for sub in range(nseg):
                            cbase = 24 + ppp * nseg + sub
                            kn_parts.append(pb.add("vector", lambda e, sub=sub, cbase=cbase, nseg=nseg: e.tensor_reduce(
                                V("stat", 0, 1, cbase, cbase + 1),
                                self.ps_c[0:1, sub * (512 // nseg):(sub + 1) * (512 // nseg)],
                                AX.X, ALU.max), deps=[m1]))
                        prev = kn_parts[-1]
                    jt = []
                    for jq in range(8):
                        t1 = pb.add("vector", lambda e, jq=jq: e.tensor_tensor(
                            V("lnkr", 0, 1, 992, 1024),
                            V("stat", 0, 1, 24, 56),
                            V("kselb", 0, 1, jq * 32, (jq + 1) * 32), ALU.add),
                            deps=kn_parts + [tok["ksel"]])
                        jt.append(pb.add("vector", lambda e, jq=jq: e.tensor_reduce(
                            V("stat", 0, 1, 56 + jq, 57 + jq),
                            V("lnkr", 0, 1, 992, 1024), AX.X, ALU.max), deps=[t1]))
                    lnj = pb.add("scalar", lambda e: e.activation(
                        V("stat", 0, 1, 56, 64), V("stat", 0, 1, 56, 64), AF.Ln), deps=jt)
                    bcj = []
                    for jq in range(8):
                        bcj.append(pb.add("vector", lambda e, jq=jq: e.tensor_scalar(
                            V("lnkr", 0, 1, jq * 128, (jq + 1) * 128),
                            V("kselb", 0, 1, 0, 128),
                            0.0, V("stat", 0, 1, 56 + jq, 57 + jq),
                            ALU.mult, op1=ALU.add), deps=[lnj, tok["ksel"]]))
                    cpL = pb.add("vector", lambda e: e.tensor_copy(
                        V("hilo", 0, 1, 0, 512), V("lnkr", 0, 1, 512, 1024)), deps=bcj)
                    mm2 = pb.add("tensor", lambda e: e.matmul(
                        self.ps_a[:, 0:512],
                        V("onesr", 0, 1), V("hilo", 0, 1, 0, 512),
                        start=True, stop=True), deps=[cpL, tok["onesr"]] + head_gate)
                    tok[("lnkr2", g)] = pb.add("vector", lambda e: e.tensor_copy(
                        V("lnkr2", 32, 33, 0, 512), self.ps_a[32:33, 0:512]), deps=[mm2])
                    tok[("lnk_bc", g)] = bcj[-1]
                kv_deps = tok[("kv", g)]
                lnk_bc = tok[("lnk_bc", g)]

                negc = {}
                qn_ev = {}
                qsq = pb.add("vector", lambda e, h=h: e.tensor_tensor(
                    V("at0", c0=0, c1=1024), V("QT", c0=h * TPC, c1=(h + 1) * TPC),
                    V("QT", c0=h * TPC, c1=(h + 1) * TPC), ALU.mult),
                    deps=[tok["QT_ld"]] + head_gate + [tok.get(("avz", h - 1))])
                mqs = {}
                for qh in range(2):
                    pr = 0 if qh == 0 else 32
                    mqs[qh] = pb.add("tensor", lambda e, qh=qh, pr=pr: e.matmul(
                        self.ps_c[pr:pr + 1, 0:512], V("onesc"),
                        V("at0", c0=qh * 512, c1=(qh + 1) * 512),
                        start=True, stop=True),
                        deps=[qsq, tok["onesc"], lnk_bc] + head_gate)
                for qh in range(2):
                    pr = 0 if qh == 0 else 32
                    qn_ev[qh] = pb.add("vector", lambda e, pr=pr: e.tensor_copy(
                        V("rows", pr, pr + 1, 0, 512), self.ps_c[pr:pr + 1, 0:512]),
                        deps=[mqs[0], mqs[1]])
                for qh in range(2):
                    pr = 0 if qh == 0 else 32
                    l1 = pb.add("scalar", lambda e, pr=pr: e.activation(
                        V("rows", pr, pr + 1, 0, 512), V("rows", pr, pr + 1, 0, 512), AF.Ln),
                        deps=[qn_ev[qh]])
                    s1 = pb.add("vector", lambda e, qh=qh, pr=pr: e.tensor_tensor(
                        V("rows", pr, pr + 1, 0, 512), V("rows", pr, pr + 1, 0, 512),
                        V("lnkr", 0, 1, qh * 512, (qh + 1) * 512) if pr == 0
                        else V("lnkr2", 32, 33, 0, 512), ALU.add),
                        deps=[l1, lnk_bc, tok.get(("lnkr2", g))])
                    e1 = pb.add("scalar", lambda e, pr=pr: e.activation(
                        V("rows", pr, pr + 1, 0, 512), V("rows", pr, pr + 1, 0, 512),
                        AF.Exp, scale=np.float32(0.5)), deps=[s1])
                    negc[qh] = pb.add("vector", lambda e, pr=pr: e.tensor_scalar(
                        V("shiftr", pr, pr + 1, 0, 512), V("rows", pr, pr + 1, 0, 512),
                        -MARGIN, -1.0, ALU.add, op1=ALU.mult), deps=[e1])

                def c0_of(kt):
                    gk = (kt // 8) + 4 * (kt % 8)
                    return max(0, cdiv(gk - 3, 4)) * 128
                kts = [kt for kt in range(32) if c0_of(kt) < 1024]
                half_kts = {qh: [kt for kt in kts if c0_of(kt) < (qh + 1) * 512] for qh in range(2)}

                exp_toks = {}
                av_last = {}
                pend = {}
                av_all = []

                def emit_zav(idx):
                    if idx < 0 or idx not in pend:
                        return
                    kt, c0, asl = pend.pop(idx)
                    for qh in range(2):
                        cc, ce = qh * 512, (qh + 1) * 512
                        if ce <= c0:
                            continue
                        ccc = max(cc, c0)
                        first = (kt == half_kts[qh][0])
                        lastf = (kt == half_kts[qh][-1])
                        zdeps = [exp_toks[idx]] + (([qn_ev[0], qn_ev[1]] + head_gate) if first else [])
                        if qh == 0:
                            pb.add("tensor", lambda e, ccc=ccc, ce=ce, cc=cc, first=first, lastf=lastf, asl=asl: e.matmul(
                                self.ps_c[0:1, ccc - cc:512], V("onesc"),
                                V("at0", c0=asl * 1024 + ccc, c1=asl * 1024 + ce),
                                start=first, stop=lastf), deps=zdeps)
                        else:
                            pb.add("tensor", lambda e, ccc=ccc, ce=ce, cc=cc, first=first, lastf=lastf, asl=asl: e.matmul(
                                self.ps_c[32:33, ccc - cc:512], V("onesc"),
                                V("at0", c0=asl * 1024 + ccc, c1=asl * 1024 + ce),
                                start=first, stop=lastf), deps=zdeps)
                        lav = pb.add("tensor", lambda e, ccc=ccc, ce=ce, kt=kt, ksl=ksl, first=first, lastf=lastf, asl=asl: e.matmul(
                            self.ps_b[:, ccc:ce],
                            V("vth", c0=ksl * 4096 + kt * 128, c1=ksl * 4096 + (kt + 1) * 128),
                            V("at0", c0=asl * 1024 + ccc, c1=asl * 1024 + ce),
                            start=first, stop=lastf), deps=zdeps)
                        av_last[idx] = lav
                        av_all.append(lav)

                for idx, kt in enumerate(kts):
                    c0 = c0_of(kt)
                    slot = idx % 2
                    pd = [negc[0], negc[1], tok["onesr"], tok["onesF"], exp_toks.get(idx - 2)]
                    mm_last = None
                    for cc in range(c0 - (c0 % 512), 1024, 512):
                        ccc = max(cc, c0)
                        ce = cc + 512
                        qh = cc // 512
                        pr = 0 if qh == 0 else 32
                        pb.add("tensor", lambda e, slot=slot, ccc=ccc, ce=ce, pr=pr, cc=cc: e.matmul(
                            self.ps_a[:, slot * 1024 + ccc: slot * 1024 + ce],
                            V("onesr", 0, 1) if pr == 0 else V("onesF", pr, pr + 1),
                            V("shiftr", pr, pr + 1, ccc - cc, 512),
                            start=True, stop=False), deps=pd)
                        mm_last = pb.add("tensor", lambda e, slot=slot, ccc=ccc, ce=ce, kt=kt, ksl=ksl, h=h: e.matmul(
                            self.ps_a[:, slot * 1024 + ccc: slot * 1024 + ce],
                            V("kth", c0=ksl * 4096 + kt * 128, c1=ksl * 4096 + (kt + 1) * 128),
                            V("QT", c0=h * TPC + ccc, c1=h * TPC + ce),
                            start=False, stop=True), deps=(kv_deps + [tok["QT_ld"]]) if cc == c0 - (c0 % 512) else [])
                    emit_zav(idx - 2)
                    mask_tok = None
                    gk = (kt // 8) + 4 * (kt % 8)
                    jb = gk // 4
                    if jb * 128 >= c0:
                        mask_tok = pb.add("vector", lambda e, slot=slot, jb=jb, kt=kt: e.tensor_tensor(
                            self.ps_a[:, slot * 1024 + jb * 128: slot * 1024 + (jb + 1) * 128],
                            self.ps_a[:, slot * 1024 + jb * 128: slot * 1024 + (jb + 1) * 128],
                            V("cm", c0=kt * 128, c1=(kt + 1) * 128), ALU.add),
                            deps=[mm_last, tok["cm_ld"]])
                    asl = idx % 2
                    ed = [mm_last, mask_tok, av_last.get(idx - 2)]
                    if idx < 2:
                        ed.append(tok.get(("avz", h - 1)))
                    exp_toks[idx] = pb.add("scalar", lambda e, slot=slot, c0=c0, asl=asl: e.activation(
                        V("at0", c0=asl * 1024 + c0, c1=asl * 1024 + 1024),
                        self.ps_a[:, slot * 1024 + c0: slot * 1024 + 1024], AF.Exp), deps=ed)
                    pend[idx] = (kt, c0, asl)
                for tail in (2, 1):
                    emit_zav(len(kts) - tail)
                tok[("avz", h)] = av_all[-1]
                emd = [exp_toks[len(kts) - 1], exp_toks.get(len(kts) - 2)]
            else:
                st = kn_toks[h]
                kv_deps = st["kd"]
                ksl = h % 2
                steps_next = make_prep(h + 1)
                kts = list(range(32))
                exp_toks = {}
                av_last = {}
                sc_tok = {}
                pend = {}
                av_all = []

                def emit_zav(idx):
                    if idx < 0 or idx not in pend:
                        return
                    kt, asl = pend.pop(idx)
                    for qh in range(2):
                        cc, ce = qh * 512, (qh + 1) * 512
                        first = (kt == kts[0])
                        lastf = (kt == kts[-1])
                        zdeps = [exp_toks[(idx, qh)]] + (head_gate if first else [])
                        pb.add("tensor", lambda e, cc=cc, ce=ce, first=first, lastf=lastf, asl=asl, qh=qh: e.matmul(
                            self.ps_c[0:1, 0:512] if qh == 0 else self.ps_c[32:33, 0:512],
                            V("onesc"),
                            V("at0", c0=asl * 1024 + cc, c1=asl * 1024 + ce),
                            start=first, stop=lastf), deps=zdeps)
                        lav = pb.add("tensor", lambda e, cc=cc, ce=ce, kt=kt, ksl=ksl, first=first, lastf=lastf, asl=asl: e.matmul(
                            self.ps_b[:, cc:ce],
                            V("vth", c0=ksl * 4096 + kt * 128, c1=ksl * 4096 + (kt + 1) * 128),
                            V("at0", c0=asl * 1024 + cc, c1=asl * 1024 + ce),
                            start=first, stop=lastf), deps=zdeps)
                        av_last[(idx, qh)] = lav
                        av_all.append(lav)

                for idx, kt in enumerate(kts):
                    slot = idx % 2
                    asl = idx % 3
                    for qh in range(2):
                        cc, ce = qh * 512, (qh + 1) * 512
                        d0 = [exp_toks.get((idx - 2, qh))]
                        if idx == 0:
                            d0 = d0 + head_gate
                        if qh == 0:
                            d0 = d0 + kv_deps + [tok["QT_ld"]]
                        sc_tok[(idx, qh)] = pb.add("tensor", lambda e, slot=slot, cc=cc, ce=ce, kt=kt, ksl=ksl, h=h: e.matmul(
                            self.ps_a[:, slot * 1024 + cc: slot * 1024 + ce],
                            V("kth", c0=ksl * 4096 + kt * 128, c1=ksl * 4096 + (kt + 1) * 128),
                            V("QT", c0=h * TPC + cc, c1=h * TPC + ce),
                            start=True, stop=True), deps=d0)
                    emit_zav(idx - 2)
                    if steps_next:
                        steps_next.pop(0)()
                    for qh in range(2):
                        colc = (h % 2) * 2 + qh
                        ed = [sc_tok[(idx, qh)], av_last.get((idx - 3, qh)), st["bias"][qh]]
                        if idx < 3:
                            ed.append(tok.get(("avz", h - 1)))
                        exp_toks[(idx, qh)] = pb.add("scalar", lambda e, slot=slot, asl=asl, qh=qh, colc=colc: e.activation(
                            V("at0", c0=asl * 1024 + qh * 512, c1=asl * 1024 + (qh + 1) * 512),
                            self.ps_a[:, slot * 1024 + qh * 512: slot * 1024 + (qh + 1) * 512],
                            AF.Exp, bias=V("sbias", 0, 128, colc, colc + 1)), deps=ed)
                    pend[idx] = (kt, asl)
                for tail in (2, 1):
                    emit_zav(len(kts) - tail)
                while steps_next:
                    steps_next.pop(0)()
                tok[("avz", h)] = av_all[-1]
                nk = len(kts)
                emd = [exp_toks[(nk - 1, 0)], exp_toks[(nk - 1, 1)],
                       exp_toks[(nk - 2, 0)], exp_toks[(nk - 2, 1)]]

            rzx = {}
            for qh in range(2):
                pr = 0 if qh == 0 else 32
                src = (lambda: self.ps_c[0:1, 0:512]) if qh == 0 else (lambda: self.ps_c[32:33, 0:512])
                zf = pb.add("vector", lambda e, pr=pr, src=src: e.tensor_scalar(
                    V("rows", pr, pr + 1, 0, 512), src(), 1e-30, None, ALU.add),
                    deps=av_all)
                rr = pb.add("vector", lambda e, pr=pr: e.reciprocal(
                    V("rows", pr, pr + 1, 0, 512), V("rows", pr, pr + 1, 0, 512)), deps=[zf])
                rzx[qh] = pb.add("vector", lambda e, pr=pr: e.tensor_scalar(
                    V("rows", pr, pr + 1, 0, 512), V("rows", pr, pr + 1, 0, 512),
                    post, None, ALU.mult), deps=[rr])
            rzbc0 = self.bcast_hilo("rows", 0, 0, 512, "rzbc", 0, self.ps_a, [rzx[0]],
                                    extra_mm_deps=emd)
            rzbc1 = self.bcast_hilo("rows", 32, 0, 512, "rzbc", 512, self.ps_a[:, 512:1024],
                                    [rzx[1]], extra_mm_deps=emd)
            tok[("rz_done", h)] = rzbc1
            for qh in range(2):
                bc = rzbc0 if qh == 0 else rzbc1
                tok[("onT_ev", h, qh)] = pb.add("vector", lambda e, h=h, qh=qh: e.tensor_tensor(
                    V("onT", c0=h * TPC + qh * 512, c1=h * TPC + (qh + 1) * 512),
                    self.ps_b[:, qh * 512:(qh + 1) * 512],
                    V("rzbc", c0=qh * 512, c1=(qh + 1) * 512), ALU.mult),
                    deps=[bc] + av_all)
            cons_kv[g if causal else h] = tok[("onT_ev", h, 1)]
        pb.barrier()

    def dump(self, name, src_name, cols, bf=False):
        if not self.dbg:
            return
        self.dout(name, (128, cols), BF16 if bf else F32)
        tk = self.pb.add("sync", lambda e: e.dma_start(
            out=self.t[name], in_=self.V(src_name, c0=0, c1=cols)),
            deps=list(self.pb.phase_deps), dma=True, phased=False, sem="dbg")
        self.pb.add("sync", lambda e: None, deps=[tk], phased=False)

    def store_x(self):
        stx = self.pb.add("sync", lambda e: e.dma_start(
            out=self.t["x_out"].rearrange("(t p) d -> p t d", p=128),
            in_=self.V("x").rearrange("p (t d) -> p t d", t=NT)),
            deps=list(self.pb.phase_deps), dma=True, phased=False, sem="sx")
        self.pb.add("sync", lambda e: None, deps=[stx], phased=False)


    def build(self):
        leg = self.leg
        self.din("x_in", (TPC, D))
        self.din("ones_row", (1, 128), BF16)
        self.din("half_row", (1, 128), BF16)
        self.din("ones_col", (128, 1), BF16)
        self.din("ones_full", (128, 128), BF16)
        self.din("ident", (128, 128), BF16)
        if leg in ("A", "B"):
            self.din("guA", (NIC, 128, 2 * D))
            self.din("dA", (4, 22, 128, 1024))
            self.din("gs", (128, NIC))
            self.din("us", (128, NIC))
            self.din("ds_row", (1, D))
        if leg == "B":
            self.t["guB"] = self.nc.dram_tensor("guB", [NIC, 128, 2 * D], BF16, kind="Internal").ap()
            self.t["dB"] = self.nc.dram_tensor("dB", [4, 22, 128, 1024], BF16, kind="Internal").ap()
        if leg in ("B", "C"):
            self.t["woB"] = self.nc.dram_tensor("woB", [4, 128, NKC * 512], BF16, kind="Internal").ap()
        if leg == "A":
            self.din("wq", (NKC, 128, D))
            self.din("wk", (NKC, 128, D))
            self.din("wv", (4, 128, NKC * 512))
            self.din("bq", (128, NKC))
            self.din("bk", (128, NKC))
            self.din("bv_row", (1, D))
            self.dout("x_out", (TPC, D))
            self.dout("KT_out", (D, TPC), BF16)
            self.dout("V_out", (TPC, D), BF16)
            self.dout("QT_out", (D, TPC), BF16)
        if leg in ("B", "C"):
            dkv = D if leg == "B" else DKV1
            self.din("QT_in", (D, TPC), BF16)
            self.din("KT_full", (4 * dkv, TPC), BF16)
            self.din("V_full", (4 * TPC, dkv), BF16)
            self.din("wo", (4, 128, NKC * 512))
            self.din("bo_row", (1, D))
            self.din("w1", (4, 128, NKC * 512))
            self.din("b1_row", (1, D))
            self.dout("x_out", (TPC, D))
        if leg == "B":
            self.din("wq1", (NKC, 128, D))
            self.din("wk1", (4, 128, D))
            self.din("wv1", (1, 128, NKC * 512))
            self.din("bq1", (128, NKC))
            self.din("bk1", (128, 4))
            self.din("bv1_row", (1, DKV1))
            self.din("cos_t", (128, TPC))
            self.din("sin_t", (128, TPC))
            self.dout("QT_out", (D, TPC), BF16)
            self.dout("KT_out", (DKV1, TPC), BF16)
            self.dout("V_out", (TPC, DKV1), BF16)
        if leg == "C":
            self.din("cmask", (32, 128, 128), BF16)
            self.din("ksel", (1, 256))

        r = self.res
        r("f", "x", NT * D)
        r("f", "stat", 64)
        r("f", "bstat", 32)
        r("f", "scrF", 2048)
        r("f", "scr0", 512, at="scrF")
        r("f", "scr1", 512, at="scrF", at_off=512)
        r("f", "sq", 2048, at="scrF")
        r("f", "qf", 1024, at="scrF", at_off=1024)
        r("f", "hiloF", 1024, at="scrF")
        r("f", "rows", 512)
        r("b", "onesr", 128)
        r("b", "halfr", 128)
        r("b", "onesc", 1)
        r("b", "onesF", 128)
        r("b", "ident", 128)
        r("b", "browB", 2048)
        r("b", "hilo", 1024)
        r("b", "shiftr", 1024, at="hilo")
        r("b", "sbf", 512)
        if leg in ("A", "B"):
            r("f", "ds_bc", 2048)
            r("f", "ds_bc2", 2048, at="ds_bc")
            r("f", "gsc", NIC)
            r("f", "usc", NIC)
        if leg == "A":
            r("b", "xnT", NKC * TPC)
            r("b", "wgu", 2 * 2 * D)
            r("b", "hdn", NIC * 512)
            r("b", "dwn", 2 * 2048)
            r("b", "stage", 2 * D, at="hdn", at_off=8192)
            r("b", "xn", 2048, at="stage")
            r("b", "wproj", 2 * D, at="wgu")
            r("b", "wvsl", NKC * 512, at="hdn")
            self.tok_gates = {"dwn_gate": None}
        if leg == "B":
            r("f", "cs_c", 1024, at="ds_bc")
            r("f", "cs_s", 1024, at="ds_bc", at_off=1024)
            r("f", "rzbc", 1536)
            r("f", "qswap", 1024, at="rzbc")
            r("f", "w1f", 1024, at="rzbc")
            r("f", "abc", 512, at="rzbc", at_off=1024)
            r("b", "QT", NKC * TPC)
            r("b", "xnT", NKC * TPC, at="QT")
            r("b", "onT", NKC * TPC)
            r("b", "at0", 3 * 1024)
            r("b", "ksq", 512)
            r("b", "kth", 2 * 4096)
            r("b", "vth", 2 * 4096)
            r("b", "stage", 2 * D, at="onT")
            r("b", "xn", 2048, at="stage")
            self.m["hdn"] = ("b", self.m["onT"][1], NIC * 512)
            assert self.m["at0"][1] == self.m["onT"][1] + 16384
            assert self.m["ksq"][1] == self.m["onT"][1] + 19456
            assert self.m["kth"][1] == self.m["onT"][1] + 19968
            self.m["wgu"] = ("b", self.m["kth"][1] + 4096, 2 * 2 * D)
            assert self.m["vth"][1] == self.m["kth"][1] + 8192
            self.m["dwn"] = ("b", self.m["vth"][1] + 4096, 2 * 2048)
            r("b", "w1sl", NKC * 512, at="kth")
            r("b", "wosl", NKC * 512, at="kth")
            r("b", "wproj", 2 * D, at="kth")
            r("b", "wvsl", NKC * 512, at="kth")
            r("b", "bcol", 6)
            r("f", "sbias", 4)
        if leg == "C":
            r("f", "kselb", 256)
            r("f", "lnkr", 1024, at="scrF", at_off=1024)
            r("f", "lnkr2", 512, at="scrF", at_off=1024)
            r("f", "rzbc", 1536)
            r("f", "w1f", 1024, at="rzbc")
            r("f", "abc", 512, at="rzbc", at_off=1024)
            r("b", "QT", NKC * TPC)
            r("b", "xnT", NKC * TPC, at="QT")
            r("b", "onT", NKC * TPC)
            r("b", "at0", 2 * 1024)
            r("b", "ksq", 512)
            r("b", "kth", 2 * 4096)
            r("b", "vth", 2 * 4096)
            r("b", "cm", 32 * 128)
            r("b", "stage", 2 * D, at="onT")
            r("b", "xn", 2048, at="stage")
            r("b", "w1sl", NKC * 512, at="kth")
            r("b", "wosl", NKC * 512, at="kth")

        with ExitStack() as st:
            nfc = max(1, self.off["f"])
            nbc = max(1, self.off["b"])
            assert nfc * 4 + nbc * 2 <= 219000, (leg, nfc * 4, nbc * 2)
            self.f32a = st.enter_context(self.nc.sbuf_tensor("f32a", [128, nfc], F32))
            self.bfa = st.enter_context(self.nc.sbuf_tensor("bfa", [128, nbc], BF16))
            self.ps_a = st.enter_context(self.nc.psum_tensor("ps_a", [128, 2048], F32))
            self.ps_b = st.enter_context(self.nc.psum_tensor("ps_b", [128, 1024], F32))
            self.ps_c = st.enter_context(self.nc.psum_tensor("ps_c", [128, 512], F32))
            self.ps_t = st.enter_context(self.nc.psum_tensor("ps_t", [128, 512], F32))
            self.ps_tb = self.ps_t.ap().bitcast(BF16)
            sems = {e: st.enter_context(self.nc.semaphore(f"s_{e}")) for e in ENGINES}
            block = st.enter_context(self.nc.Block())
            self._sem_stack = st

            self.consts()
            if self.leg == "A":
                self.emit_ln("ln1")
                self.dump("d_xnT", "xnT", NKC * TPC, bf=True)
                self.dump("d_stat", "stat", 64)
                self.emit_ffn0()
                self.dump("d_ds", "ds_bc2", 2048)
                self.dump("d_dwn", "dwn", 2 * 2048, bf=True)
                self.dump("d_hdn", "hdn", NIC * 512, bf=True)
                self.dump("d_x", "x", NT * D)
                self.emit_ln("ln2")
                self.tok["wproj_gate"] = self.tok["gu_last"]
                self.emit_proj_fm("wk", "bk", NKC, "KT_out", 0)
                self.tok["wproj_gate"] = self.tok["fm_last"]
                self.emit_proj_fm("wq", "bq", NKC, "QT_out", 16, scale=1.0 / math.sqrt(HD))
                self.tok["wvsl_gate"] = self.tok["dw_last"]
                bv = self.load_row_bf("bv_row", 0, D, deps=[])
                self.emit_proj_tm("wv", "wvsl", bv, "V_out", D)
                self.store_x()
            elif self.leg == "B":
                if PRECAST:
                    pc_last = {}
                    for n4 in range(4):
                        pc_last[n4] = self.pb.add("gpsimd", lambda e, n4=n4: e.dma_start(
                            out=self.t["woB"][n4], in_=self.t["wo"][n4]),
                            dma=True, phased=False, sem=f"pc{n4}")
                    self.tok["precast_wo"] = [pc_last[k] for k in range(4)]
                    for ic in range(NIC):
                        pc_last[ic % 4] = self.pb.add("gpsimd", lambda e, ic=ic: e.dma_start(
                            out=self.t["guB"][ic], in_=self.t["guA"][ic]),
                            dma=True, phased=False, sem=f"pc{ic % 4}")
                    for n4 in range(4):
                        for g11 in range(22):
                            k4 = (n4 * 22 + g11) % 4
                            pc_last[k4] = self.pb.add("gpsimd", lambda e, n4=n4, g11=g11, k4=k4: e.dma_start(
                                out=self.t["dB"][n4, g11], in_=self.t["dA"][n4, g11]),
                                dma=True, phased=False, sem=f"pc{k4}")
                    self.tok["precast"] = [pc_last[k] for k in range(4)]
                self.emit_attention(0)
                self.tok["wgu_gate"] = self.tok[("onT_ev", NH - 1, 1)]
                self.tok["dwn_gate"] = self.tok[("onT_ev", NH - 1, 1)]
                self.tok["wosl_gate"] = self.tok[("onT_ev", NH - 1, 1)]
                self.tok["w1f_gate"] = self.tok[("onT_ev", NH - 1, 1)]
                self.dump("d_xat", "x", NT * D)
                self.emit_out_proj("wo", "bo_row")
                self.dump("d_xop", "x", NT * D)
                self.emit_ln("ln3")
                self.emit_ffn0()
                self.dump("d_wgu", "wgu", 2 * 2 * D, bf=True)
                self.dump("d_dwn", "dwn", 2 * 2048, bf=True)
                self.dump("d_xf0", "x", NT * D)
                self.emit_ln("ln4")
                self.emit_ffn1()
                self.emit_ln("ln5")
                csc = self.pb.add("sync", lambda e: e.dma_start(
                    out=self.V("cs_c"), in_=self.t["cos_t"]), dma=True, phased=False,
                    deps=list(self.pb.phase_deps), sem="cs")
                css = self.pb.add("sync", lambda e: e.dma_start(
                    out=self.V("cs_s"), in_=self.t["sin_t"]), dma=True, phased=False,
                    deps=[csc], sem="cs")
                self.tok["cs"] = css
                self.tok["wproj_gate"] = self.tok["b1_last"]
                self.emit_proj_fm("wk1", "bk1", 4, "KT_out", 0, rope=True)
                self.tok["wproj_gate"] = self.tok["fm_last"]
                self.emit_proj_fm("wq1", "bq1", NKC, "QT_out", 16, rope=True)
                self.tok["wvsl_gate"] = self.tok["fm_last"]
                bv = self.load_row_bf("bv1_row", 0, DKV1, deps=[self.tok.get("b1_last")])
                self.emit_proj_tm("wv1", "wvsl", bv, "V_out", DKV1)
                self.store_x()
            else:
                if PRECAST and PRECAST_CONSUME:
                    pcw = []
                    for n4 in range(4):
                        pcw.append(self.pb.add("gpsimd", lambda e, n4=n4: e.dma_start(
                            out=self.t["woB"][n4], in_=self.t["wo"][n4]),
                            dma=True, phased=False, sem=f"pc{n4}"))
                    self.tok["precast_wo"] = pcw
                self.emit_attention(1)
                self.tok["wosl_gate"] = self.tok[("onT_ev", NH - 1, 1)]
                self.tok["w1f_gate"] = self.tok[("onT_ev", NH - 1, 1)]
                self.emit_out_proj("wo", "bo_row")
                self.emit_ln("ln6")
                self.emit_ffn1()
                self.store_x()

            lsems = {lbl: st.enter_context(self.nc.semaphore(f"d_{lbl}"))
                     for lbl in self.pb.label_count}
            self.pb.render(self.nc, block, sems, lsems)
        return self.nc



_cache = {}
_nc_cache = {}


def rope_perm(nh):
    p = []
    for h in range(nh):
        base = h * HD
        p += [base + i for i in range(0, HD, 2)]
        p += [base + i for i in range(1, HD, 2)]
    return np.array(p, dtype=np.int64)


def _host_prep(inputs):
    if "done" in _cache:
        return _cache
    c = _cache
    f32 = lambda a: np.ascontiguousarray(np.asarray(a, dtype=np.float32))

    gate_w = f32(inputs["L0_ffn_gate_w"])
    up_w = f32(inputs["L0_ffn_up_w"])
    down_w = f32(inputs["L0_ffn_down_w"])

    def lhsT_ic(w):
        a = w.reshape(NIC, 128, NKC, 128)
        return np.ascontiguousarray(a.transpose(0, 3, 2, 1)).reshape(NIC, 128, D)
    c["guA"] = np.ascontiguousarray(np.concatenate([lhsT_ic(gate_w), lhsT_ic(up_w)], axis=2))
    dT = np.ascontiguousarray(down_w.T)
    a = dT.reshape(22, 2, 128, 4, 512)
    c["dA"] = np.ascontiguousarray(a.transpose(3, 0, 2, 1, 4)).reshape(4, 22, 128, 1024)
    c["gs"] = np.ascontiguousarray(f32(inputs["L0_ffn_gate_s"]).reshape(NIC, 128).T)
    c["us"] = np.ascontiguousarray(f32(inputs["L0_ffn_up_s"]).reshape(NIC, 128).T)
    c["ds_row"] = f32(inputs["L0_ffn_down_s"]).reshape(1, D)

    def lhsT_m(w):
        K, M = w.shape
        a = w.reshape(K // 128, 128, M // 128, 128)
        return np.ascontiguousarray(a.transpose(2, 1, 0, 3)).reshape(M // 128, 128, K)

    def rhs_nsl(w, nsl):
        K, N = w.shape
        ns = N // nsl
        a = w.reshape(K // 128, 128, nsl, ns)
        return np.ascontiguousarray(a.transpose(2, 1, 0, 3)).reshape(nsl, 128, (K // 128) * ns)

    c["wq"] = lhsT_m(f32(inputs["L0_attn_Wq"]))
    c["wk"] = lhsT_m(f32(inputs["L0_attn_Wk"]))
    c["wv"] = rhs_nsl(f32(inputs["L0_attn_Wv"]), 4)
    c["bq"] = np.ascontiguousarray(f32(inputs["L0_attn_bq"]).reshape(NKC, 128).T)
    c["bk"] = np.ascontiguousarray(f32(inputs["L0_attn_bk"]).reshape(NKC, 128).T)
    c["bv_row"] = f32(inputs["L0_attn_bv"]).reshape(1, D)
    c["wo"] = rhs_nsl(f32(inputs["L0_attn_Wo"]), 4)
    c["bo_row"] = f32(inputs["L0_attn_bo"]).reshape(1, D)

    c["w1"] = rhs_nsl(np.ascontiguousarray(f32(inputs["L1_ffn_weight"]).T), 4)
    c["b1_row"] = f32(inputs["L1_ffn_bias"]).reshape(1, D)

    pq = rope_perm(NH)
    pk = rope_perm(4)
    c["wq1"] = lhsT_m(f32(inputs["L1_attn_Wq"])[:, pq])
    c["bq1"] = np.ascontiguousarray(f32(inputs["L1_attn_bq"])[pq].reshape(NKC, 128).T)
    c["wk1"] = lhsT_m(f32(inputs["L1_attn_Wk"])[:, pk])
    c["bk1"] = np.ascontiguousarray(f32(inputs["L1_attn_bk"])[pk].reshape(4, 128).T)
    c["wv1"] = rhs_nsl(f32(inputs["L1_attn_Wv"]), 1)
    c["bv1_row"] = f32(inputs["L1_attn_bv"]).reshape(1, DKV1)
    c["wo1"] = rhs_nsl(f32(inputs["L1_attn_Wo"]), 4)
    c["bo1_row"] = f32(inputs["L1_attn_bo"]).reshape(1, D)

    c["ones_row"] = np.ones((1, 128), dtype=BF)
    c["half_row"] = np.full((1, 128), 0.5, dtype=BF)
    c["ones_col"] = np.ones((128, 1), dtype=BF)
    c["ones_full"] = np.ones((128, 128), dtype=BF)
    c["ident"] = np.eye(128, dtype=np.float32).astype(BF)

    freqs = 10000.0 ** (-(np.arange(0, 64, dtype=np.float64) * 2.0 / D))
    cos_c, sin_c, cmask_c, ksel_c = [], [], [], []
    for p in range(4):
        pos = np.concatenate([np.arange(128 * (p + 4 * j), 128 * (p + 4 * j) + 128) for j in range(8)])
        ang = np.outer(freqs, pos.astype(np.float64))
        cc = np.cos(ang).astype(np.float32)
        ss = np.sin(ang).astype(np.float32)
        cos_c.append(np.ascontiguousarray(np.concatenate([cc, cc], axis=0)))
        sin_c.append(np.ascontiguousarray(np.concatenate([ss, ss], axis=0)))
        ks = np.full((8, 32), -1e9, dtype=np.float32)
        for jq in range(8):
            for kt in range(32):
                gk = (kt // 8) + 4 * (kt % 8)
                if gk <= p + 4 * jq:
                    ks[jq, kt] = 0.0
        ksel_c.append(np.ascontiguousarray(ks.reshape(1, 256)))
        msk = np.zeros((32, 128, 128), dtype=np.float32)
        for kt in range(32):
            rr, jk = kt // 8, kt % 8
            gk = rr + 4 * jk
            jb = gk // 4
            gq = p + 4 * jb
            if gk > gq:
                msk[kt, :, :] = -1e30
            elif gk == gq:
                kk = np.arange(128)[:, None]
                qq = np.arange(128)[None, :]
                msk[kt][kk > qq] = -1e30
        cmask_c.append(np.ascontiguousarray(msk.astype(BF)))
    c["cos_t"] = cos_c
    c["sin_t"] = sin_c
    c["cmask"] = cmask_c
    c["ksel"] = ksel_c
    c["done"] = True
    return c


def _shard_x(x):
    out = []
    for cid in range(8):
        b, p = cid // 4, cid % 4
        rows = np.concatenate([x[b, 128 * (p + 4 * j):128 * (p + 4 * j) + 128] for j in range(8)])
        out.append(np.ascontiguousarray(rows))
    return out


def _gather_kv(KTs, Vs):
    ktf, vf = [], []
    for grp in range(2):
        kt = np.ascontiguousarray(np.concatenate([KTs[grp * 4 + r] for r in range(4)], axis=0))
        v = np.ascontiguousarray(np.concatenate([Vs[grp * 4 + r] for r in range(4)], axis=0))
        for r in range(4):
            ktf.append(kt)
            vf.append(v)
    return ktf, vf


def _get_nc(leg):
    if leg not in _nc_cache:
        _nc_cache[leg] = Leg(leg).build()
    return _nc_cache[leg]


def run_leg_a(c, xs):
    ncA = _get_nc("A")
    in_maps = []
    for cid in range(8):
        m = {k: c[k] for k in ("ones_row", "half_row", "ones_col", "ones_full", "ident",
                               "guA", "dA", "gs", "us", "ds_row",
                               "wq", "wk", "wv", "bq", "bk", "bv_row")}
        m["x_in"] = xs[cid]
        in_maps.append(m)
    return run_bass_kernel_spmd(ncA, in_maps, core_ids=list(range(8)))


def run_leg_b(c, x1, QT0, ktf, vf):
    ncB = _get_nc("B")
    in_maps = []
    for cid in range(8):
        m = {k: c[k] for k in ("ones_row", "half_row", "ones_col", "ones_full", "ident",
                               "guA", "dA", "gs", "us", "ds_row",
                               "wo", "bo_row", "w1", "b1_row",
                               "wq1", "bq1", "wk1", "bk1", "wv1", "bv1_row")}
        m["x_in"] = x1[cid]
        m["QT_in"] = QT0[cid]
        m["KT_full"] = ktf[cid]
        m["V_full"] = vf[cid]
        m["cos_t"] = c["cos_t"][cid % 4]
        m["sin_t"] = c["sin_t"][cid % 4]
        in_maps.append(m)
    return run_bass_kernel_spmd(ncB, in_maps, core_ids=list(range(8)))


def run_leg_c(c, x2, QT1, ktf1, vf1):
    ncC = _get_nc("C")
    in_maps = []
    for cid in range(8):
        m = {k: c[k] for k in ("ones_row", "half_row", "ones_col", "ones_full", "ident", "b1_row", "w1")}
        m["wo"] = c["wo1"]
        m["bo_row"] = c["bo1_row"]
        m["x_in"] = x2[cid]
        m["QT_in"] = QT1[cid]
        m["KT_full"] = ktf1[cid]
        m["V_full"] = vf1[cid]
        m["cmask"] = c["cmask"][cid % 4]
        m["ksel"] = c["ksel"][cid % 4]
        in_maps.append(m)
    return run_bass_kernel_spmd(ncC, in_maps, core_ids=list(range(8)))


def kernel(**inputs):
    c = _host_prep(inputs)
    x = np.asarray(inputs["L0_attn_X"], dtype=np.float32)
    xs = _shard_x(x)

    resA = run_leg_a(c, xs)
    x1 = [np.asarray(resA.results[i]["x_out"]) for i in range(8)]
    KT0 = [np.asarray(resA.results[i]["KT_out"]) for i in range(8)]
    V0 = [np.asarray(resA.results[i]["V_out"]) for i in range(8)]
    QT0 = [np.asarray(resA.results[i]["QT_out"]) for i in range(8)]
    ktf, vf = _gather_kv(KT0, V0)

    resB = run_leg_b(c, x1, QT0, ktf, vf)
    x2 = [np.asarray(resB.results[i]["x_out"]) for i in range(8)]
    QT1 = [np.asarray(resB.results[i]["QT_out"]) for i in range(8)]
    KT1 = [np.asarray(resB.results[i]["KT_out"]) for i in range(8)]
    V1 = [np.asarray(resB.results[i]["V_out"]) for i in range(8)]
    ktf1, vf1 = _gather_kv(KT1, V1)

    resC = run_leg_c(c, x2, QT1, ktf1, vf1)
    y = [np.asarray(resC.results[i]["x_out"]) for i in range(8)]

    out = np.empty((B, S, D), dtype=np.float32)
    for cid in range(8):
        b, p = cid // 4, cid % 4
        for j in range(8):
            out[b, 128 * (p + 4 * j):128 * (p + 4 * j) + 128] = y[cid][j * 128:(j + 1) * 128]
    return out


# version 11
# speedup vs baseline: 1.0320x; 1.0320x over previous
import math
import numpy as np
import ml_dtypes
import concourse.bass as bass
import concourse.mybir as mybir
from contextlib import ExitStack
from concourse.bass_utils import run_bass_kernel_spmd

F32 = mybir.dt.float32
BF16 = mybir.dt.bfloat16
AF = mybir.ActivationFunctionType
ALU = mybir.AluOpType
AX = mybir.AxisListType
BF = ml_dtypes.bfloat16

D = 2048
INTER = 5632
B = 2
S = 4096
TPC = 1024
NT = 8
NH = 16
DKV1 = 512
HD = 128
NKC = 16
NIC = 44
MARGIN = 44.0

ENGINES = ("tensor", "vector", "scalar", "sync", "gpsimd")
PRECAST = True
PRECAST_CONSUME = True


def cdiv(a, b):
    return (a + b - 1) // b


class PB:

    def __init__(self):
        self.items = {e: [] for e in ENGINES}
        self.label_count = {}
        self.phase_deps = []

    def add(self, eng, emit, deps=(), dma=False, phased=True, sem=None):
        ds = [d for d in deps if d is not None]
        if phased:
            ds = ds + self.phase_deps
        if dma:
            assert sem is not None
            k = self.label_count.get(sem, 0)
            self.label_count[sem] = k + 1
            tokn = ("D", sem, k)
            if k > 0:
                ds.append(("D", sem, k - 1))
        else:
            tokn = ("E", eng, len(self.items[eng]))
        self.items[eng].append({"emit": emit, "deps": ds, "dma": dma,
                                "marked": dma, "sem": sem, "tok": tokn})
        return tokn

    def last(self, eng):
        for it in reversed(self.items[eng]):
            if not it["dma"]:
                return it["tok"]
        return None

    def barrier(self, engs=("tensor", "vector", "scalar")):
        self.phase_deps = [t for t in (self.last(e) for e in engs) if t is not None]

    def render(self, nc, block, sems, lsems):
        eidx = {}
        for e in ENGINES:
            for i, it in enumerate(self.items[e]):
                eidx[("E", e, i)] = (e, i)
        for e in ENGINES:
            for it in self.items[e]:
                for d in it["deps"]:
                    if d[0] == "E":
                        e2, i2 = eidx[d]
                        self.items[e2][i2]["marked"] = True
        for e in ("vector", "scalar"):
            for it in self.items[e]:
                if not it["dma"]:
                    it["marked"] = True
        semval = {}
        for e in ENGINES:
            v, vals = 0, []
            for it in self.items[e]:
                if it["marked"] and not it["dma"]:
                    v += 1
                vals.append(v)
            semval[e] = vals
        tot = {lbl: 16 * n for lbl, n in self.label_count.items()}

        def runner(ename):
            auto_serial = ename in ("vector", "scalar")
            def body(eng):
                hw = {}
                nprev = 0
                for it in self.items[ename]:
                    if auto_serial and not it["dma"] and nprev > 0:
                        if nprev > hw.get(ename, 0):
                            eng.wait_ge(sems[ename], nprev)
                            hw[ename] = nprev
                    for d in it["deps"]:
                        if d[0] == "E":
                            e2, i2 = eidx[d]
                            if e2 == ename:
                                continue
                            key, v = e2, semval[e2][i2]
                            sm = sems[e2]
                        else:
                            _, lbl, k = d
                            key = "D" + lbl
                            v = 16 * (k + 1)
                            sm = lsems[lbl]
                        if v > hw.get(key, 0):
                            eng.wait_ge(sm, v)
                            hw[key] = v
                    inst = it["emit"](eng)
                    if it["dma"]:
                        inst.then_inc(lsems[it["sem"]], 16)
                    elif it["marked"]:
                        assert inst is not None
                        inst.then_inc(sems[ename], 1)
                        nprev = semval[ename][self.items[ename].index(it)] if False else nprev + 1
            return body

        block.tensor(runner("tensor"))
        block.vector(runner("vector"))
        block.scalar(runner("scalar"))
        block.sync(runner("sync"))
        block.gpsimd(runner("gpsimd"))


class Leg:
    def __init__(self, leg, dbg=False):
        self.dbg = dbg
        self.leg = leg
        self.nc = bass.Bass("TRN2", target_bir_lowering=False,
                            dynamic_dma_scratch_size=24576)
        self.pb = PB()
        self.t = {}
        self.tok = {}
        self.m = {}
        self.off = {"f": 0, "b": 0}

    def res(self, space, name, cols, at=None, at_off=0):
        if at is not None:
            o = self.m[at][1] + at_off
        else:
            o = self.off[space]
            self.off[space] += cols
        self.m[name] = (space, o, cols)

    def V(self, name, p0=0, p1=128, c0=0, c1=None):
        sp, off, cols = self.m[name]
        if c1 is None:
            c1 = cols
        a = self.f32a if sp == "f" else self.bfa
        return a[p0:p1, off + c0:off + c1]

    def din(self, name, shape, dt=F32):
        self.t[name] = self.nc.dram_tensor(name, list(shape), dt, kind="ExternalInput").ap()

    def dout(self, name, shape, dt=F32):
        self.t[name] = self.nc.dram_tensor(name, list(shape), dt, kind="ExternalOutput").ap()


    def one_shot(self, name, emit, eng="sync"):
        lbl = "ld" if eng == "sync" else "ldg"
        self.tok[name] = self.pb.add(eng, emit, dma=True, phased=False, sem=lbl)

    def consts(self):
        V, t = self.V, self.t
        self.one_shot("onesr", lambda e: e.dma_start(out=V("onesr", 0, 1), in_=t["ones_row"]))
        self.one_shot("halfr", lambda e: e.dma_start(out=V("halfr", 0, 1), in_=t["half_row"]))
        self.one_shot("onesc", lambda e: e.dma_start(out=V("onesc"), in_=t["ones_col"]))
        self.one_shot("onesF", lambda e: e.dma_start(out=V("onesF"), in_=t["ones_full"]))
        self.one_shot("ident", lambda e: e.dma_start(out=V("ident"), in_=t["ident"]))
        self.one_shot("x", lambda e: e.dma_start(
            out=V("x").rearrange("p (t d) -> p t d", t=NT),
            in_=t["x_in"].rearrange("(t p) d -> p t d", p=128)))
        if self.leg in ("A", "B"):
            self.one_shot("gs", lambda e: e.dma_start(out=V("gsc"), in_=t["gs"]))
            self.one_shot("us", lambda e: e.dma_start(out=V("usc"), in_=t["us"]))
            self.one_shot("ds", lambda e: e.dma_start(out=V("ds_bc", 0, 1, 0, D), in_=t["ds_row"]))
        if self.leg == "A":
            self.one_shot("b_wk", lambda e: e.dma_start(out=V("bstat", c0=0, c1=NKC), in_=t["bk"]))
            self.one_shot("b_wq", lambda e: e.dma_start(out=V("bstat", c0=16, c1=16 + NKC), in_=t["bq"]))
        if self.leg == "B":
            self.one_shot("b_wk1", lambda e: e.dma_start(out=V("bstat", c0=0, c1=4), in_=t["bk1"]))
            self.one_shot("b_wq1", lambda e: e.dma_start(out=V("bstat", c0=16, c1=16 + NKC), in_=t["bq1"]))
        if self.leg in ("B", "C"):
            self.one_shot("QT_ld0", lambda e: e.dma_start(
                out=V("QT").rearrange("p (m q) -> p m q", m=NKC),
                in_=t["QT_in"].rearrange("(m p) q -> p m q", p=128)))
            self.one_shot("bo", lambda e: e.dma_start(
                out=V("browB", 0, 1, 0, D), in_=t["bo_row"]), eng="gpsimd")
        if self.leg == "C":
            self.one_shot("cm0", lambda e: e.dma_start(
                out=V("cm").rearrange("p (k q) -> p k q", k=32),
                in_=t["cmask"].rearrange("k p q -> p k q")))
            self.one_shot("ksel", lambda e: e.dma_start(
                out=V("kselb", 0, 1), in_=t["ksel"]))

    def load_row_bf(self, src, dst_c0, cols, deps=(), sem="rowb"):
        pb, V, t = self.pb, self.V, self.t
        return pb.add("gpsimd", lambda e: e.dma_start(
            out=V("browB", 0, 1, dst_c0, dst_c0 + cols), in_=t[src]),
            dma=True, phased=False, deps=list(deps), sem=sem)

    def bcast_hilo(self, src_name, src_p, src_c0, width, dst_name, dst_c0, psum, deps,
                   extra_mm_deps=()):
        pb, V = self.pb, self.V
        p0, p1 = src_p, src_p + 1
        src = lambda: V(src_name, p0, p1, src_c0, src_c0 + width)
        hi = pb.add("vector", lambda e: e.tensor_copy(
            V("hilo", p0, p1, 0, width), src()), deps=list(deps))
        lo1 = pb.add("vector", lambda e: e.tensor_tensor(
            V("hiloF", p0, p1, 0, width), src(), V("hilo", p0, p1, 0, width), ALU.subtract),
            deps=[hi])
        lo = pb.add("vector", lambda e: e.tensor_copy(
            V("hilo", p0, p1, width, 2 * width), V("hiloF", p0, p1, 0, width)), deps=[lo1])
        ones = (lambda: V("onesr", 0, 1)) if src_p == 0 else (lambda: V("onesF", p0, p1))
        mms = []
        for cc in range(0, width, 512):
            ce = min(width, cc + 512)
            pb.add("tensor", lambda e, cc=cc, ce=ce: e.matmul(
                psum[:, cc:ce], ones(), V("hilo", p0, p1, cc, ce),
                start=True, stop=False), deps=[hi, self.tok["onesr"], self.tok["onesF"]] + list(extra_mm_deps))
            mms.append(pb.add("tensor", lambda e, cc=cc, ce=ce: e.matmul(
                psum[:, cc:ce], ones(), V("hilo", p0, p1, width + cc, width + ce),
                start=False, stop=True), deps=[lo]))
        return pb.add("vector", lambda e: e.tensor_copy(
            V(dst_name, 0, 128, dst_c0, dst_c0 + width), psum[:, 0:width]), deps=mms)

    def emit_ln(self, tag):
        pb, V = self.pb, self.V
        tok = self.tok
        sq_toks, mean_toks = [], []
        for tt in range(NT):
            sq_toks.append(pb.add("scalar", lambda e, tt=tt: e.activation(
                V("sq"), V("x", c0=tt * D, c1=(tt + 1) * D), AF.Square,
                accum_out=V("stat", c0=8 + tt, c1=9 + tt)), deps=[tok["x"]]))
            mean_toks.append(pb.add("vector", lambda e, tt=tt: e.tensor_reduce(
                V("stat", c0=tt, c1=tt + 1), V("x", c0=tt * D, c1=(tt + 1) * D),
                AX.X, ALU.add), deps=[tok["x"]]))
        tmean = pb.add("vector", lambda e: e.tensor_scalar(
            V("stat", c0=16, c1=24), V("stat", c0=0, c1=8), -1.0 / D, None, ALU.mult),
            deps=mean_toks)
        tmsq = pb.add("vector", lambda e: e.tensor_tensor(
            V("stat", c0=24, c1=32), V("stat", c0=16, c1=24), V("stat", c0=16, c1=24), ALU.mult),
            deps=[tmean])
        ts2 = pb.add("vector", lambda e: e.tensor_scalar(
            V("stat", c0=32, c1=40), V("stat", c0=8, c1=16), 1.0 / D, None, ALU.mult),
            deps=sq_toks)
        tvar = pb.add("vector", lambda e: e.tensor_tensor(
            V("stat", c0=40, c1=48), V("stat", c0=32, c1=40), V("stat", c0=24, c1=32), ALU.subtract),
            deps=[tmsq, ts2])
        tsd = pb.add("scalar", lambda e: e.activation(
            V("stat", c0=48, c1=56), V("stat", c0=40, c1=48), AF.Sqrt, bias=np.float32(1e-5)), deps=[tvar])
        trstd = pb.add("vector", lambda e: e.reciprocal(
            V("stat", c0=56, c1=64), V("stat", c0=48, c1=56)), deps=[tsd])

        copy_toks = {}
        for tt in range(NT):
            deps = [trstd, tok.get(("stageW", 0)), tok.get(("stageW", 1))]
            if tt >= 1:
                deps.append(copy_toks[(tt - 1, 1)])
            xn_t = pb.add("vector", lambda e, tt=tt: e.tensor_scalar(
                V("xn"), V("x", c0=tt * D, c1=(tt + 1) * D),
                V("stat", c0=16 + tt, c1=17 + tt), V("stat", c0=56 + tt, c1=57 + tt),
                ALU.add, ALU.mult), deps=deps)
            for rnd in range(2):
                tps = []
                for jj in range(8):
                    j = rnd * 8 + jj
                    d2 = [xn_t, tok["ident"]]
                    if jj == 0:
                        prev = (tt, 0) if rnd == 1 else ((tt - 1, 1) if tt >= 1 else None)
                        if prev in copy_toks:
                            d2.append(copy_toks[prev])
                    tps.append(pb.add("tensor", lambda e, j=j, jj=jj: e.transpose(
                        self.ps_tb[:, jj * 128:(jj + 1) * 128],
                        V("xn", c0=j * 128, c1=(j + 1) * 128),
                        V("ident")), deps=d2))
                sp, xoff, _ = self.m["xnT"]
                a = self.f32a if sp == "f" else self.bfa
                def emc(e, tt=tt, rnd=rnd, xoff=xoff, a=a):
                    dst = a[:, xoff + rnd * 8 * TPC: xoff + (rnd + 1) * 8 * TPC]
                    dstv = dst.rearrange("p (j q) -> p j q", j=8)[:, :, tt * 128:(tt + 1) * 128]
                    return e.tensor_copy(dstv, self.ps_tb[:, :].rearrange("p (j q) -> p j q", j=8))
                copy_toks[(tt, rnd)] = pb.add("vector", emc, deps=[tps[-1]])
        tok["xnT_done"] = list(copy_toks.values())
        pb.barrier()

    def xnT(self, kc, q0, q1):
        return self.V("xnT", c0=kc * TPC + q0, c1=kc * TPC + q1)

    def emit_ffn0(self):
        pb, V, t, tok = self.pb, self.V, self.t, self.tok
        pre = self.leg == "B" and PRECAST and PRECAST_CONSUME
        gu_src = t["guB"] if pre else t["guA"]
        d_src = t["dB"] if pre else t["dA"]
        w_eng = "sync" if pre else "gpsimd"
        gs_t = tok["gs"]
        us_t = tok["us"]
        ds_ld = tok["ds"]
        ds_half = pb.add("vector", lambda e: e.tensor_scalar(
            V("ds_bc", 0, 1, 0, D), V("ds_bc", 0, 1, 0, D), 0.5, None, ALU.mult),
            deps=[ds_ld] + list(self.pb.phase_deps))
        dsbc = ds_half
        for qd in range(4):
            dsbc = self.bcast_hilo("ds_bc", 0, qd * 512, 512, "ds_bc2", qd * 512,
                                   self.ps_a, [ds_half, dsbc])

        hdn_tok = {}
        gu_dma = {}
        cons_gu = {}
        epi_done = {}
        dn_ev = {}
        gcons = {}
        def emit_gu_dma(half, ic):
            if ic >= NIC:
                return
            sl = ic % 2
            war = [cons_gu.get((half, ic - 2))]
            if half == 1 and ic < 2:
                war.append(cons_gu.get((0, NIC - 2 + ic)))
            if half == 0 and ic < 2:
                war = war + tok["xnT_done"] + [tok.get("wgu_gate")]
            if pre and ic < 2 and half == 0:
                war = war + (tok.get("precast") or [None])
            gu_dma[(half, ic)] = pb.add(w_eng, lambda e, ic=ic, sl=sl: e.dma_start(
                out=V("wgu", c0=sl * 2 * D, c1=(sl + 1) * 2 * D), in_=gu_src[ic]),
                deps=war, dma=True, phased=False, sem=f"gu{sl}")

        for half in range(2):
            q0, q1 = half * 512, (half + 1) * 512
            if half == 0:
                emit_gu_dma(0, 0)
                emit_gu_dma(0, 1)
            for ic in range(NIC):
                sl = ic % 2
                psl = ic % 2
                mg = mu = None
                d_first = [gu_dma[(half, ic)], epi_done.get((half, ic - 2)), dsbc] + tok["xnT_done"]
                if half == 1 and ic < 2:
                    d_first.append(epi_done.get((0, NIC - 2 + ic)))
                    d_first += [dn_ev.get((0, 3, tx)) for tx in range(4)]
                for kc in range(NKC):
                    mg = pb.add("tensor", lambda e, sl=sl, kc=kc, psl=psl, q0=q0, q1=q1: e.matmul(
                        self.ps_a[:, psl * 1024: psl * 1024 + 512],
                        V("wgu", c0=sl * 2 * D + kc * 128, c1=sl * 2 * D + (kc + 1) * 128),
                        self.xnT(kc, q0, q1),
                        start=(kc == 0), stop=(kc == NKC - 1)),
                        deps=(d_first if kc == 0 else []))
                for kc in range(NKC):
                    mu = pb.add("tensor", lambda e, sl=sl, kc=kc, psl=psl, q0=q0, q1=q1: e.matmul(
                        self.ps_a[:, psl * 1024 + 512: psl * 1024 + 1024],
                        V("wgu", c0=sl * 2 * D + D + kc * 128, c1=sl * 2 * D + D + (kc + 1) * 128),
                        self.xnT(kc, q0, q1),
                        start=(kc == 0), stop=(kc == NKC - 1)), deps=[])
                cons_gu[(half, ic)] = mu
                if ic + 2 < NIC:
                    emit_gu_dma(half, ic + 2)
                elif half == 0:
                    emit_gu_dma(1, ic + 2 - NIC)
                ty = pb.add("vector", lambda e, ic=ic, psl=psl: e.tensor_scalar(
                    V("scr0"), self.ps_a[:, psl * 1024: psl * 1024 + 512],
                    V("gsc", c0=ic, c1=ic + 1), None, ALU.mult), deps=[mg, gs_t])
                tsl = pb.add("scalar", lambda e: e.activation(V("sbf"), V("scr0"), AF.Silu),
                             deps=[ty, hdn_tok.get((half, ic - 1), hdn_tok.get((half - 1, NIC - 1)))])
                tu = pb.add("vector", lambda e, ic=ic, psl=psl: e.tensor_scalar(
                    V("scr1"), self.ps_a[:, psl * 1024 + 512: psl * 1024 + 1024],
                    V("usc", c0=ic, c1=ic + 1), None, ALU.mult), deps=[mu, us_t])
                hdn_tok[(half, ic)] = pb.add("vector", lambda e, ic=ic: e.tensor_tensor(
                    V("hdn", c0=ic * 512, c1=(ic + 1) * 512), V("sbf"), V("scr1"), ALU.mult),
                    deps=[tsl, tu])
                epi_done[(half, ic)] = hdn_tok[(half, ic)]
            hdeps = [hdn_tok[(half, ic)] for ic in range(NIC)]
            if self.dbg and half == 0:
                self.dout("d_hdn0", (128, NIC * 512), BF16)
                tk = pb.add("sync", lambda e: e.dma_start(
                    out=self.t["d_hdn0"], in_=V("hdn")),
                    deps=hdeps, dma=True, phased=False, sem="dbg")
                pb.add("sync", lambda e: None, deps=[tk], phased=False)
            groups = [self.ps_b[:, 0:512], self.ps_b[:, 512:1024],
                      self.ps_a[:, 0:512], self.ps_a[:, 512:1024]]
            dw_dma = {}

            def emit_dw_dma(half, seq):
                if seq >= 44:
                    return
                n2, j11 = seq // 11, seq % 11
                gseq = half * 44 + seq
                sl = gseq % 2
                war = [gcons.get(gseq - 2)]
                if gseq < 2:
                    war = war + tok["xnT_done"] + [tok.get("dwn_gate")]
                if pre and gseq < 2:
                    war = war + (tok.get("precast") or [None])
                dw_dma[(n2, j11)] = (sl, pb.add(w_eng, lambda e, n2=n2, j11=j11, sl=sl: e.dma_start(
                    out=V("dwn", c0=sl * 2048, c1=(sl + 1) * 2048).rearrange("p (g c) -> p g c", g=2),
                    in_=d_src[n2, 2 * j11:2 * j11 + 2].rearrange("g p c -> p g c")),
                    deps=war, dma=True, phased=False, sem=f"dw{sl}"))

            emit_dw_dma(half, 0)
            emit_dw_dma(half, 1)
            for n in range(4):
                last = {}
                for j11 in range(11):
                    gseq = half * 44 + n * 11 + j11
                    sl, dtok = dw_dma[(n, j11)]
                    for kc in range(4):
                        ic = j11 * 4 + kc
                        for t4 in range(4):
                            d0 = []
                            if ic == 0 and t4 == 0:
                                d0 = hdeps + [dtok] + [dn_ev.get((half, n - 1, tx)) for tx in range(4)]
                                if n == 0:
                                    d0 += [cons_gu[(half, NIC - 1)], dsbc]
                                    d0 += [dn_ev.get((half - 1, 3, tx)) for tx in range(4)]
                            elif kc == 0 and t4 == 0:
                                d0 = [dtok]
                            last[t4] = pb.add("tensor", lambda e, ic=ic, sl=sl, kc=kc, t4=t4, grp=groups[t4]: e.matmul(
                                grp,
                                V("hdn", c0=ic * 512 + t4 * 128, c1=ic * 512 + (t4 + 1) * 128),
                                V("dwn", c0=sl * 2048 + kc * 512, c1=sl * 2048 + (kc + 1) * 512),
                                start=(ic == 0), stop=(ic == NIC - 1)), deps=d0)
                    gcons[gseq] = last[3]
                    emit_dw_dma(half, n * 11 + j11 + 2)
                for t4 in range(4):
                    tt = half * 4 + t4
                    scr = "scr0" if t4 % 2 == 0 else "scr1"
                    tm = pb.add("vector", lambda e, n=n, scr=scr, grp=groups[t4]: e.tensor_tensor(
                        V(scr), grp, V("ds_bc2", c0=n * 512, c1=(n + 1) * 512), ALU.mult),
                        deps=[last[t4], dsbc, hdn_tok[(half, NIC - 1)]])
                    dn_ev[(half, n, t4)] = pb.add("vector", lambda e, tt=tt, n=n, scr=scr: e.tensor_tensor(
                        V("x", c0=tt * D + n * 512, c1=tt * D + (n + 1) * 512),
                        V(scr),
                        V("x", c0=tt * D + n * 512, c1=tt * D + (n + 1) * 512), ALU.add),
                        deps=[tm])
        tok["gu_last"] = cons_gu[(1, NIC - 1)]
        tok["dw_last"] = gcons[87]
        if self.dbg:
            self.dout("d_xh0", (128, NT * D), F32)
            tk = pb.add("sync", lambda e: e.dma_start(
                out=self.t["d_xh0"], in_=V("x")),
                deps=[dn_ev[(0, n2, t2)] for n2 in range(4) for t2 in range(4)],
                dma=True, phased=False, sem="dbg")
            pb.add("sync", lambda e: None, deps=[tk], phased=False)
        pb.barrier()

    def emit_proj_fm(self, w_name, b_name, nm, out_name, bias_c0, scale=None, rope=False):
        pb, V, t, tok = self.pb, self.V, self.t, self.tok
        bq = tok["b_" + w_name]
        if scale is not None:
            bq = pb.add("vector", lambda e: e.tensor_scalar(
                V("bstat", c0=bias_c0, c1=bias_c0 + nm),
                V("bstat", c0=bias_c0, c1=bias_c0 + nm), scale, None, ALU.mult),
                deps=[bq] + list(self.pb.phase_deps))
        wdma, cons, ev_tok, st_tok = {}, {}, {}, {}

        pre2 = self.leg == "B" and PRECAST and PRECAST_CONSUME and w_name in ("wk1", "wq1")
        w_src = t[w_name + "B"] if pre2 else t[w_name]
        wp_eng = "sync" if pre2 else "gpsimd"

        def emit_wdma(mm_):
            if mm_ >= nm:
                return
            sl = mm_ % 2
            war = [cons.get(mm_ - 2)]
            if mm_ < 2:
                war.append(tok.get("wproj_gate"))
                if pre2:
                    war = war + (tok.get("precast_wp") or [None])
            wdma[mm_] = pb.add(wp_eng, lambda e, m=mm_, sl=sl: e.dma_start(
                out=V("wproj", c0=sl * D, c1=(sl + 1) * D), in_=w_src[m]),
                deps=war, dma=True, phased=False, sem=f"wp{sl}")

        emit_wdma(0)
        emit_wdma(1)
        for m in range(nm):
            sl = m % 2
            psl = self.ps_b if m % 2 == 0 else self.ps_a
            mmt = None
            for qh in range(2):
                for kc in range(NKC):
                    d0 = ([wdma[m], ev_tok.get(m - 2)] + tok["xnT_done"]) if (qh == 0 and kc == 0) else []
                    mmt = pb.add("tensor", lambda e, m=m, sl=sl, kc=kc, qh=qh, psl=psl: e.matmul(
                        psl[:, qh * 512:(qh + 1) * 512],
                        V("wproj", c0=sl * D + kc * 128, c1=sl * D + (kc + 1) * 128),
                        self.xnT(kc, qh * 512, (qh + 1) * 512),
                        start=(kc == 0), stop=(kc == NKC - 1)), deps=d0)
            cons[m] = mmt
            emit_wdma(m + 2)
            ssl = m % 2
            sc = 1.0 if scale is None else scale
            slot_war = st_tok.get(m - 2, tok.get(("stageW", ssl)))
            if not rope:
                ev = pb.add("scalar", lambda e, m=m, psl=psl, ssl=ssl, sc=sc: e.activation(
                    V("stage", c0=ssl * D, c1=ssl * D + TPC), psl[:, 0:1024], AF.Identity,
                    bias=V("bstat", c0=bias_c0 + m, c1=bias_c0 + m + 1), scale=np.float32(sc)),
                    deps=[mmt, bq, slot_war])
            else:
                evf = pb.add("scalar", lambda e, m=m, psl=psl: e.activation(
                    V("qf"), psl[:, 0:1024], AF.Identity,
                    bias=V("bstat", c0=bias_c0 + m, c1=bias_c0 + m + 1), scale=np.float32(1.0)),
                    deps=[mmt, bq, slot_war, ev_tok.get(m - 1)])
                ev = self.emit_rope(evf, ssl, prev_ev=ev_tok.get(m - 1), slot_war=slot_war)
            ev_tok[m] = ev
            st_tok[m] = pb.add("sync", lambda e, m=m, ssl=ssl: e.dma_start(
                out=t[out_name][m * 128:(m + 1) * 128, :],
                in_=V("stage", c0=ssl * D, c1=ssl * D + TPC)),
                deps=[ev], dma=True, phased=False, sem=f"st{ssl}")
            tok[("stageW", ssl)] = st_tok[m]
        tok["fm_last"] = cons[nm - 1]
        for m in range(nm):
            self.pb.add("sync", lambda e: None, deps=[st_tok[m]], phased=False)
        pb.barrier()

    def emit_rope(self, evf, ssl, prev_ev=None, slot_war=None):
        pb, V = self.pb, self.V
        sw0 = pb.add("sync", lambda e: e.dma_start(
            out=V("qswap", 0, 64), in_=V("qf", 64, 128)), deps=[evf, prev_ev],
            dma=True, phased=False, sem="sw")
        sw1 = pb.add("sync", lambda e: e.dma_start(
            out=V("qswap", 64, 128), in_=V("qf", 0, 64)), deps=[evf, prev_ev],
            dma=True, phased=False, sem="sw")
        outs = []
        for hh in range(2):
            c0, c1 = hh * 512, (hh + 1) * 512
            ta = pb.add("vector", lambda e, c0=c0, c1=c1: e.tensor_tensor(
                V("scr0", 0, 64), V("qf", 0, 64, c0, c1), V("cs_c", 0, 64, c0, c1), ALU.mult),
                deps=[evf, self.tok["cs"], outs[-1] if outs else None])
            tb = pb.add("vector", lambda e, c0=c0, c1=c1: e.tensor_tensor(
                V("scr1", 0, 64), V("qswap", 0, 64, c0, c1), V("cs_s", 0, 64, c0, c1), ALU.mult),
                deps=[sw0, self.tok["cs"]])
            o1 = pb.add("vector", lambda e, c0=c0, c1=c1, ssl=ssl: e.tensor_tensor(
                V("stage", 0, 64, ssl * D + c0, ssl * D + c1),
                V("scr0", 0, 64), V("scr1", 0, 64), ALU.subtract), deps=[ta, tb, slot_war])
            tc = pb.add("vector", lambda e, c0=c0, c1=c1: e.tensor_tensor(
                V("scr0", 64, 128), V("qswap", 64, 128, c0, c1), V("cs_s", 64, 128, c0, c1), ALU.mult),
                deps=[sw1, o1])
            td = pb.add("vector", lambda e, c0=c0, c1=c1: e.tensor_tensor(
                V("scr1", 64, 128), V("qf", 64, 128, c0, c1), V("cs_c", 64, 128, c0, c1), ALU.mult),
                deps=[o1])
            o2 = pb.add("vector", lambda e, c0=c0, c1=c1, ssl=ssl: e.tensor_tensor(
                V("stage", 64, 128, ssl * D + c0, ssl * D + c1),
                V("scr0", 64, 128), V("scr1", 64, 128), ALU.add), deps=[tc, td])
            outs.append(o2)
        return outs[-1]

    def emit_proj_tm(self, w_name, wbuf, bias_tok, out_name, width):
        pb, V, t, tok = self.pb, self.V, self.t, self.tok
        pre2 = self.leg == "B" and PRECAST and PRECAST_CONSUME and w_name == "wv1"
        wv_src = t[w_name + "B"] if pre2 else t[w_name]
        wv_eng = "sync" if pre2 else "gpsimd"
        nsl = width // 512
        cons_prev, ev, stt = {}, {}, {}
        for n in range(nsl):
            war = [cons_prev.get(n - 1)]
            if n == 0:
                war.append(tok.get("wvsl_gate"))
                if pre2:
                    war = war + (tok.get("precast_wv") or [None])
            wd = pb.add(wv_eng, lambda e, n=n: e.dma_start(
                out=V(wbuf, c0=0, c1=NKC * 512), in_=wv_src[n]),
                deps=war, dma=True, phased=False, sem="wv0")
            for tt in range(NT):
                idx = n * NT + tt
                sl = idx % 2
                mmt = None
                prev_psum = ev.get(idx - 2)
                for kc in range(NKC):
                    d0 = ([wd, prev_psum] + tok["xnT_done"]) if kc == 0 else []
                    mmt = pb.add("tensor", lambda e, n=n, tt=tt, kc=kc: e.matmul(
                        self.ps_b[:, (tt % 2) * 512:(tt % 2) * 512 + 512],
                        self.xnT(kc, tt * 128, (tt + 1) * 128),
                        V(wbuf, c0=kc * 512, c1=(kc + 1) * 512),
                        start=(kc == 0), stop=False), deps=d0)
                mb = pb.add("tensor", lambda e, n=n, tt=tt: e.matmul(
                    self.ps_b[:, (tt % 2) * 512:(tt % 2) * 512 + 512],
                    V("onesr", 0, 1),
                    V("browB", 0, 1, n * 512, (n + 1) * 512),
                    start=False, stop=True), deps=[bias_tok, tok["onesr"]])
                cons_prev[n] = mb
                slot_war = stt.get(idx - 2, tok.get(("stageW", sl)))
                ev[idx] = pb.add("vector", lambda e, tt=tt, sl=sl: e.tensor_copy(
                    V("stage", c0=sl * 512, c1=(sl + 1) * 512),
                    self.ps_b[:, (tt % 2) * 512:(tt % 2) * 512 + 512]),
                    deps=[mb, slot_war])
                stt[idx] = pb.add("sync", lambda e, n=n, tt=tt, sl=sl: e.dma_start(
                    out=t[out_name][tt * 128:(tt + 1) * 128, n * 512:(n + 1) * 512],
                    in_=V("stage", c0=sl * 512, c1=(sl + 1) * 512)),
                    deps=[ev[idx]], dma=True, phased=False, sem=f"sv{sl}")
                tok[("stageW", sl)] = stt[idx]
        for k in stt:
            self.pb.add("sync", lambda e: None, deps=[stt[k]], phased=False)
        pb.barrier()

    def emit_out_proj(self, w_name, b_name):
        pb, V, t, tok = self.pb, self.V, self.t, self.tok
        pre2 = self.leg in ("B", "C") and PRECAST and PRECAST_CONSUME and w_name == "wo"
        wo_src = t["woB"] if pre2 else t[w_name]
        wo_eng = "sync" if pre2 else "gpsimd"
        brow_t = tok["bo"]
        cons_prev = {}
        opev = {}
        dbuf = self.leg == "C"
        for n in range(4):
            wb = "wosl2" if (dbuf and n % 2 == 1) else "wosl"
            war = [cons_prev.get(n - 2 if dbuf else n - 1)]
            if n < (2 if dbuf else 1):
                war.append(tok.get("wosl_gate"))
                if pre2:
                    war = war + (tok.get("precast_wo") or [None])
            wd = pb.add(wo_eng, lambda e, n=n, wb=wb: e.dma_start(
                out=V(wb, c0=0, c1=NKC * 512), in_=wo_src[n]),
                deps=war, dma=True, phased=False, sem=f"wo{n % 2 if dbuf else 0}")
            for tt in range(NT):
                mmt = None
                pev = opev.get((n, tt - 2)) if tt >= 2 else opev.get((n - 1, NT - 2 + tt))
                for kc in range(NKC):
                    d0 = [wd, pev] if kc == 0 else []
                    mmt = pb.add("tensor", lambda e, n=n, tt=tt, kc=kc, wb=wb: e.matmul(
                        self.ps_b[:, (tt % 2) * 512:(tt % 2) * 512 + 512],
                        V("onT", c0=kc * TPC + tt * 128, c1=kc * TPC + (tt + 1) * 128),
                        V(wb, c0=kc * 512, c1=(kc + 1) * 512),
                        start=(kc == 0), stop=False), deps=d0)
                mb = pb.add("tensor", lambda e, n=n, tt=tt: e.matmul(
                    self.ps_b[:, (tt % 2) * 512:(tt % 2) * 512 + 512],
                    V("onesr", 0, 1), V("browB", 0, 1, n * 512, (n + 1) * 512),
                    start=False, stop=True), deps=[brow_t, tok["onesr"]])
                cons_prev[n] = mb
                opev[(n, tt)] = pb.add("vector", lambda e, n=n, tt=tt: e.tensor_tensor(
                    V("x", c0=tt * D + n * 512, c1=tt * D + (n + 1) * 512),
                    self.ps_b[:, (tt % 2) * 512:(tt % 2) * 512 + 512],
                    V("x", c0=tt * D + n * 512, c1=tt * D + (n + 1) * 512), ALU.add),
                    deps=[mb])
        tok["bo_last"] = cons_prev[3]
        pb.barrier()

    def emit_ffn1(self):
        pb, V, t, tok = self.pb, self.V, self.t, self.tok
        brow_t = self.load_row_bf("b1_row", 0, D, deps=[tok.get("bo_last")])
        f1ev = {}
        psc_free = tok.get("psc_free_f1")
        for n in range(4):
            amms, sgn = [], []
            for kc in range(NKC):
                sl = kc % 2
                war = [sgn[kc - 2] if kc >= 2 else None]
                if kc < 2:
                    war += [f1ev.get((n - 1, NT - 1)), tok.get("w1f_gate")]
                wd = pb.add("gpsimd", lambda e, n=n, kc=kc, sl=sl: e.dma_start(
                    out=V("w1f", c0=sl * 512, c1=(sl + 1) * 512),
                    in_=t["w1"][n][:, kc * 512:(kc + 1) * 512]),
                    deps=war, dma=True, phased=False, sem=f"w1{sl}")
                ab = pb.add("scalar", lambda e, sl=sl: e.activation(
                    V("sbf"), V("w1f", c0=sl * 512, c1=(sl + 1) * 512), AF.Abs),
                    deps=[wd, amms[kc - 1] if kc >= 1 else tok.get("amms_last")])
                amms.append(pb.add("tensor", lambda e, kc=kc: e.matmul(
                    self.ps_c[0:1, 0:512], V("onesc"), V("sbf"),
                    start=(kc == 0), stop=(kc == NKC - 1)),
                    deps=([ab, tok["onesc"], psc_free] if kc == 0 else [ab])))
                sgn.append(pb.add("scalar", lambda e, kc=kc, sl=sl: e.activation(
                    V("w1sl", c0=kc * 512, c1=(kc + 1) * 512),
                    V("w1f", c0=sl * 512, c1=(sl + 1) * 512), AF.Sign), deps=[wd]))
            al = pb.add("vector", lambda e: e.tensor_scalar(
                V("rows", 0, 1, 0, 512), self.ps_c[0:1, 0:512], 0.5 / D, None, ALU.mult),
                deps=[amms[-1]])
            tok["amms_last"] = amms[-1]
            psc_free = al
            albc = self.bcast_hilo("rows", 0, 0, 512, "abc", 0, self.ps_a, [al])
            bw = []
            for kc in range(NKC):
                bw.append(pb.add("vector", lambda e, kc=kc: e.tensor_tensor(
                    V("w1sl", c0=kc * 512, c1=(kc + 1) * 512),
                    V("w1sl", c0=kc * 512, c1=(kc + 1) * 512),
                    V("abc", c0=0, c1=512), ALU.mult), deps=[albc, sgn[kc]]))
            for tt in range(NT):
                mmt = None
                pev = f1ev.get((n, tt - 2)) if tt >= 2 else f1ev.get((n - 1, NT - 2 + tt))
                for kc in range(NKC):
                    d0 = (bw + tok["xnT_done"] + [pev]) if kc == 0 else []
                    mmt = pb.add("tensor", lambda e, n=n, tt=tt, kc=kc: e.matmul(
                        self.ps_b[:, (tt % 2) * 512:(tt % 2) * 512 + 512],
                        self.xnT(kc, tt * 128, (tt + 1) * 128),
                        V("w1sl", c0=kc * 512, c1=(kc + 1) * 512),
                        start=(kc == 0), stop=False), deps=d0)
                mb = pb.add("tensor", lambda e, n=n, tt=tt: e.matmul(
                    self.ps_b[:, (tt % 2) * 512:(tt % 2) * 512 + 512],
                    V("halfr", 0, 1), V("browB", 0, 1, n * 512, (n + 1) * 512),
                    start=False, stop=True), deps=[brow_t, tok["halfr"]])
                f1ev[(n, tt)] = pb.add("vector", lambda e, n=n, tt=tt: e.tensor_tensor(
                    V("x", c0=tt * D + n * 512, c1=tt * D + (n + 1) * 512),
                    self.ps_b[:, (tt % 2) * 512:(tt % 2) * 512 + 512],
                    V("x", c0=tt * D + n * 512, c1=tt * D + (n + 1) * 512), ALU.add),
                    deps=[mb])
        tok["psc_free_f1"] = psc_free
        tok["b1_last"] = pb.last("tensor")
        pb.barrier()

    def emit_attention(self, layer):
        pb, V, t, tok = self.pb, self.V, self.t, self.tok
        causal = (layer == 1)
        dkv = DKV1 if causal else D
        post = (2048.0 ** -0.5) if causal else 1.0
        tok["QT_ld"] = tok["QT_ld0"]
        if causal:
            tok["cm_ld"] = tok["cm0"]
            ms0 = pb.add("vector", lambda e: e.memset(V("shiftr", 0, 128, 0, 1024), 0.0))
        if not causal:
            prevq = {}
            for h2 in range(NH):
                qsq = pb.add("vector", lambda e, h2=h2: e.tensor_tensor(
                    V("at0", c0=0, c1=1024), V("QT", c0=h2 * TPC, c1=(h2 + 1) * TPC),
                    V("QT", c0=h2 * TPC, c1=(h2 + 1) * TPC), ALU.mult),
                    deps=[tok["QT_ld"], prevq.get(h2 - 1)])
                mq2 = {}
                for qh in range(2):
                    pr = 0 if qh == 0 else 32
                    mq2[qh] = pb.add("tensor", lambda e, qh=qh, pr=pr: e.matmul(
                        self.ps_c[pr:pr + 1, 0:512], V("onesc"),
                        V("at0", c0=qh * 512, c1=(qh + 1) * 512),
                        start=True, stop=True),
                        deps=[qsq, tok["onesc"], prevq.get(h2 - 1)])
                for qh in range(2):
                    pr = 0 if qh == 0 else 32
                    prevq[h2] = pb.add("vector", lambda e, pr=pr, h2=h2: e.tensor_reduce(
                        V("stat", pr, pr + 1, 40 + h2, 41 + h2),
                        self.ps_c[pr:pr + 1, 0:512], AX.X, ALU.max),
                        deps=[mq2[0], mq2[1]])
            lnq_toks = []
            for pr in (0, 32):
                lnq_toks.append(pb.add("scalar", lambda e, pr=pr: e.activation(
                    V("stat", pr, pr + 1, 40, 56), V("stat", pr, pr + 1, 40, 56), AF.Ln),
                    deps=[prevq[NH - 1]]))
            tok["lnq_all"] = lnq_toks[-1]
            tok["maxq_done"] = prevq[NH - 1]
        cons_kv = {}
        kn_toks = {}

        if not causal:
            def make_prep(hh):
                if hh >= NH:
                    return []
                ksl2 = hh % 2
                st = kn_toks.setdefault(hh, {"bias": {}, "bias_mm": {}})
                steps = []

                def s_kd():
                    kd2 = []
                    for r2 in range(4):
                        kd2.append(pb.add("sync", lambda e, hh=hh, r2=r2, ksl2=ksl2: e.dma_start(
                            out=V("kth", c0=ksl2 * 4096 + r2 * 1024, c1=ksl2 * 4096 + (r2 + 1) * 1024),
                            in_=t["KT_full"][r2 * dkv + hh * 128: r2 * dkv + hh * 128 + 128, :]),
                            deps=[cons_kv.get(hh - 2)], dma=True, phased=False, sem=f"kt{ksl2}"))
                        kd2.append(pb.add("sync", lambda e, hh=hh, r2=r2, ksl2=ksl2: e.dma_start(
                            out=V("vth", c0=ksl2 * 4096 + r2 * 1024, c1=ksl2 * 4096 + (r2 + 1) * 1024)
                                .rearrange("p (j d) -> p j d", j=8),
                            in_=t["V_full"][r2 * TPC:(r2 + 1) * TPC, hh * 128:(hh + 1) * 128]
                                .rearrange("(j p) d -> p j d", p=128)),
                            deps=[cons_kv.get(hh - 2)], dma=True, phased=False, sem=f"vt{ksl2}"))
                    st["kd"] = kd2
                steps.append(s_kd)

                for ppp in range(8):
                    def s_kn(ppp=ppp):
                        prev_c = kn_toks.get(hh - 1, {})
                        ks = pb.add("vector", lambda e, ksl2=ksl2, ppp=ppp: e.tensor_tensor(
                            V("ksq"),
                            V("kth", c0=ksl2 * 4096 + ppp * 512, c1=ksl2 * 4096 + (ppp + 1) * 512),
                            V("kth", c0=ksl2 * 4096 + ppp * 512, c1=ksl2 * 4096 + (ppp + 1) * 512),
                            ALU.mult),
                            deps=st["kd"] + [st.get("red"),
                                             prev_c.get("last_m1") if ppp == 0 else None])
                        m1 = pb.add("tensor", lambda e: e.matmul(
                            self.ps_t[0:1, 0:512], V("onesc"), V("ksq"),
                            start=True, stop=True),
                            deps=[ks, tok["onesc"], st.get("red")] +
                                 ([prev_c["bias"].get(0), prev_c["bias"].get(1)]
                                  if ppp == 0 and prev_c.get("bias") else []))
                        st["red"] = pb.add("vector", lambda e, ppp=ppp: e.tensor_reduce(
                            V("stat", 0, 1, 24 + ppp, 25 + ppp),
                            self.ps_t[0:1, 0:512], AX.X, ALU.max), deps=[m1])
                        st.setdefault("reds", []).append(st["red"])
                        st["last_m1"] = m1
                    steps.append(s_kn)

                def s_lnk():
                    knm = pb.add("vector", lambda e: e.tensor_reduce(
                        V("stat", 0, 1, 32, 33), V("stat", 0, 1, 24, 32), AX.X, ALU.max),
                        deps=st["reds"])
                    lnk = pb.add("scalar", lambda e: e.activation(
                        V("stat", 0, 1, 33, 34), V("stat", 0, 1, 32, 33), AF.Ln), deps=[knm])
                    lnk_b = pb.add("vector", lambda e, hh=hh: e.tensor_copy(
                        V("bcol", 0, 1, 4 + hh % 2, 5 + hh % 2), V("stat", 0, 1, 33, 34)), deps=[lnk])
                    mmk = pb.add("tensor", lambda e, hh=hh: e.matmul(
                        self.ps_t[:, 16:17], V("onesr", 0, 1),
                        V("bcol", 0, 1, 4 + hh % 2, 5 + hh % 2),
                        start=True, stop=True), deps=[lnk_b, tok["onesr"]])
                    st["lnk_bc"] = pb.add("vector", lambda e: e.tensor_copy(
                        V("stat", c0=34, c1=35), self.ps_t[:, 16:17]), deps=[mmk])
                steps.append(s_lnk)

                for qh in range(2):
                    def s_bias(qh=qh):
                        pr = 0 if qh == 0 else 32
                        colc = (hh % 2) * 2 + qh
                        prev2 = kn_toks.get(hh - 2, {})
                        t1 = pb.add("vector", lambda e, pr=pr, hh=hh: e.tensor_scalar(
                            V("stat", pr, pr + 1, 37, 38), V("stat", pr, pr + 1, 40 + hh, 41 + hh),
                            V("stat", pr, pr + 1, 34, 35), None, ALU.add),
                            deps=[tok["lnq_all"], st["lnk_bc"]])
                        t2 = pb.add("scalar", lambda e, pr=pr: e.activation(
                            V("stat", pr, pr + 1, 38, 39), V("stat", pr, pr + 1, 37, 38),
                            AF.Exp, scale=np.float32(0.5)), deps=[t1])
                        t3 = pb.add("vector", lambda e, pr=pr: e.tensor_scalar(
                            V("stat", pr, pr + 1, 39, 40), V("stat", pr, pr + 1, 38, 39),
                            -1.0, MARGIN, ALU.mult, op1=ALU.add), deps=[t2])
                        cast = pb.add("vector", lambda e, pr=pr, colc=colc: e.tensor_copy(
                            V("bcol", pr, pr + 1, colc, colc + 1), V("stat", pr, pr + 1, 39, 40)),
                            deps=[t3, prev2.get("bias_mm", {}).get(qh)])
                        bmm = pb.add("tensor", lambda e, pr=pr, colc=colc: e.matmul(
                            self.ps_t[:, 20 + colc:21 + colc],
                            V("onesr", 0, 1) if pr == 0 else V("onesF", 32, 33),
                            V("bcol", pr, pr + 1, colc, colc + 1),
                            start=True, stop=True),
                            deps=[cast, tok["onesF"], tok["onesr"],
                                  prev2.get("bias", {}).get(qh)])
                        st["bias_mm"][qh] = bmm
                        st["bias"][qh] = pb.add("vector", lambda e, colc=colc: e.tensor_copy(
                            V("sbias", 0, 128, colc, colc + 1), self.ps_t[:, 20 + colc:21 + colc]),
                            deps=[bmm])
                    steps.append(s_bias)
                return steps

            for s in make_prep(0):
                s()

        for h in range(NH):
            head_gate = [tok.get(("rz_done", h - 1)), tok.get(("onT_ev", h - 1, 1))]
            if h == 0:
                head_gate.append(tok.get("maxq_done"))
            if causal:
                g = h // 4
                new_kv = (h % 4 == 0)
                ksl = g % 2
                if new_kv:
                    kd = []
                    for r in range(4):
                        kd.append(pb.add("sync", lambda e, g=g, r=r, ksl=ksl: e.dma_start(
                            out=V("kth", c0=ksl * 4096 + r * 1024, c1=ksl * 4096 + (r + 1) * 1024),
                            in_=t["KT_full"][r * dkv + g * 128: r * dkv + g * 128 + 128, :]),
                            deps=[cons_kv.get(g - 2)], dma=True, phased=False,
                            sem=f"kt{ksl}"))
                        kd.append(pb.add("sync", lambda e, g=g, r=r, ksl=ksl: e.dma_start(
                            out=V("vth", c0=ksl * 4096 + r * 1024, c1=ksl * 4096 + (r + 1) * 1024)
                                .rearrange("p (j d) -> p j d", j=8),
                            in_=t["V_full"][r * TPC:(r + 1) * TPC, g * 128:(g + 1) * 128]
                                .rearrange("(j p) d -> p j d", p=128)),
                            deps=[cons_kv.get(g - 2)], dma=True, phased=False,
                            sem=f"vt{ksl}"))
                    tok[("kv", g)] = kd
                    kn_parts = []
                    prev = None
                    nseg = 4
                    for ppp in range(8):
                        ks = pb.add("vector", lambda e, ksl=ksl, ppp=ppp: e.tensor_tensor(
                            V("ksq"),
                            V("kth", c0=ksl * 4096 + ppp * 512, c1=ksl * 4096 + (ppp + 1) * 512),
                            V("kth", c0=ksl * 4096 + ppp * 512, c1=ksl * 4096 + (ppp + 1) * 512),
                            ALU.mult), deps=kd + head_gate + ([prev] if prev else []))
                        m1 = pb.add("tensor", lambda e: e.matmul(
                            self.ps_c[0:1, 0:512], V("onesc"), V("ksq"),
                            start=True, stop=True),
                            deps=[ks, tok["onesc"]] + head_gate + ([kn_parts[-1]] if kn_parts else []))
                        for sub in range(nseg):
                            cbase = 24 + ppp * nseg + sub
                            kn_parts.append(pb.add("vector", lambda e, sub=sub, cbase=cbase, nseg=nseg: e.tensor_reduce(
                                V("stat", 0, 1, cbase, cbase + 1),
                                self.ps_c[0:1, sub * (512 // nseg):(sub + 1) * (512 // nseg)],
                                AX.X, ALU.max), deps=[m1]))
                        prev = kn_parts[-1]
                    jt = []
                    for jq in range(8):
                        t1 = pb.add("vector", lambda e, jq=jq: e.tensor_tensor(
                            V("lnkr", 0, 1, 992, 1024),
                            V("stat", 0, 1, 24, 56),
                            V("kselb", 0, 1, jq * 32, (jq + 1) * 32), ALU.add),
                            deps=kn_parts + [tok["ksel"]])
                        jt.append(pb.add("vector", lambda e, jq=jq: e.tensor_reduce(
                            V("stat", 0, 1, 56 + jq, 57 + jq),
                            V("lnkr", 0, 1, 992, 1024), AX.X, ALU.max), deps=[t1]))
                    lnj = pb.add("scalar", lambda e: e.activation(
                        V("stat", 0, 1, 56, 64), V("stat", 0, 1, 56, 64), AF.Ln), deps=jt)
                    bcj = []
                    for jq in range(8):
                        bcj.append(pb.add("vector", lambda e, jq=jq: e.tensor_scalar(
                            V("lnkr", 0, 1, jq * 128, (jq + 1) * 128),
                            V("kselb", 0, 1, 0, 128),
                            0.0, V("stat", 0, 1, 56 + jq, 57 + jq),
                            ALU.mult, op1=ALU.add), deps=[lnj, tok["ksel"]]))
                    cpL = pb.add("vector", lambda e: e.tensor_copy(
                        V("hilo", 0, 1, 0, 512), V("lnkr", 0, 1, 512, 1024)), deps=bcj)
                    mm2 = pb.add("tensor", lambda e: e.matmul(
                        self.ps_a[:, 0:512],
                        V("onesr", 0, 1), V("hilo", 0, 1, 0, 512),
                        start=True, stop=True), deps=[cpL, tok["onesr"]] + head_gate)
                    tok[("lnkr2", g)] = pb.add("vector", lambda e: e.tensor_copy(
                        V("lnkr2", 32, 33, 0, 512), self.ps_a[32:33, 0:512]), deps=[mm2])
                    tok[("lnk_bc", g)] = bcj[-1]
                kv_deps = tok[("kv", g)]
                lnk_bc = tok[("lnk_bc", g)]

                negc = {}
                qn_ev = {}
                qsq = pb.add("vector", lambda e, h=h: e.tensor_tensor(
                    V("at0", c0=0, c1=1024), V("QT", c0=h * TPC, c1=(h + 1) * TPC),
                    V("QT", c0=h * TPC, c1=(h + 1) * TPC), ALU.mult),
                    deps=[tok["QT_ld"]] + head_gate + [tok.get(("avz", h - 1))])
                mqs = {}
                for qh in range(2):
                    pr = 0 if qh == 0 else 32
                    mqs[qh] = pb.add("tensor", lambda e, qh=qh, pr=pr: e.matmul(
                        self.ps_c[pr:pr + 1, 0:512], V("onesc"),
                        V("at0", c0=qh * 512, c1=(qh + 1) * 512),
                        start=True, stop=True),
                        deps=[qsq, tok["onesc"], lnk_bc] + head_gate)
                for qh in range(2):
                    pr = 0 if qh == 0 else 32
                    qn_ev[qh] = pb.add("vector", lambda e, pr=pr: e.tensor_copy(
                        V("rows", pr, pr + 1, 0, 512), self.ps_c[pr:pr + 1, 0:512]),
                        deps=[mqs[0], mqs[1]])
                for qh in range(2):
                    pr = 0 if qh == 0 else 32
                    l1 = pb.add("scalar", lambda e, pr=pr: e.activation(
                        V("rows", pr, pr + 1, 0, 512), V("rows", pr, pr + 1, 0, 512), AF.Ln),
                        deps=[qn_ev[qh]])
                    s1 = pb.add("vector", lambda e, qh=qh, pr=pr: e.tensor_tensor(
                        V("rows", pr, pr + 1, 0, 512), V("rows", pr, pr + 1, 0, 512),
                        V("lnkr", 0, 1, qh * 512, (qh + 1) * 512) if pr == 0
                        else V("lnkr2", 32, 33, 0, 512), ALU.add),
                        deps=[l1, lnk_bc, tok.get(("lnkr2", g))])
                    e1 = pb.add("scalar", lambda e, pr=pr: e.activation(
                        V("rows", pr, pr + 1, 0, 512), V("rows", pr, pr + 1, 0, 512),
                        AF.Exp, scale=np.float32(0.5)), deps=[s1])
                    negc[qh] = pb.add("vector", lambda e, pr=pr: e.tensor_scalar(
                        V("shiftr", pr, pr + 1, 0, 512), V("rows", pr, pr + 1, 0, 512),
                        -MARGIN, -1.0, ALU.add, op1=ALU.mult), deps=[e1])

                def c0_of(kt):
                    gk = (kt // 8) + 4 * (kt % 8)
                    return max(0, cdiv(gk - 3, 4)) * 128
                kts = [kt for kt in range(32) if c0_of(kt) < 1024]
                half_kts = {qh: [kt for kt in kts if c0_of(kt) < (qh + 1) * 512] for qh in range(2)}

                exp_toks = {}
                av_last = {}
                pend = {}
                av_all = []

                def emit_zav(idx):
                    if idx < 0 or idx not in pend:
                        return
                    kt, c0, asl = pend.pop(idx)
                    for qh in range(2):
                        cc, ce = qh * 512, (qh + 1) * 512
                        if ce <= c0:
                            continue
                        ccc = max(cc, c0)
                        first = (kt == half_kts[qh][0])
                        lastf = (kt == half_kts[qh][-1])
                        zdeps = [exp_toks[idx]] + (([qn_ev[0], qn_ev[1]] + head_gate) if first else [])
                        if qh == 0:
                            pb.add("tensor", lambda e, ccc=ccc, ce=ce, cc=cc, first=first, lastf=lastf, asl=asl: e.matmul(
                                self.ps_c[0:1, ccc - cc:512], V("onesc"),
                                V("at0", c0=asl * 1024 + ccc, c1=asl * 1024 + ce),
                                start=first, stop=lastf), deps=zdeps)
                        else:
                            pb.add("tensor", lambda e, ccc=ccc, ce=ce, cc=cc, first=first, lastf=lastf, asl=asl: e.matmul(
                                self.ps_c[32:33, ccc - cc:512], V("onesc"),
                                V("at0", c0=asl * 1024 + ccc, c1=asl * 1024 + ce),
                                start=first, stop=lastf), deps=zdeps)
                        lav = pb.add("tensor", lambda e, ccc=ccc, ce=ce, kt=kt, ksl=ksl, first=first, lastf=lastf, asl=asl: e.matmul(
                            self.ps_b[:, ccc:ce],
                            V("vth", c0=ksl * 4096 + kt * 128, c1=ksl * 4096 + (kt + 1) * 128),
                            V("at0", c0=asl * 1024 + ccc, c1=asl * 1024 + ce),
                            start=first, stop=lastf), deps=zdeps)
                        av_last[idx] = lav
                        av_all.append(lav)

                for idx, kt in enumerate(kts):
                    c0 = c0_of(kt)
                    slot = idx % 2
                    pd = [negc[0], negc[1], tok["onesr"], tok["onesF"], exp_toks.get(idx - 2)]
                    mm_last = None
                    for cc in range(c0 - (c0 % 512), 1024, 512):
                        ccc = max(cc, c0)
                        ce = cc + 512
                        qh = cc // 512
                        pr = 0 if qh == 0 else 32
                        pb.add("tensor", lambda e, slot=slot, ccc=ccc, ce=ce, pr=pr, cc=cc: e.matmul(
                            self.ps_a[:, slot * 1024 + ccc: slot * 1024 + ce],
                            V("onesr", 0, 1) if pr == 0 else V("onesF", pr, pr + 1),
                            V("shiftr", pr, pr + 1, ccc - cc, 512),
                            start=True, stop=False), deps=pd)
                        mm_last = pb.add("tensor", lambda e, slot=slot, ccc=ccc, ce=ce, kt=kt, ksl=ksl, h=h: e.matmul(
                            self.ps_a[:, slot * 1024 + ccc: slot * 1024 + ce],
                            V("kth", c0=ksl * 4096 + kt * 128, c1=ksl * 4096 + (kt + 1) * 128),
                            V("QT", c0=h * TPC + ccc, c1=h * TPC + ce),
                            start=False, stop=True), deps=(kv_deps + [tok["QT_ld"]]) if cc == c0 - (c0 % 512) else [])
                    emit_zav(idx - 2)
                    mask_tok = None
                    gk = (kt // 8) + 4 * (kt % 8)
                    jb = gk // 4
                    if jb * 128 >= c0:
                        mask_tok = pb.add("vector", lambda e, slot=slot, jb=jb, kt=kt: e.tensor_tensor(
                            self.ps_a[:, slot * 1024 + jb * 128: slot * 1024 + (jb + 1) * 128],
                            self.ps_a[:, slot * 1024 + jb * 128: slot * 1024 + (jb + 1) * 128],
                            V("cm", c0=kt * 128, c1=(kt + 1) * 128), ALU.add),
                            deps=[mm_last, tok["cm_ld"]])
                    asl = idx % 2
                    ed = [mm_last, mask_tok, av_last.get(idx - 2)]
                    if idx < 2:
                        ed.append(tok.get(("avz", h - 1)))
                    exp_toks[idx] = pb.add("scalar", lambda e, slot=slot, c0=c0, asl=asl: e.activation(
                        V("at0", c0=asl * 1024 + c0, c1=asl * 1024 + 1024),
                        self.ps_a[:, slot * 1024 + c0: slot * 1024 + 1024], AF.Exp), deps=ed)
                    pend[idx] = (kt, c0, asl)
                for tail in (2, 1):
                    emit_zav(len(kts) - tail)
                tok[("avz", h)] = av_all[-1]
                emd = [exp_toks[len(kts) - 1], exp_toks.get(len(kts) - 2)]
            else:
                st = kn_toks[h]
                kv_deps = st["kd"]
                ksl = h % 2
                steps_next = make_prep(h + 1)
                kts = list(range(32))
                exp_toks = {}
                av_last = {}
                sc_tok = {}
                pend = {}
                av_all = []

                def emit_zav(idx):
                    if idx < 0 or idx not in pend:
                        return
                    kt, asl = pend.pop(idx)
                    for qh in range(2):
                        cc, ce = qh * 512, (qh + 1) * 512
                        first = (kt == kts[0])
                        lastf = (kt == kts[-1])
                        zdeps = [exp_toks[(idx, qh)]] + (head_gate if first else [])
                        pb.add("tensor", lambda e, cc=cc, ce=ce, first=first, lastf=lastf, asl=asl, qh=qh: e.matmul(
                            self.ps_c[0:1, 0:512] if qh == 0 else self.ps_c[32:33, 0:512],
                            V("onesc"),
                            V("at0", c0=asl * 1024 + cc, c1=asl * 1024 + ce),
                            start=first, stop=lastf), deps=zdeps)
                        lav = pb.add("tensor", lambda e, cc=cc, ce=ce, kt=kt, ksl=ksl, first=first, lastf=lastf, asl=asl: e.matmul(
                            self.ps_b[:, cc:ce],
                            V("vth", c0=ksl * 4096 + kt * 128, c1=ksl * 4096 + (kt + 1) * 128),
                            V("at0", c0=asl * 1024 + cc, c1=asl * 1024 + ce),
                            start=first, stop=lastf), deps=zdeps)
                        av_last[(idx, qh)] = lav
                        av_all.append(lav)

                for idx, kt in enumerate(kts):
                    slot = idx % 2
                    asl = idx % 3
                    for qh in range(2):
                        cc, ce = qh * 512, (qh + 1) * 512
                        d0 = [exp_toks.get((idx - 2, qh))]
                        if idx == 0:
                            d0 = d0 + head_gate
                        if qh == 0:
                            d0 = d0 + kv_deps + [tok["QT_ld"]]
                        sc_tok[(idx, qh)] = pb.add("tensor", lambda e, slot=slot, cc=cc, ce=ce, kt=kt, ksl=ksl, h=h: e.matmul(
                            self.ps_a[:, slot * 1024 + cc: slot * 1024 + ce],
                            V("kth", c0=ksl * 4096 + kt * 128, c1=ksl * 4096 + (kt + 1) * 128),
                            V("QT", c0=h * TPC + cc, c1=h * TPC + ce),
                            start=True, stop=True), deps=d0)
                    emit_zav(idx - 2)
                    if steps_next:
                        steps_next.pop(0)()
                    for qh in range(2):
                        colc = (h % 2) * 2 + qh
                        ed = [sc_tok[(idx, qh)], av_last.get((idx - 3, qh)), st["bias"][qh]]
                        if idx < 3:
                            ed.append(tok.get(("avz", h - 1)))
                        exp_toks[(idx, qh)] = pb.add("scalar", lambda e, slot=slot, asl=asl, qh=qh, colc=colc: e.activation(
                            V("at0", c0=asl * 1024 + qh * 512, c1=asl * 1024 + (qh + 1) * 512),
                            self.ps_a[:, slot * 1024 + qh * 512: slot * 1024 + (qh + 1) * 512],
                            AF.Exp, bias=V("sbias", 0, 128, colc, colc + 1)), deps=ed)
                    pend[idx] = (kt, asl)
                for tail in (2, 1):
                    emit_zav(len(kts) - tail)
                while steps_next:
                    steps_next.pop(0)()
                tok[("avz", h)] = av_all[-1]
                nk = len(kts)
                emd = [exp_toks[(nk - 1, 0)], exp_toks[(nk - 1, 1)],
                       exp_toks[(nk - 2, 0)], exp_toks[(nk - 2, 1)]]

            rzx = {}
            for qh in range(2):
                pr = 0 if qh == 0 else 32
                src = (lambda: self.ps_c[0:1, 0:512]) if qh == 0 else (lambda: self.ps_c[32:33, 0:512])
                zf = pb.add("vector", lambda e, pr=pr, src=src: e.tensor_scalar(
                    V("rows", pr, pr + 1, 0, 512), src(), 1e-30, None, ALU.add),
                    deps=av_all)
                rr = pb.add("vector", lambda e, pr=pr: e.reciprocal(
                    V("rows", pr, pr + 1, 0, 512), V("rows", pr, pr + 1, 0, 512)), deps=[zf])
                rzx[qh] = pb.add("vector", lambda e, pr=pr: e.tensor_scalar(
                    V("rows", pr, pr + 1, 0, 512), V("rows", pr, pr + 1, 0, 512),
                    post, None, ALU.mult), deps=[rr])
            rzbc0 = self.bcast_hilo("rows", 0, 0, 512, "rzbc", 0, self.ps_a, [rzx[0]],
                                    extra_mm_deps=emd)
            rzbc1 = self.bcast_hilo("rows", 32, 0, 512, "rzbc", 512, self.ps_a[:, 512:1024],
                                    [rzx[1]], extra_mm_deps=emd)
            tok[("rz_done", h)] = rzbc1
            for qh in range(2):
                bc = rzbc0 if qh == 0 else rzbc1
                tok[("onT_ev", h, qh)] = pb.add("vector", lambda e, h=h, qh=qh: e.tensor_tensor(
                    V("onT", c0=h * TPC + qh * 512, c1=h * TPC + (qh + 1) * 512),
                    self.ps_b[:, qh * 512:(qh + 1) * 512],
                    V("rzbc", c0=qh * 512, c1=(qh + 1) * 512), ALU.mult),
                    deps=[bc] + av_all)
            cons_kv[g if causal else h] = tok[("onT_ev", h, 1)]
        pb.barrier()

    def dump(self, name, src_name, cols, bf=False):
        if not self.dbg:
            return
        self.dout(name, (128, cols), BF16 if bf else F32)
        tk = self.pb.add("sync", lambda e: e.dma_start(
            out=self.t[name], in_=self.V(src_name, c0=0, c1=cols)),
            deps=list(self.pb.phase_deps), dma=True, phased=False, sem="dbg")
        self.pb.add("sync", lambda e: None, deps=[tk], phased=False)

    def store_x(self):
        stx = self.pb.add("sync", lambda e: e.dma_start(
            out=self.t["x_out"].rearrange("(t p) d -> p t d", p=128),
            in_=self.V("x").rearrange("p (t d) -> p t d", t=NT)),
            deps=list(self.pb.phase_deps), dma=True, phased=False, sem="sx")
        self.pb.add("sync", lambda e: None, deps=[stx], phased=False)


    def build(self):
        leg = self.leg
        self.din("x_in", (TPC, D))
        self.din("ones_row", (1, 128), BF16)
        self.din("half_row", (1, 128), BF16)
        self.din("ones_col", (128, 1), BF16)
        self.din("ones_full", (128, 128), BF16)
        self.din("ident", (128, 128), BF16)
        if leg in ("A", "B"):
            self.din("guA", (NIC, 128, 2 * D))
            self.din("dA", (4, 22, 128, 1024))
            self.din("gs", (128, NIC))
            self.din("us", (128, NIC))
            self.din("ds_row", (1, D))
        if leg == "B":
            self.t["guB"] = self.nc.dram_tensor("guB", [NIC, 128, 2 * D], BF16, kind="Internal").ap()
            self.t["dB"] = self.nc.dram_tensor("dB", [4, 22, 128, 1024], BF16, kind="Internal").ap()
        if leg in ("B", "C"):
            self.t["woB"] = self.nc.dram_tensor("woB", [4, 128, NKC * 512], BF16, kind="Internal").ap()
        if leg == "B":
            self.t["wq1B"] = self.nc.dram_tensor("wq1B", [NKC, 128, D], BF16, kind="Internal").ap()
            self.t["wk1B"] = self.nc.dram_tensor("wk1B", [4, 128, D], BF16, kind="Internal").ap()
            self.t["wv1B"] = self.nc.dram_tensor("wv1B", [1, 128, NKC * 512], BF16, kind="Internal").ap()
        if leg == "A":
            self.din("wq", (NKC, 128, D))
            self.din("wk", (NKC, 128, D))
            self.din("wv", (4, 128, NKC * 512))
            self.din("bq", (128, NKC))
            self.din("bk", (128, NKC))
            self.din("bv_row", (1, D))
            self.dout("x_out", (TPC, D))
            self.dout("KT_out", (D, TPC), BF16)
            self.dout("V_out", (TPC, D), BF16)
            self.dout("QT_out", (D, TPC), BF16)
        if leg in ("B", "C"):
            dkv = D if leg == "B" else DKV1
            self.din("QT_in", (D, TPC), BF16)
            self.din("KT_full", (4 * dkv, TPC), BF16)
            self.din("V_full", (4 * TPC, dkv), BF16)
            self.din("wo", (4, 128, NKC * 512))
            self.din("bo_row", (1, D))
            self.din("w1", (4, 128, NKC * 512))
            self.din("b1_row", (1, D))
            self.dout("x_out", (TPC, D))
        if leg == "B":
            self.din("wq1", (NKC, 128, D))
            self.din("wk1", (4, 128, D))
            self.din("wv1", (1, 128, NKC * 512))
            self.din("bq1", (128, NKC))
            self.din("bk1", (128, 4))
            self.din("bv1_row", (1, DKV1))
            self.din("cos_t", (128, TPC))
            self.din("sin_t", (128, TPC))
            self.dout("QT_out", (D, TPC), BF16)
            self.dout("KT_out", (DKV1, TPC), BF16)
            self.dout("V_out", (TPC, DKV1), BF16)
        if leg == "C":
            self.din("cmask", (32, 128, 128), BF16)
            self.din("ksel", (1, 256))

        r = self.res
        r("f", "x", NT * D)
        r("f", "stat", 64)
        r("f", "bstat", 32)
        r("f", "scrF", 2048)
        r("f", "scr0", 512, at="scrF")
        r("f", "scr1", 512, at="scrF", at_off=512)
        r("f", "sq", 2048, at="scrF")
        r("f", "qf", 1024, at="scrF", at_off=1024)
        r("f", "hiloF", 1024, at="scrF")
        r("f", "rows", 512)
        r("b", "onesr", 128)
        r("b", "halfr", 128)
        r("b", "onesc", 1)
        r("b", "onesF", 128)
        r("b", "ident", 128)
        r("b", "browB", 2048)
        r("b", "hilo", 1024)
        r("b", "shiftr", 1024, at="hilo")
        r("b", "sbf", 512)
        if leg in ("A", "B"):
            r("f", "ds_bc", 2048)
            r("f", "ds_bc2", 2048, at="ds_bc")
            r("f", "gsc", NIC)
            r("f", "usc", NIC)
        if leg == "A":
            r("b", "xnT", NKC * TPC)
            r("b", "wgu", 2 * 2 * D)
            r("b", "hdn", NIC * 512)
            r("b", "dwn", 2 * 2048)
            r("b", "stage", 2 * D, at="hdn", at_off=8192)
            r("b", "xn", 2048, at="stage")
            r("b", "wproj", 2 * D, at="wgu")
            r("b", "wvsl", NKC * 512, at="hdn")
            self.tok_gates = {"dwn_gate": None}
        if leg == "B":
            r("f", "cs_c", 1024, at="ds_bc")
            r("f", "cs_s", 1024, at="ds_bc", at_off=1024)
            r("f", "rzbc", 1536)
            r("f", "qswap", 1024, at="rzbc")
            r("f", "w1f", 1024, at="rzbc")
            r("f", "abc", 512, at="rzbc", at_off=1024)
            r("b", "QT", NKC * TPC)
            r("b", "xnT", NKC * TPC, at="QT")
            r("b", "onT", NKC * TPC)
            r("b", "at0", 3 * 1024)
            r("b", "ksq", 512)
            r("b", "kth", 2 * 4096)
            r("b", "vth", 2 * 4096)
            r("b", "stage", 2 * D, at="onT")
            r("b", "xn", 2048, at="stage")
            self.m["hdn"] = ("b", self.m["onT"][1], NIC * 512)
            assert self.m["at0"][1] == self.m["onT"][1] + 16384
            assert self.m["ksq"][1] == self.m["onT"][1] + 19456
            assert self.m["kth"][1] == self.m["onT"][1] + 19968
            self.m["wgu"] = ("b", self.m["kth"][1] + 4096, 2 * 2 * D)
            assert self.m["vth"][1] == self.m["kth"][1] + 8192
            self.m["dwn"] = ("b", self.m["vth"][1] + 4096, 2 * 2048)
            r("b", "w1sl", NKC * 512, at="kth")
            r("b", "wosl", NKC * 512, at="kth")
            r("b", "wproj", 2 * D, at="kth")
            r("b", "wvsl", NKC * 512, at="kth")
            r("b", "bcol", 6)
            r("f", "sbias", 4)
        if leg == "C":
            r("f", "kselb", 256)
            r("f", "lnkr", 1024, at="scrF", at_off=1024)
            r("f", "lnkr2", 512, at="scrF", at_off=1024)
            r("f", "rzbc", 1536)
            r("f", "w1f", 1024, at="rzbc")
            r("f", "abc", 512, at="rzbc", at_off=1024)
            r("b", "QT", NKC * TPC)
            r("b", "xnT", NKC * TPC, at="QT")
            r("b", "onT", NKC * TPC)
            r("b", "at0", 2 * 1024)
            r("b", "ksq", 512)
            r("b", "kth", 2 * 4096)
            r("b", "vth", 2 * 4096)
            r("b", "cm", 32 * 128)
            r("b", "stage", 2 * D, at="onT")
            r("b", "xn", 2048, at="stage")
            r("b", "w1sl", NKC * 512, at="kth")
            r("b", "wosl", NKC * 512, at="kth")
            r("b", "wosl2", NKC * 512, at="vth")

        with ExitStack() as st:
            nfc = max(1, self.off["f"])
            nbc = max(1, self.off["b"])
            assert nfc * 4 + nbc * 2 <= 219000, (leg, nfc * 4, nbc * 2)
            self.f32a = st.enter_context(self.nc.sbuf_tensor("f32a", [128, nfc], F32))
            self.bfa = st.enter_context(self.nc.sbuf_tensor("bfa", [128, nbc], BF16))
            self.ps_a = st.enter_context(self.nc.psum_tensor("ps_a", [128, 2048], F32))
            self.ps_b = st.enter_context(self.nc.psum_tensor("ps_b", [128, 1024], F32))
            self.ps_c = st.enter_context(self.nc.psum_tensor("ps_c", [128, 512], F32))
            self.ps_t = st.enter_context(self.nc.psum_tensor("ps_t", [128, 512], F32))
            self.ps_tb = self.ps_t.ap().bitcast(BF16)
            sems = {e: st.enter_context(self.nc.semaphore(f"s_{e}")) for e in ENGINES}
            block = st.enter_context(self.nc.Block())
            self._sem_stack = st

            self.consts()
            if self.leg == "A":
                self.emit_ln("ln1")
                self.dump("d_xnT", "xnT", NKC * TPC, bf=True)
                self.dump("d_stat", "stat", 64)
                self.emit_ffn0()
                self.dump("d_ds", "ds_bc2", 2048)
                self.dump("d_dwn", "dwn", 2 * 2048, bf=True)
                self.dump("d_hdn", "hdn", NIC * 512, bf=True)
                self.dump("d_x", "x", NT * D)
                self.emit_ln("ln2")
                self.tok["wproj_gate"] = self.tok["gu_last"]
                self.emit_proj_fm("wk", "bk", NKC, "KT_out", 0)
                self.tok["wproj_gate"] = self.tok["fm_last"]
                self.emit_proj_fm("wq", "bq", NKC, "QT_out", 16, scale=1.0 / math.sqrt(HD))
                self.tok["wvsl_gate"] = self.tok["dw_last"]
                bv = self.load_row_bf("bv_row", 0, D, deps=[])
                self.emit_proj_tm("wv", "wvsl", bv, "V_out", D)
                self.store_x()
            elif self.leg == "B":
                if PRECAST:
                    pc_last = {}
                    for n4 in range(4):
                        pc_last[n4] = self.pb.add("gpsimd", lambda e, n4=n4: e.dma_start(
                            out=self.t["woB"][n4], in_=self.t["wo"][n4]),
                            dma=True, phased=False, sem=f"pc{n4}")
                    self.tok["precast_wo"] = [pc_last[k] for k in range(4)]
                    for ic in range(NIC):
                        pc_last[ic % 4] = self.pb.add("gpsimd", lambda e, ic=ic: e.dma_start(
                            out=self.t["guB"][ic], in_=self.t["guA"][ic]),
                            dma=True, phased=False, sem=f"pc{ic % 4}")
                    for n4 in range(4):
                        for g11 in range(22):
                            k4 = (n4 * 22 + g11) % 4
                            pc_last[k4] = self.pb.add("gpsimd", lambda e, n4=n4, g11=g11, k4=k4: e.dma_start(
                                out=self.t["dB"][n4, g11], in_=self.t["dA"][n4, g11]),
                                dma=True, phased=False, sem=f"pc{k4}")
                    self.tok["precast"] = [pc_last[k] for k in range(4)]
                    for n4 in range(4):
                        pc_last[n4] = self.pb.add("gpsimd", lambda e, n4=n4: e.dma_start(
                            out=self.t["wk1B"][n4], in_=self.t["wk1"][n4]),
                            dma=True, phased=False, sem=f"pc{n4}")
                    for m16 in range(NKC):
                        k4 = m16 % 4
                        pc_last[k4] = self.pb.add("gpsimd", lambda e, m16=m16, k4=k4: e.dma_start(
                            out=self.t["wq1B"][m16], in_=self.t["wq1"][m16]),
                            dma=True, phased=False, sem=f"pc{k4}")
                    self.tok["precast_wp"] = [pc_last[k] for k in range(4)]
                    self.tok["precast_wv"] = [self.pb.add("gpsimd", lambda e: e.dma_start(
                        out=self.t["wv1B"][0], in_=self.t["wv1"][0]),
                        dma=True, phased=False, sem="pc0")]
                self.emit_attention(0)
                self.tok["wgu_gate"] = self.tok[("onT_ev", NH - 1, 1)]
                self.tok["dwn_gate"] = self.tok[("onT_ev", NH - 1, 1)]
                self.tok["wosl_gate"] = self.tok[("onT_ev", NH - 1, 1)]
                self.tok["w1f_gate"] = self.tok[("onT_ev", NH - 1, 1)]
                self.dump("d_xat", "x", NT * D)
                self.emit_out_proj("wo", "bo_row")
                self.dump("d_xop", "x", NT * D)
                self.emit_ln("ln3")
                self.emit_ffn0()
                self.dump("d_wgu", "wgu", 2 * 2 * D, bf=True)
                self.dump("d_dwn", "dwn", 2 * 2048, bf=True)
                self.dump("d_xf0", "x", NT * D)
                self.emit_ln("ln4")
                self.emit_ffn1()
                self.emit_ln("ln5")
                csc = self.pb.add("sync", lambda e: e.dma_start(
                    out=self.V("cs_c"), in_=self.t["cos_t"]), dma=True, phased=False,
                    deps=list(self.pb.phase_deps), sem="cs")
                css = self.pb.add("sync", lambda e: e.dma_start(
                    out=self.V("cs_s"), in_=self.t["sin_t"]), dma=True, phased=False,
                    deps=[csc], sem="cs")
                self.tok["cs"] = css
                self.tok["wproj_gate"] = self.tok["b1_last"]
                self.emit_proj_fm("wk1", "bk1", 4, "KT_out", 0, rope=True)
                self.tok["wproj_gate"] = self.tok["fm_last"]
                self.emit_proj_fm("wq1", "bq1", NKC, "QT_out", 16, rope=True)
                self.tok["wvsl_gate"] = self.tok["fm_last"]
                bv = self.load_row_bf("bv1_row", 0, DKV1, deps=[self.tok.get("b1_last")])
                self.emit_proj_tm("wv1", "wvsl", bv, "V_out", DKV1)
                self.store_x()
            else:
                if PRECAST and PRECAST_CONSUME:
                    pcw = []
                    for n4 in range(4):
                        pcw.append(self.pb.add("gpsimd", lambda e, n4=n4: e.dma_start(
                            out=self.t["woB"][n4], in_=self.t["wo"][n4]),
                            dma=True, phased=False, sem=f"pc{n4}"))
                    self.tok["precast_wo"] = pcw
                self.emit_attention(1)
                self.tok["wosl_gate"] = self.tok[("onT_ev", NH - 1, 1)]
                self.tok["w1f_gate"] = self.tok[("onT_ev", NH - 1, 1)]
                self.emit_out_proj("wo", "bo_row")
                self.emit_ln("ln6")
                self.emit_ffn1()
                self.store_x()

            lsems = {lbl: st.enter_context(self.nc.semaphore(f"d_{lbl}"))
                     for lbl in self.pb.label_count}
            self.pb.render(self.nc, block, sems, lsems)
        return self.nc



_cache = {}
_nc_cache = {}


def rope_perm(nh):
    p = []
    for h in range(nh):
        base = h * HD
        p += [base + i for i in range(0, HD, 2)]
        p += [base + i for i in range(1, HD, 2)]
    return np.array(p, dtype=np.int64)


def _host_prep(inputs):
    if "done" in _cache:
        return _cache
    c = _cache
    f32 = lambda a: np.ascontiguousarray(np.asarray(a, dtype=np.float32))

    gate_w = f32(inputs["L0_ffn_gate_w"])
    up_w = f32(inputs["L0_ffn_up_w"])
    down_w = f32(inputs["L0_ffn_down_w"])

    def lhsT_ic(w):
        a = w.reshape(NIC, 128, NKC, 128)
        return np.ascontiguousarray(a.transpose(0, 3, 2, 1)).reshape(NIC, 128, D)
    c["guA"] = np.ascontiguousarray(np.concatenate([lhsT_ic(gate_w), lhsT_ic(up_w)], axis=2))
    dT = np.ascontiguousarray(down_w.T)
    a = dT.reshape(22, 2, 128, 4, 512)
    c["dA"] = np.ascontiguousarray(a.transpose(3, 0, 2, 1, 4)).reshape(4, 22, 128, 1024)
    c["gs"] = np.ascontiguousarray(f32(inputs["L0_ffn_gate_s"]).reshape(NIC, 128).T)
    c["us"] = np.ascontiguousarray(f32(inputs["L0_ffn_up_s"]).reshape(NIC, 128).T)
    c["ds_row"] = f32(inputs["L0_ffn_down_s"]).reshape(1, D)

    def lhsT_m(w):
        K, M = w.shape
        a = w.reshape(K // 128, 128, M // 128, 128)
        return np.ascontiguousarray(a.transpose(2, 1, 0, 3)).reshape(M // 128, 128, K)

    def rhs_nsl(w, nsl):
        K, N = w.shape
        ns = N // nsl
        a = w.reshape(K // 128, 128, nsl, ns)
        return np.ascontiguousarray(a.transpose(2, 1, 0, 3)).reshape(nsl, 128, (K // 128) * ns)

    c["wq"] = lhsT_m(f32(inputs["L0_attn_Wq"]))
    c["wk"] = lhsT_m(f32(inputs["L0_attn_Wk"]))
    c["wv"] = rhs_nsl(f32(inputs["L0_attn_Wv"]), 4)
    c["bq"] = np.ascontiguousarray(f32(inputs["L0_attn_bq"]).reshape(NKC, 128).T)
    c["bk"] = np.ascontiguousarray(f32(inputs["L0_attn_bk"]).reshape(NKC, 128).T)
    c["bv_row"] = f32(inputs["L0_attn_bv"]).reshape(1, D)
    c["wo"] = rhs_nsl(f32(inputs["L0_attn_Wo"]), 4)
    c["bo_row"] = f32(inputs["L0_attn_bo"]).reshape(1, D)

    c["w1"] = rhs_nsl(np.ascontiguousarray(f32(inputs["L1_ffn_weight"]).T), 4)
    c["b1_row"] = f32(inputs["L1_ffn_bias"]).reshape(1, D)

    pq = rope_perm(NH)
    pk = rope_perm(4)
    c["wq1"] = lhsT_m(f32(inputs["L1_attn_Wq"])[:, pq])
    c["bq1"] = np.ascontiguousarray(f32(inputs["L1_attn_bq"])[pq].reshape(NKC, 128).T)
    c["wk1"] = lhsT_m(f32(inputs["L1_attn_Wk"])[:, pk])
    c["bk1"] = np.ascontiguousarray(f32(inputs["L1_attn_bk"])[pk].reshape(4, 128).T)
    c["wv1"] = rhs_nsl(f32(inputs["L1_attn_Wv"]), 1)
    c["bv1_row"] = f32(inputs["L1_attn_bv"]).reshape(1, DKV1)
    c["wo1"] = rhs_nsl(f32(inputs["L1_attn_Wo"]), 4)
    c["bo1_row"] = f32(inputs["L1_attn_bo"]).reshape(1, D)

    c["ones_row"] = np.ones((1, 128), dtype=BF)
    c["half_row"] = np.full((1, 128), 0.5, dtype=BF)
    c["ones_col"] = np.ones((128, 1), dtype=BF)
    c["ones_full"] = np.ones((128, 128), dtype=BF)
    c["ident"] = np.eye(128, dtype=np.float32).astype(BF)

    freqs = 10000.0 ** (-(np.arange(0, 64, dtype=np.float64) * 2.0 / D))
    cos_c, sin_c, cmask_c, ksel_c = [], [], [], []
    for p in range(4):
        pos = np.concatenate([np.arange(128 * (p + 4 * j), 128 * (p + 4 * j) + 128) for j in range(8)])
        ang = np.outer(freqs, pos.astype(np.float64))
        cc = np.cos(ang).astype(np.float32)
        ss = np.sin(ang).astype(np.float32)
        cos_c.append(np.ascontiguousarray(np.concatenate([cc, cc], axis=0)))
        sin_c.append(np.ascontiguousarray(np.concatenate([ss, ss], axis=0)))
        ks = np.full((8, 32), -1e9, dtype=np.float32)
        for jq in range(8):
            for kt in range(32):
                gk = (kt // 8) + 4 * (kt % 8)
                if gk <= p + 4 * jq:
                    ks[jq, kt] = 0.0
        ksel_c.append(np.ascontiguousarray(ks.reshape(1, 256)))
        msk = np.zeros((32, 128, 128), dtype=np.float32)
        for kt in range(32):
            rr, jk = kt // 8, kt % 8
            gk = rr + 4 * jk
            jb = gk // 4
            gq = p + 4 * jb
            if gk > gq:
                msk[kt, :, :] = -1e30
            elif gk == gq:
                kk = np.arange(128)[:, None]
                qq = np.arange(128)[None, :]
                msk[kt][kk > qq] = -1e30
        cmask_c.append(np.ascontiguousarray(msk.astype(BF)))
    c["cos_t"] = cos_c
    c["sin_t"] = sin_c
    c["cmask"] = cmask_c
    c["ksel"] = ksel_c
    c["done"] = True
    return c


def _shard_x(x):
    out = []
    for cid in range(8):
        b, p = cid // 4, cid % 4
        rows = np.concatenate([x[b, 128 * (p + 4 * j):128 * (p + 4 * j) + 128] for j in range(8)])
        out.append(np.ascontiguousarray(rows))
    return out


def _gather_kv(KTs, Vs):
    ktf, vf = [], []
    for grp in range(2):
        kt = np.ascontiguousarray(np.concatenate([KTs[grp * 4 + r] for r in range(4)], axis=0))
        v = np.ascontiguousarray(np.concatenate([Vs[grp * 4 + r] for r in range(4)], axis=0))
        for r in range(4):
            ktf.append(kt)
            vf.append(v)
    return ktf, vf


def _get_nc(leg):
    if leg not in _nc_cache:
        _nc_cache[leg] = Leg(leg).build()
    return _nc_cache[leg]


def run_leg_a(c, xs):
    ncA = _get_nc("A")
    in_maps = []
    for cid in range(8):
        m = {k: c[k] for k in ("ones_row", "half_row", "ones_col", "ones_full", "ident",
                               "guA", "dA", "gs", "us", "ds_row",
                               "wq", "wk", "wv", "bq", "bk", "bv_row")}
        m["x_in"] = xs[cid]
        in_maps.append(m)
    return run_bass_kernel_spmd(ncA, in_maps, core_ids=list(range(8)))


def run_leg_b(c, x1, QT0, ktf, vf):
    ncB = _get_nc("B")
    in_maps = []
    for cid in range(8):
        m = {k: c[k] for k in ("ones_row", "half_row", "ones_col", "ones_full", "ident",
                               "guA", "dA", "gs", "us", "ds_row",
                               "wo", "bo_row", "w1", "b1_row",
                               "wq1", "bq1", "wk1", "bk1", "wv1", "bv1_row")}
        m["x_in"] = x1[cid]
        m["QT_in"] = QT0[cid]
        m["KT_full"] = ktf[cid]
        m["V_full"] = vf[cid]
        m["cos_t"] = c["cos_t"][cid % 4]
        m["sin_t"] = c["sin_t"][cid % 4]
        in_maps.append(m)
    return run_bass_kernel_spmd(ncB, in_maps, core_ids=list(range(8)))


def run_leg_c(c, x2, QT1, ktf1, vf1):
    ncC = _get_nc("C")
    in_maps = []
    for cid in range(8):
        m = {k: c[k] for k in ("ones_row", "half_row", "ones_col", "ones_full", "ident", "b1_row", "w1")}
        m["wo"] = c["wo1"]
        m["bo_row"] = c["bo1_row"]
        m["x_in"] = x2[cid]
        m["QT_in"] = QT1[cid]
        m["KT_full"] = ktf1[cid]
        m["V_full"] = vf1[cid]
        m["cmask"] = c["cmask"][cid % 4]
        m["ksel"] = c["ksel"][cid % 4]
        in_maps.append(m)
    return run_bass_kernel_spmd(ncC, in_maps, core_ids=list(range(8)))


def kernel(**inputs):
    c = _host_prep(inputs)
    x = np.asarray(inputs["L0_attn_X"], dtype=np.float32)
    xs = _shard_x(x)

    resA = run_leg_a(c, xs)
    x1 = [np.asarray(resA.results[i]["x_out"]) for i in range(8)]
    KT0 = [np.asarray(resA.results[i]["KT_out"]) for i in range(8)]
    V0 = [np.asarray(resA.results[i]["V_out"]) for i in range(8)]
    QT0 = [np.asarray(resA.results[i]["QT_out"]) for i in range(8)]
    ktf, vf = _gather_kv(KT0, V0)

    resB = run_leg_b(c, x1, QT0, ktf, vf)
    x2 = [np.asarray(resB.results[i]["x_out"]) for i in range(8)]
    QT1 = [np.asarray(resB.results[i]["QT_out"]) for i in range(8)]
    KT1 = [np.asarray(resB.results[i]["KT_out"]) for i in range(8)]
    V1 = [np.asarray(resB.results[i]["V_out"]) for i in range(8)]
    ktf1, vf1 = _gather_kv(KT1, V1)

    resC = run_leg_c(c, x2, QT1, ktf1, vf1)
    y = [np.asarray(resC.results[i]["x_out"]) for i in range(8)]

    out = np.empty((B, S, D), dtype=np.float32)
    for cid in range(8):
        b, p = cid // 4, cid % 4
        for j in range(8):
            out[b, 128 * (p + 4 * j):128 * (p + 4 * j) + 128] = y[cid][j * 128:(j + 1) * 128]
    return out


# version 12 (final)
# speedup vs baseline: 1.0498x; 1.0172x over previous
import math
import numpy as np
import ml_dtypes
import concourse.bass as bass
import concourse.mybir as mybir
from contextlib import ExitStack
from concourse.bass_utils import run_bass_kernel_spmd

F32 = mybir.dt.float32
BF16 = mybir.dt.bfloat16
AF = mybir.ActivationFunctionType
ALU = mybir.AluOpType
AX = mybir.AxisListType
BF = ml_dtypes.bfloat16

D = 2048
INTER = 5632
B = 2
S = 4096
TPC = 1024
NT = 8
NH = 16
DKV1 = 512
HD = 128
NKC = 16
NIC = 44
MARGIN = 44.0

ENGINES = ("tensor", "vector", "scalar", "sync", "gpsimd")
PRECAST = True
PRECAST_CONSUME = True


def cdiv(a, b):
    return (a + b - 1) // b


class PB:

    def __init__(self):
        self.items = {e: [] for e in ENGINES}
        self.label_count = {}
        self.phase_deps = []

    def add(self, eng, emit, deps=(), dma=False, phased=True, sem=None):
        ds = [d for d in deps if d is not None]
        if phased:
            ds = ds + self.phase_deps
        if dma:
            assert sem is not None
            k = self.label_count.get(sem, 0)
            self.label_count[sem] = k + 1
            tokn = ("D", sem, k)
            if k > 0:
                ds.append(("D", sem, k - 1))
        else:
            tokn = ("E", eng, len(self.items[eng]))
        self.items[eng].append({"emit": emit, "deps": ds, "dma": dma,
                                "marked": dma, "sem": sem, "tok": tokn})
        return tokn

    def last(self, eng):
        for it in reversed(self.items[eng]):
            if not it["dma"]:
                return it["tok"]
        return None

    def barrier(self, engs=("tensor", "vector", "scalar")):
        self.phase_deps = [t for t in (self.last(e) for e in engs) if t is not None]

    def render(self, nc, block, sems, lsems):
        eidx = {}
        for e in ENGINES:
            for i, it in enumerate(self.items[e]):
                eidx[("E", e, i)] = (e, i)
        for e in ENGINES:
            for it in self.items[e]:
                for d in it["deps"]:
                    if d[0] == "E":
                        e2, i2 = eidx[d]
                        self.items[e2][i2]["marked"] = True
        for e in ("vector", "scalar"):
            for it in self.items[e]:
                if not it["dma"]:
                    it["marked"] = True
        semval = {}
        for e in ENGINES:
            v, vals = 0, []
            for it in self.items[e]:
                if it["marked"] and not it["dma"]:
                    v += 1
                vals.append(v)
            semval[e] = vals
        tot = {lbl: 16 * n for lbl, n in self.label_count.items()}

        def runner(ename):
            auto_serial = ename in ("vector", "scalar")
            def body(eng):
                hw = {}
                nprev = 0
                for it in self.items[ename]:
                    if auto_serial and not it["dma"] and nprev > 0:
                        if nprev > hw.get(ename, 0):
                            eng.wait_ge(sems[ename], nprev)
                            hw[ename] = nprev
                    for d in it["deps"]:
                        if d[0] == "E":
                            e2, i2 = eidx[d]
                            if e2 == ename:
                                continue
                            key, v = e2, semval[e2][i2]
                            sm = sems[e2]
                        else:
                            _, lbl, k = d
                            key = "D" + lbl
                            v = 16 * (k + 1)
                            sm = lsems[lbl]
                        if v > hw.get(key, 0):
                            eng.wait_ge(sm, v)
                            hw[key] = v
                    inst = it["emit"](eng)
                    if it["dma"]:
                        inst.then_inc(lsems[it["sem"]], 16)
                    elif it["marked"]:
                        assert inst is not None
                        inst.then_inc(sems[ename], 1)
                        nprev = semval[ename][self.items[ename].index(it)] if False else nprev + 1
            return body

        block.tensor(runner("tensor"))
        block.vector(runner("vector"))
        block.scalar(runner("scalar"))
        block.sync(runner("sync"))
        block.gpsimd(runner("gpsimd"))


class Leg:
    def __init__(self, leg, dbg=False):
        self.dbg = dbg
        self.leg = leg
        self.nc = bass.Bass("TRN2", target_bir_lowering=False,
                            dynamic_dma_scratch_size=24576)
        self.pb = PB()
        self.t = {}
        self.tok = {}
        self.m = {}
        self.off = {"f": 0, "b": 0}

    def res(self, space, name, cols, at=None, at_off=0):
        if at is not None:
            o = self.m[at][1] + at_off
        else:
            o = self.off[space]
            self.off[space] += cols
        self.m[name] = (space, o, cols)

    def V(self, name, p0=0, p1=128, c0=0, c1=None):
        sp, off, cols = self.m[name]
        if c1 is None:
            c1 = cols
        a = self.f32a if sp == "f" else self.bfa
        return a[p0:p1, off + c0:off + c1]

    def din(self, name, shape, dt=F32):
        self.t[name] = self.nc.dram_tensor(name, list(shape), dt, kind="ExternalInput").ap()

    def dout(self, name, shape, dt=F32):
        self.t[name] = self.nc.dram_tensor(name, list(shape), dt, kind="ExternalOutput").ap()


    def one_shot(self, name, emit, eng="sync"):
        lbl = "ld" if eng == "sync" else "ldg"
        self.tok[name] = self.pb.add(eng, emit, dma=True, phased=False, sem=lbl)

    def consts(self):
        V, t = self.V, self.t
        self.one_shot("onesr", lambda e: e.dma_start(out=V("onesr", 0, 1), in_=t["ones_row"]))
        self.one_shot("halfr", lambda e: e.dma_start(out=V("halfr", 0, 1), in_=t["half_row"]))
        self.one_shot("onesc", lambda e: e.dma_start(out=V("onesc"), in_=t["ones_col"]))
        self.one_shot("onesF", lambda e: e.dma_start(out=V("onesF"), in_=t["ones_full"]))
        self.one_shot("ident", lambda e: e.dma_start(out=V("ident"), in_=t["ident"]))
        self.one_shot("x", lambda e: e.dma_start(
            out=V("x").rearrange("p (t d) -> p t d", t=NT),
            in_=t["x_in"].rearrange("(t p) d -> p t d", p=128)))
        if self.leg in ("A", "B"):
            self.one_shot("gs", lambda e: e.dma_start(out=V("gsc"), in_=t["gs"]))
            self.one_shot("us", lambda e: e.dma_start(out=V("usc"), in_=t["us"]))
            self.one_shot("ds", lambda e: e.dma_start(out=V("ds_bc", 0, 1, 0, D), in_=t["ds_row"]))
        if self.leg == "A":
            self.one_shot("b_wk", lambda e: e.dma_start(out=V("bstat", c0=0, c1=NKC), in_=t["bk"]))
            self.one_shot("b_wq", lambda e: e.dma_start(out=V("bstat", c0=16, c1=16 + NKC), in_=t["bq"]))
        if self.leg == "B":
            self.one_shot("b_wk1", lambda e: e.dma_start(out=V("bstat", c0=0, c1=4), in_=t["bk1"]))
            self.one_shot("b_wq1", lambda e: e.dma_start(out=V("bstat", c0=16, c1=16 + NKC), in_=t["bq1"]))
        if self.leg in ("B", "C"):
            self.one_shot("QT_ld0", lambda e: e.dma_start(
                out=V("QT").rearrange("p (m q) -> p m q", m=NKC),
                in_=t["QT_in"].rearrange("(m p) q -> p m q", p=128)))
            self.one_shot("bo", lambda e: e.dma_start(
                out=V("browB", 0, 1, 0, D), in_=t["bo_row"]), eng="gpsimd")
        if self.leg == "C":
            self.one_shot("cm0", lambda e: e.dma_start(
                out=V("cm").rearrange("p (k q) -> p k q", k=32),
                in_=t["cmask"].rearrange("k p q -> p k q")))
            self.one_shot("ksel", lambda e: e.dma_start(
                out=V("kselb", 0, 1), in_=t["ksel"]))

    def load_row_bf(self, src, dst_c0, cols, deps=(), sem="rowb"):
        pb, V, t = self.pb, self.V, self.t
        return pb.add("gpsimd", lambda e: e.dma_start(
            out=V("browB", 0, 1, dst_c0, dst_c0 + cols), in_=t[src]),
            dma=True, phased=False, deps=list(deps), sem=sem)

    def bcast_hilo(self, src_name, src_p, src_c0, width, dst_name, dst_c0, psum, deps,
                   extra_mm_deps=()):
        pb, V = self.pb, self.V
        p0, p1 = src_p, src_p + 1
        src = lambda: V(src_name, p0, p1, src_c0, src_c0 + width)
        hi = pb.add("vector", lambda e: e.tensor_copy(
            V("hilo", p0, p1, 0, width), src()), deps=list(deps))
        lo1 = pb.add("vector", lambda e: e.tensor_tensor(
            V("hiloF", p0, p1, 0, width), src(), V("hilo", p0, p1, 0, width), ALU.subtract),
            deps=[hi])
        lo = pb.add("vector", lambda e: e.tensor_copy(
            V("hilo", p0, p1, width, 2 * width), V("hiloF", p0, p1, 0, width)), deps=[lo1])
        ones = (lambda: V("onesr", 0, 1)) if src_p == 0 else (lambda: V("onesF", p0, p1))
        mms = []
        for cc in range(0, width, 512):
            ce = min(width, cc + 512)
            pb.add("tensor", lambda e, cc=cc, ce=ce: e.matmul(
                psum[:, cc:ce], ones(), V("hilo", p0, p1, cc, ce),
                start=True, stop=False), deps=[hi, self.tok["onesr"], self.tok["onesF"]] + list(extra_mm_deps))
            mms.append(pb.add("tensor", lambda e, cc=cc, ce=ce: e.matmul(
                psum[:, cc:ce], ones(), V("hilo", p0, p1, width + cc, width + ce),
                start=False, stop=True), deps=[lo]))
        return pb.add("vector", lambda e: e.tensor_copy(
            V(dst_name, 0, 128, dst_c0, dst_c0 + width), psum[:, 0:width]), deps=mms)

    def emit_ln(self, tag):
        pb, V = self.pb, self.V
        tok = self.tok
        sq_toks, mean_toks = [], []
        for tt in range(NT):
            sq_toks.append(pb.add("scalar", lambda e, tt=tt: e.activation(
                V("sq"), V("x", c0=tt * D, c1=(tt + 1) * D), AF.Square,
                accum_out=V("stat", c0=8 + tt, c1=9 + tt)), deps=[tok["x"]]))
            mean_toks.append(pb.add("vector", lambda e, tt=tt: e.tensor_reduce(
                V("stat", c0=tt, c1=tt + 1), V("x", c0=tt * D, c1=(tt + 1) * D),
                AX.X, ALU.add), deps=[tok["x"]]))
        tmean = pb.add("vector", lambda e: e.tensor_scalar(
            V("stat", c0=16, c1=24), V("stat", c0=0, c1=8), -1.0 / D, None, ALU.mult),
            deps=mean_toks)
        tmsq = pb.add("vector", lambda e: e.tensor_tensor(
            V("stat", c0=24, c1=32), V("stat", c0=16, c1=24), V("stat", c0=16, c1=24), ALU.mult),
            deps=[tmean])
        ts2 = pb.add("vector", lambda e: e.tensor_scalar(
            V("stat", c0=32, c1=40), V("stat", c0=8, c1=16), 1.0 / D, None, ALU.mult),
            deps=sq_toks)
        tvar = pb.add("vector", lambda e: e.tensor_tensor(
            V("stat", c0=40, c1=48), V("stat", c0=32, c1=40), V("stat", c0=24, c1=32), ALU.subtract),
            deps=[tmsq, ts2])
        tsd = pb.add("scalar", lambda e: e.activation(
            V("stat", c0=48, c1=56), V("stat", c0=40, c1=48), AF.Sqrt, bias=np.float32(1e-5)), deps=[tvar])
        trstd = pb.add("vector", lambda e: e.reciprocal(
            V("stat", c0=56, c1=64), V("stat", c0=48, c1=56)), deps=[tsd])

        copy_toks = {}
        for tt in range(NT):
            deps = [trstd, tok.get(("stageW", 0)), tok.get(("stageW", 1))]
            if tt >= 1:
                deps.append(copy_toks[(tt - 1, 1)])
            xn_t = pb.add("vector", lambda e, tt=tt: e.tensor_scalar(
                V("xn"), V("x", c0=tt * D, c1=(tt + 1) * D),
                V("stat", c0=16 + tt, c1=17 + tt), V("stat", c0=56 + tt, c1=57 + tt),
                ALU.add, ALU.mult), deps=deps)
            for rnd in range(2):
                tps = []
                for jj in range(8):
                    j = rnd * 8 + jj
                    d2 = [xn_t, tok["ident"]]
                    if jj == 0:
                        prev = (tt, 0) if rnd == 1 else ((tt - 1, 1) if tt >= 1 else None)
                        if prev in copy_toks:
                            d2.append(copy_toks[prev])
                    tps.append(pb.add("tensor", lambda e, j=j, jj=jj: e.transpose(
                        self.ps_tb[:, jj * 128:(jj + 1) * 128],
                        V("xn", c0=j * 128, c1=(j + 1) * 128),
                        V("ident")), deps=d2))
                sp, xoff, _ = self.m["xnT"]
                a = self.f32a if sp == "f" else self.bfa
                def emc(e, tt=tt, rnd=rnd, xoff=xoff, a=a):
                    dst = a[:, xoff + rnd * 8 * TPC: xoff + (rnd + 1) * 8 * TPC]
                    dstv = dst.rearrange("p (j q) -> p j q", j=8)[:, :, tt * 128:(tt + 1) * 128]
                    return e.tensor_copy(dstv, self.ps_tb[:, :].rearrange("p (j q) -> p j q", j=8))
                copy_toks[(tt, rnd)] = pb.add("vector", emc, deps=[tps[-1]])
        tok["xnT_done"] = list(copy_toks.values())
        pb.barrier()

    def xnT(self, kc, q0, q1):
        return self.V("xnT", c0=kc * TPC + q0, c1=kc * TPC + q1)

    def emit_ffn0(self):
        pb, V, t, tok = self.pb, self.V, self.t, self.tok
        pre = self.leg == "B" and PRECAST and PRECAST_CONSUME
        gu_src = t["guB"] if pre else t["guA"]
        d_src = t["dB"] if pre else t["dA"]
        w_eng = "sync" if pre else "gpsimd"
        gs_t = tok["gs"]
        us_t = tok["us"]
        ds_ld = tok["ds"]
        ds_half = pb.add("vector", lambda e: e.tensor_scalar(
            V("ds_bc", 0, 1, 0, D), V("ds_bc", 0, 1, 0, D), 0.5, None, ALU.mult),
            deps=[ds_ld] + list(self.pb.phase_deps))
        dsbc = ds_half
        for qd in range(4):
            dsbc = self.bcast_hilo("ds_bc", 0, qd * 512, 512, "ds_bc2", qd * 512,
                                   self.ps_a, [ds_half, dsbc])

        hdn_tok = {}
        gu_dma = {}
        cons_gu = {}
        epi_done = {}
        dn_ev = {}
        gcons = {}
        def emit_gu_dma(half, ic):
            if ic >= NIC:
                return
            sl = ic % 2
            war = [cons_gu.get((half, ic - 2))]
            if half == 1 and ic < 2:
                war.append(cons_gu.get((0, NIC - 2 + ic)))
            if half == 0 and ic < 2:
                war = war + tok["xnT_done"] + [tok.get("wgu_gate")]
            if pre and ic < 2 and half == 0:
                war = war + (tok.get("precast") or [None])
            gu_dma[(half, ic)] = pb.add(w_eng, lambda e, ic=ic, sl=sl: e.dma_start(
                out=V("wgu", c0=sl * 2 * D, c1=(sl + 1) * 2 * D), in_=gu_src[ic]),
                deps=war, dma=True, phased=False, sem=f"gu{sl}")

        for half in range(2):
            q0, q1 = half * 512, (half + 1) * 512
            if half == 0:
                emit_gu_dma(0, 0)
                emit_gu_dma(0, 1)
            for ic in range(NIC):
                sl = ic % 2
                psl = ic % 2
                mg = mu = None
                d_first = [gu_dma[(half, ic)], epi_done.get((half, ic - 2)), dsbc] + tok["xnT_done"]
                if half == 1 and ic < 2:
                    d_first.append(epi_done.get((0, NIC - 2 + ic)))
                    d_first += [dn_ev.get((0, 3, tx)) for tx in range(4)]
                for kc in range(NKC):
                    mg = pb.add("tensor", lambda e, sl=sl, kc=kc, psl=psl, q0=q0, q1=q1: e.matmul(
                        self.ps_a[:, psl * 1024: psl * 1024 + 512],
                        V("wgu", c0=sl * 2 * D + kc * 128, c1=sl * 2 * D + (kc + 1) * 128),
                        self.xnT(kc, q0, q1),
                        start=(kc == 0), stop=(kc == NKC - 1)),
                        deps=(d_first if kc == 0 else []))
                for kc in range(NKC):
                    mu = pb.add("tensor", lambda e, sl=sl, kc=kc, psl=psl, q0=q0, q1=q1: e.matmul(
                        self.ps_a[:, psl * 1024 + 512: psl * 1024 + 1024],
                        V("wgu", c0=sl * 2 * D + D + kc * 128, c1=sl * 2 * D + D + (kc + 1) * 128),
                        self.xnT(kc, q0, q1),
                        start=(kc == 0), stop=(kc == NKC - 1)), deps=[])
                cons_gu[(half, ic)] = mu
                if ic + 2 < NIC:
                    emit_gu_dma(half, ic + 2)
                elif half == 0:
                    emit_gu_dma(1, ic + 2 - NIC)
                ty = pb.add("vector", lambda e, ic=ic, psl=psl: e.tensor_scalar(
                    V("scr0"), self.ps_a[:, psl * 1024: psl * 1024 + 512],
                    V("gsc", c0=ic, c1=ic + 1), None, ALU.mult), deps=[mg, gs_t])
                tsl = pb.add("scalar", lambda e: e.activation(V("sbf"), V("scr0"), AF.Silu),
                             deps=[ty, hdn_tok.get((half, ic - 1), hdn_tok.get((half - 1, NIC - 1)))])
                tu = pb.add("vector", lambda e, ic=ic, psl=psl: e.tensor_scalar(
                    V("scr1"), self.ps_a[:, psl * 1024 + 512: psl * 1024 + 1024],
                    V("usc", c0=ic, c1=ic + 1), None, ALU.mult), deps=[mu, us_t])
                hdn_tok[(half, ic)] = pb.add("vector", lambda e, ic=ic: e.tensor_tensor(
                    V("hdn", c0=ic * 512, c1=(ic + 1) * 512), V("sbf"), V("scr1"), ALU.mult),
                    deps=[tsl, tu])
                epi_done[(half, ic)] = hdn_tok[(half, ic)]
            hdeps = [hdn_tok[(half, ic)] for ic in range(NIC)]
            if self.dbg and half == 0:
                self.dout("d_hdn0", (128, NIC * 512), BF16)
                tk = pb.add("sync", lambda e: e.dma_start(
                    out=self.t["d_hdn0"], in_=V("hdn")),
                    deps=hdeps, dma=True, phased=False, sem="dbg")
                pb.add("sync", lambda e: None, deps=[tk], phased=False)
            groups = [self.ps_b[:, 0:512], self.ps_b[:, 512:1024],
                      self.ps_a[:, 0:512], self.ps_a[:, 512:1024]]
            dw_dma = {}

            def emit_dw_dma(half, seq):
                if seq >= 44:
                    return
                n2, j11 = seq // 11, seq % 11
                gseq = half * 44 + seq
                sl = gseq % 2
                war = [gcons.get(gseq - 2)]
                if gseq < 2:
                    war = war + tok["xnT_done"] + [tok.get("dwn_gate")]
                if pre and gseq < 2:
                    war = war + (tok.get("precast") or [None])
                dw_dma[(n2, j11)] = (sl, pb.add(w_eng, lambda e, n2=n2, j11=j11, sl=sl: e.dma_start(
                    out=V("dwn", c0=sl * 2048, c1=(sl + 1) * 2048).rearrange("p (g c) -> p g c", g=2),
                    in_=d_src[n2, 2 * j11:2 * j11 + 2].rearrange("g p c -> p g c")),
                    deps=war, dma=True, phased=False, sem=f"dw{sl}"))

            emit_dw_dma(half, 0)
            emit_dw_dma(half, 1)
            for n in range(4):
                last = {}
                for j11 in range(11):
                    gseq = half * 44 + n * 11 + j11
                    sl, dtok = dw_dma[(n, j11)]
                    for kc in range(4):
                        ic = j11 * 4 + kc
                        for t4 in range(4):
                            d0 = []
                            if ic == 0 and t4 == 0:
                                d0 = hdeps + [dtok] + [dn_ev.get((half, n - 1, tx)) for tx in range(4)]
                                if n == 0:
                                    d0 += [cons_gu[(half, NIC - 1)], dsbc]
                                    d0 += [dn_ev.get((half - 1, 3, tx)) for tx in range(4)]
                            elif kc == 0 and t4 == 0:
                                d0 = [dtok]
                            last[t4] = pb.add("tensor", lambda e, ic=ic, sl=sl, kc=kc, t4=t4, grp=groups[t4]: e.matmul(
                                grp,
                                V("hdn", c0=ic * 512 + t4 * 128, c1=ic * 512 + (t4 + 1) * 128),
                                V("dwn", c0=sl * 2048 + kc * 512, c1=sl * 2048 + (kc + 1) * 512),
                                start=(ic == 0), stop=(ic == NIC - 1)), deps=d0)
                    gcons[gseq] = last[3]
                    emit_dw_dma(half, n * 11 + j11 + 2)
                for t4 in range(4):
                    tt = half * 4 + t4
                    scr = "scr0" if t4 % 2 == 0 else "scr1"
                    tm = pb.add("vector", lambda e, n=n, scr=scr, grp=groups[t4]: e.tensor_tensor(
                        V(scr), grp, V("ds_bc2", c0=n * 512, c1=(n + 1) * 512), ALU.mult),
                        deps=[last[t4], dsbc, hdn_tok[(half, NIC - 1)]])
                    dn_ev[(half, n, t4)] = pb.add("vector", lambda e, tt=tt, n=n, scr=scr: e.tensor_tensor(
                        V("x", c0=tt * D + n * 512, c1=tt * D + (n + 1) * 512),
                        V(scr),
                        V("x", c0=tt * D + n * 512, c1=tt * D + (n + 1) * 512), ALU.add),
                        deps=[tm])
        tok["gu_last"] = cons_gu[(1, NIC - 1)]
        tok["dw_last"] = gcons[87]
        if self.dbg:
            self.dout("d_xh0", (128, NT * D), F32)
            tk = pb.add("sync", lambda e: e.dma_start(
                out=self.t["d_xh0"], in_=V("x")),
                deps=[dn_ev[(0, n2, t2)] for n2 in range(4) for t2 in range(4)],
                dma=True, phased=False, sem="dbg")
            pb.add("sync", lambda e: None, deps=[tk], phased=False)
        pb.barrier()

    def emit_proj_fm(self, w_name, b_name, nm, out_name, bias_c0, scale=None, rope=False):
        pb, V, t, tok = self.pb, self.V, self.t, self.tok
        bq = tok["b_" + w_name]
        if scale is not None:
            bq = pb.add("vector", lambda e: e.tensor_scalar(
                V("bstat", c0=bias_c0, c1=bias_c0 + nm),
                V("bstat", c0=bias_c0, c1=bias_c0 + nm), scale, None, ALU.mult),
                deps=[bq] + list(self.pb.phase_deps))
        wdma, cons, ev_tok, st_tok = {}, {}, {}, {}

        pre2 = False and self.leg == "B" and PRECAST and PRECAST_CONSUME and w_name in ("wk1", "wq1")
        w_src = t[w_name + "B"] if pre2 else t[w_name]
        wp_eng = "sync" if pre2 else "gpsimd"

        def emit_wdma(mm_):
            if mm_ >= nm:
                return
            sl = mm_ % 2
            war = [cons.get(mm_ - 2)]
            if mm_ < 2:
                war.append(tok.get("wproj_gate"))
                if pre2:
                    war = war + (tok.get("precast_wp") or [None])
            wdma[mm_] = pb.add(wp_eng, lambda e, m=mm_, sl=sl: e.dma_start(
                out=V("wproj", c0=sl * D, c1=(sl + 1) * D), in_=w_src[m]),
                deps=war, dma=True, phased=False, sem=f"wp{sl}")

        emit_wdma(0)
        emit_wdma(1)
        for m in range(nm):
            sl = m % 2
            psl = self.ps_b if m % 2 == 0 else self.ps_a
            mmt = None
            for qh in range(2):
                for kc in range(NKC):
                    d0 = ([wdma[m], ev_tok.get(m - 2)] + tok["xnT_done"]) if (qh == 0 and kc == 0) else []
                    mmt = pb.add("tensor", lambda e, m=m, sl=sl, kc=kc, qh=qh, psl=psl: e.matmul(
                        psl[:, qh * 512:(qh + 1) * 512],
                        V("wproj", c0=sl * D + kc * 128, c1=sl * D + (kc + 1) * 128),
                        self.xnT(kc, qh * 512, (qh + 1) * 512),
                        start=(kc == 0), stop=(kc == NKC - 1)), deps=d0)
            cons[m] = mmt
            emit_wdma(m + 2)
            ssl = m % 2
            sc = 1.0 if scale is None else scale
            slot_war = st_tok.get(m - 2, tok.get(("stageW", ssl)))
            if not rope:
                ev = pb.add("scalar", lambda e, m=m, psl=psl, ssl=ssl, sc=sc: e.activation(
                    V("stage", c0=ssl * D, c1=ssl * D + TPC), psl[:, 0:1024], AF.Identity,
                    bias=V("bstat", c0=bias_c0 + m, c1=bias_c0 + m + 1), scale=np.float32(sc)),
                    deps=[mmt, bq, slot_war])
            else:
                evf = pb.add("scalar", lambda e, m=m, psl=psl: e.activation(
                    V("qf"), psl[:, 0:1024], AF.Identity,
                    bias=V("bstat", c0=bias_c0 + m, c1=bias_c0 + m + 1), scale=np.float32(1.0)),
                    deps=[mmt, bq, slot_war, ev_tok.get(m - 1)])
                ev = self.emit_rope(evf, ssl, prev_ev=ev_tok.get(m - 1), slot_war=slot_war)
            ev_tok[m] = ev
            st_tok[m] = pb.add("sync", lambda e, m=m, ssl=ssl: e.dma_start(
                out=t[out_name][m * 128:(m + 1) * 128, :],
                in_=V("stage", c0=ssl * D, c1=ssl * D + TPC)),
                deps=[ev], dma=True, phased=False, sem=f"st{ssl}")
            tok[("stageW", ssl)] = st_tok[m]
        tok["fm_last"] = cons[nm - 1]
        for m in range(nm):
            self.pb.add("sync", lambda e: None, deps=[st_tok[m]], phased=False)
        pb.barrier()

    def emit_rope(self, evf, ssl, prev_ev=None, slot_war=None):
        pb, V = self.pb, self.V
        sw0 = pb.add("sync", lambda e: e.dma_start(
            out=V("qswap", 0, 64), in_=V("qf", 64, 128)), deps=[evf, prev_ev],
            dma=True, phased=False, sem="sw")
        sw1 = pb.add("sync", lambda e: e.dma_start(
            out=V("qswap", 64, 128), in_=V("qf", 0, 64)), deps=[evf, prev_ev],
            dma=True, phased=False, sem="sw")
        outs = []
        for hh in range(2):
            c0, c1 = hh * 512, (hh + 1) * 512
            ta = pb.add("vector", lambda e, c0=c0, c1=c1: e.tensor_tensor(
                V("scr0", 0, 64), V("qf", 0, 64, c0, c1), V("cs_c", 0, 64, c0, c1), ALU.mult),
                deps=[evf, self.tok["cs"], outs[-1] if outs else None])
            tb = pb.add("vector", lambda e, c0=c0, c1=c1: e.tensor_tensor(
                V("scr1", 0, 64), V("qswap", 0, 64, c0, c1), V("cs_s", 0, 64, c0, c1), ALU.mult),
                deps=[sw0, self.tok["cs"]])
            o1 = pb.add("vector", lambda e, c0=c0, c1=c1, ssl=ssl: e.tensor_tensor(
                V("stage", 0, 64, ssl * D + c0, ssl * D + c1),
                V("scr0", 0, 64), V("scr1", 0, 64), ALU.subtract), deps=[ta, tb, slot_war])
            tc = pb.add("vector", lambda e, c0=c0, c1=c1: e.tensor_tensor(
                V("scr0", 64, 128), V("qswap", 64, 128, c0, c1), V("cs_s", 64, 128, c0, c1), ALU.mult),
                deps=[sw1, o1])
            td = pb.add("vector", lambda e, c0=c0, c1=c1: e.tensor_tensor(
                V("scr1", 64, 128), V("qf", 64, 128, c0, c1), V("cs_c", 64, 128, c0, c1), ALU.mult),
                deps=[o1])
            o2 = pb.add("vector", lambda e, c0=c0, c1=c1, ssl=ssl: e.tensor_tensor(
                V("stage", 64, 128, ssl * D + c0, ssl * D + c1),
                V("scr0", 64, 128), V("scr1", 64, 128), ALU.add), deps=[tc, td])
            outs.append(o2)
        return outs[-1]

    def emit_proj_tm(self, w_name, wbuf, bias_tok, out_name, width):
        pb, V, t, tok = self.pb, self.V, self.t, self.tok
        pre2 = False and self.leg == "B" and PRECAST and PRECAST_CONSUME and w_name == "wv1"
        wv_src = t[w_name + "B"] if pre2 else t[w_name]
        wv_eng = "sync" if pre2 else "gpsimd"
        nsl = width // 512
        cons_prev, ev, stt = {}, {}, {}
        for n in range(nsl):
            war = [cons_prev.get(n - 1)]
            if n == 0:
                war.append(tok.get("wvsl_gate"))
                if pre2:
                    war = war + (tok.get("precast_wv") or [None])
            wd = pb.add(wv_eng, lambda e, n=n: e.dma_start(
                out=V(wbuf, c0=0, c1=NKC * 512), in_=wv_src[n]),
                deps=war, dma=True, phased=False, sem="wv0")
            for tt in range(NT):
                idx = n * NT + tt
                sl = idx % 2
                mmt = None
                prev_psum = ev.get(idx - 2)
                for kc in range(NKC):
                    d0 = ([wd, prev_psum] + tok["xnT_done"]) if kc == 0 else []
                    mmt = pb.add("tensor", lambda e, n=n, tt=tt, kc=kc: e.matmul(
                        self.ps_b[:, (tt % 2) * 512:(tt % 2) * 512 + 512],
                        self.xnT(kc, tt * 128, (tt + 1) * 128),
                        V(wbuf, c0=kc * 512, c1=(kc + 1) * 512),
                        start=(kc == 0), stop=False), deps=d0)
                mb = pb.add("tensor", lambda e, n=n, tt=tt: e.matmul(
                    self.ps_b[:, (tt % 2) * 512:(tt % 2) * 512 + 512],
                    V("onesr", 0, 1),
                    V("browB", 0, 1, n * 512, (n + 1) * 512),
                    start=False, stop=True), deps=[bias_tok, tok["onesr"]])
                cons_prev[n] = mb
                slot_war = stt.get(idx - 2, tok.get(("stageW", sl)))
                ev[idx] = pb.add("vector", lambda e, tt=tt, sl=sl: e.tensor_copy(
                    V("stage", c0=sl * 512, c1=(sl + 1) * 512),
                    self.ps_b[:, (tt % 2) * 512:(tt % 2) * 512 + 512]),
                    deps=[mb, slot_war])
                stt[idx] = pb.add("sync", lambda e, n=n, tt=tt, sl=sl: e.dma_start(
                    out=t[out_name][tt * 128:(tt + 1) * 128, n * 512:(n + 1) * 512],
                    in_=V("stage", c0=sl * 512, c1=(sl + 1) * 512)),
                    deps=[ev[idx]], dma=True, phased=False, sem=f"sv{sl}")
                tok[("stageW", sl)] = stt[idx]
        for k in stt:
            self.pb.add("sync", lambda e: None, deps=[stt[k]], phased=False)
        pb.barrier()

    def emit_out_proj(self, w_name, b_name):
        pb, V, t, tok = self.pb, self.V, self.t, self.tok
        pre2 = self.leg in ("B", "C") and PRECAST and PRECAST_CONSUME and w_name == "wo"
        wo_src = t["woB"] if pre2 else t[w_name]
        wo_eng = "sync" if pre2 else "gpsimd"
        brow_t = tok["bo"]
        cons_prev = {}
        opev = {}
        dbuf = self.leg == "C"
        for n in range(4):
            wb = "wosl2" if (dbuf and n % 2 == 1) else "wosl"
            war = [cons_prev.get(n - 2 if dbuf else n - 1)]
            if n < (2 if dbuf else 1):
                war.append(tok.get("wosl_gate"))
                if pre2:
                    war = war + (tok.get("precast_wo") or [None])
            wd = pb.add(wo_eng, lambda e, n=n, wb=wb: e.dma_start(
                out=V(wb, c0=0, c1=NKC * 512), in_=wo_src[n]),
                deps=war, dma=True, phased=False, sem=f"wo{n % 2 if dbuf else 0}")
            for tt in range(NT):
                mmt = None
                pev = opev.get((n, tt - 2)) if tt >= 2 else opev.get((n - 1, NT - 2 + tt))
                for kc in range(NKC):
                    d0 = [wd, pev] if kc == 0 else []
                    mmt = pb.add("tensor", lambda e, n=n, tt=tt, kc=kc, wb=wb: e.matmul(
                        self.ps_b[:, (tt % 2) * 512:(tt % 2) * 512 + 512],
                        V("onT", c0=kc * TPC + tt * 128, c1=kc * TPC + (tt + 1) * 128),
                        V(wb, c0=kc * 512, c1=(kc + 1) * 512),
                        start=(kc == 0), stop=False), deps=d0)
                mb = pb.add("tensor", lambda e, n=n, tt=tt: e.matmul(
                    self.ps_b[:, (tt % 2) * 512:(tt % 2) * 512 + 512],
                    V("onesr", 0, 1), V("browB", 0, 1, n * 512, (n + 1) * 512),
                    start=False, stop=True), deps=[brow_t, tok["onesr"]])
                cons_prev[n] = mb
                opev[(n, tt)] = pb.add("vector", lambda e, n=n, tt=tt: e.tensor_tensor(
                    V("x", c0=tt * D + n * 512, c1=tt * D + (n + 1) * 512),
                    self.ps_b[:, (tt % 2) * 512:(tt % 2) * 512 + 512],
                    V("x", c0=tt * D + n * 512, c1=tt * D + (n + 1) * 512), ALU.add),
                    deps=[mb])
        tok["bo_last"] = cons_prev[3]
        pb.barrier()

    def emit_ffn1(self):
        pb, V, t, tok = self.pb, self.V, self.t, self.tok
        brow_t = self.load_row_bf("b1_row", 0, D, deps=[tok.get("bo_last")])
        f1ev = {}
        psc_free = tok.get("psc_free_f1")
        for n in range(4):
            amms, sgn = [], []
            for kc in range(NKC):
                sl = kc % 2
                war = [sgn[kc - 2] if kc >= 2 else None]
                if kc < 2:
                    war += [f1ev.get((n - 1, NT - 1)), tok.get("w1f_gate")]
                wd = pb.add("gpsimd", lambda e, n=n, kc=kc, sl=sl: e.dma_start(
                    out=V("w1f", c0=sl * 512, c1=(sl + 1) * 512),
                    in_=t["w1"][n][:, kc * 512:(kc + 1) * 512]),
                    deps=war, dma=True, phased=False, sem=f"w1{sl}")
                ab = pb.add("scalar", lambda e, sl=sl: e.activation(
                    V("sbf"), V("w1f", c0=sl * 512, c1=(sl + 1) * 512), AF.Abs),
                    deps=[wd, amms[kc - 1] if kc >= 1 else tok.get("amms_last")])
                amms.append(pb.add("tensor", lambda e, kc=kc: e.matmul(
                    self.ps_c[0:1, 0:512], V("onesc"), V("sbf"),
                    start=(kc == 0), stop=(kc == NKC - 1)),
                    deps=([ab, tok["onesc"], psc_free] if kc == 0 else [ab])))
                sgn.append(pb.add("scalar", lambda e, kc=kc, sl=sl: e.activation(
                    V("w1sl", c0=kc * 512, c1=(kc + 1) * 512),
                    V("w1f", c0=sl * 512, c1=(sl + 1) * 512), AF.Sign), deps=[wd]))
            al = pb.add("vector", lambda e: e.tensor_scalar(
                V("rows", 0, 1, 0, 512), self.ps_c[0:1, 0:512], 0.5 / D, None, ALU.mult),
                deps=[amms[-1]])
            tok["amms_last"] = amms[-1]
            psc_free = al
            albc = self.bcast_hilo("rows", 0, 0, 512, "abc", 0, self.ps_a, [al])
            bw = []
            for kc in range(NKC):
                bw.append(pb.add("vector", lambda e, kc=kc: e.tensor_tensor(
                    V("w1sl", c0=kc * 512, c1=(kc + 1) * 512),
                    V("w1sl", c0=kc * 512, c1=(kc + 1) * 512),
                    V("abc", c0=0, c1=512), ALU.mult), deps=[albc, sgn[kc]]))
            for tt in range(NT):
                mmt = None
                pev = f1ev.get((n, tt - 2)) if tt >= 2 else f1ev.get((n - 1, NT - 2 + tt))
                for kc in range(NKC):
                    d0 = (bw + tok["xnT_done"] + [pev]) if kc == 0 else []
                    mmt = pb.add("tensor", lambda e, n=n, tt=tt, kc=kc: e.matmul(
                        self.ps_b[:, (tt % 2) * 512:(tt % 2) * 512 + 512],
                        self.xnT(kc, tt * 128, (tt + 1) * 128),
                        V("w1sl", c0=kc * 512, c1=(kc + 1) * 512),
                        start=(kc == 0), stop=False), deps=d0)
                mb = pb.add("tensor", lambda e, n=n, tt=tt: e.matmul(
                    self.ps_b[:, (tt % 2) * 512:(tt % 2) * 512 + 512],
                    V("halfr", 0, 1), V("browB", 0, 1, n * 512, (n + 1) * 512),
                    start=False, stop=True), deps=[brow_t, tok["halfr"]])
                f1ev[(n, tt)] = pb.add("vector", lambda e, n=n, tt=tt: e.tensor_tensor(
                    V("x", c0=tt * D + n * 512, c1=tt * D + (n + 1) * 512),
                    self.ps_b[:, (tt % 2) * 512:(tt % 2) * 512 + 512],
                    V("x", c0=tt * D + n * 512, c1=tt * D + (n + 1) * 512), ALU.add),
                    deps=[mb])
        tok["psc_free_f1"] = psc_free
        tok["b1_last"] = pb.last("tensor")
        pb.barrier()

    def emit_attention(self, layer):
        pb, V, t, tok = self.pb, self.V, self.t, self.tok
        causal = (layer == 1)
        dkv = DKV1 if causal else D
        post = (2048.0 ** -0.5) if causal else 1.0
        tok["QT_ld"] = tok["QT_ld0"]
        if causal:
            tok["cm_ld"] = tok["cm0"]
            ms0 = pb.add("vector", lambda e: e.memset(V("shiftr", 0, 128, 0, 1024), 0.0))
        if not causal:
            prevq = {}
            for h2 in range(NH):
                qsq = pb.add("vector", lambda e, h2=h2: e.tensor_tensor(
                    V("at0", c0=0, c1=1024), V("QT", c0=h2 * TPC, c1=(h2 + 1) * TPC),
                    V("QT", c0=h2 * TPC, c1=(h2 + 1) * TPC), ALU.mult),
                    deps=[tok["QT_ld"], prevq.get(h2 - 1)])
                mq2 = {}
                for qh in range(2):
                    pr = 0 if qh == 0 else 32
                    mq2[qh] = pb.add("tensor", lambda e, qh=qh, pr=pr: e.matmul(
                        self.ps_c[pr:pr + 1, 0:512], V("onesc"),
                        V("at0", c0=qh * 512, c1=(qh + 1) * 512),
                        start=True, stop=True),
                        deps=[qsq, tok["onesc"], prevq.get(h2 - 1)])
                for qh in range(2):
                    pr = 0 if qh == 0 else 32
                    prevq[h2] = pb.add("vector", lambda e, pr=pr, h2=h2: e.tensor_reduce(
                        V("stat", pr, pr + 1, 40 + h2, 41 + h2),
                        self.ps_c[pr:pr + 1, 0:512], AX.X, ALU.max),
                        deps=[mq2[0], mq2[1]])
            lnq_toks = []
            for pr in (0, 32):
                lnq_toks.append(pb.add("scalar", lambda e, pr=pr: e.activation(
                    V("stat", pr, pr + 1, 40, 56), V("stat", pr, pr + 1, 40, 56), AF.Ln),
                    deps=[prevq[NH - 1]]))
            tok["lnq_all"] = lnq_toks[-1]
            tok["maxq_done"] = prevq[NH - 1]
        cons_kv = {}
        kn_toks = {}

        if not causal:
            def make_prep(hh):
                if hh >= NH:
                    return []
                ksl2 = hh % 2
                st = kn_toks.setdefault(hh, {"bias": {}, "bias_mm": {}})
                steps = []

                def s_kd():
                    kd2 = []
                    for r2 in range(4):
                        kd2.append(pb.add("sync", lambda e, hh=hh, r2=r2, ksl2=ksl2: e.dma_start(
                            out=V("kth", c0=ksl2 * 4096 + r2 * 1024, c1=ksl2 * 4096 + (r2 + 1) * 1024),
                            in_=t["KT_full"][r2 * dkv + hh * 128: r2 * dkv + hh * 128 + 128, :]),
                            deps=[cons_kv.get(hh - 2)], dma=True, phased=False, sem=f"kt{ksl2}"))
                        kd2.append(pb.add("sync", lambda e, hh=hh, r2=r2, ksl2=ksl2: e.dma_start(
                            out=V("vth", c0=ksl2 * 4096 + r2 * 1024, c1=ksl2 * 4096 + (r2 + 1) * 1024)
                                .rearrange("p (j d) -> p j d", j=8),
                            in_=t["V_full"][r2 * TPC:(r2 + 1) * TPC, hh * 128:(hh + 1) * 128]
                                .rearrange("(j p) d -> p j d", p=128)),
                            deps=[cons_kv.get(hh - 2)], dma=True, phased=False, sem=f"vt{ksl2}"))
                    st["kd"] = kd2
                steps.append(s_kd)

                for ppp in range(8):
                    def s_kn(ppp=ppp):
                        prev_c = kn_toks.get(hh - 1, {})
                        ks = pb.add("vector", lambda e, ksl2=ksl2, ppp=ppp: e.tensor_tensor(
                            V("ksq"),
                            V("kth", c0=ksl2 * 4096 + ppp * 512, c1=ksl2 * 4096 + (ppp + 1) * 512),
                            V("kth", c0=ksl2 * 4096 + ppp * 512, c1=ksl2 * 4096 + (ppp + 1) * 512),
                            ALU.mult),
                            deps=st["kd"] + [st.get("red"),
                                             prev_c.get("last_m1") if ppp == 0 else None])
                        m1 = pb.add("tensor", lambda e: e.matmul(
                            self.ps_t[0:1, 0:512], V("onesc"), V("ksq"),
                            start=True, stop=True),
                            deps=[ks, tok["onesc"], st.get("red")] +
                                 ([prev_c["bias"].get(0), prev_c["bias"].get(1)]
                                  if ppp == 0 and prev_c.get("bias") else []))
                        st["red"] = pb.add("vector", lambda e, ppp=ppp: e.tensor_reduce(
                            V("stat", 0, 1, 24 + ppp, 25 + ppp),
                            self.ps_t[0:1, 0:512], AX.X, ALU.max), deps=[m1])
                        st.setdefault("reds", []).append(st["red"])
                        st["last_m1"] = m1
                    steps.append(s_kn)

                def s_lnk():
                    knm = pb.add("vector", lambda e: e.tensor_reduce(
                        V("stat", 0, 1, 32, 33), V("stat", 0, 1, 24, 32), AX.X, ALU.max),
                        deps=st["reds"])
                    lnk = pb.add("scalar", lambda e: e.activation(
                        V("stat", 0, 1, 33, 34), V("stat", 0, 1, 32, 33), AF.Ln), deps=[knm])
                    lnk_b = pb.add("vector", lambda e, hh=hh: e.tensor_copy(
                        V("bcol", 0, 1, 4 + hh % 2, 5 + hh % 2), V("stat", 0, 1, 33, 34)), deps=[lnk])
                    mmk = pb.add("tensor", lambda e, hh=hh: e.matmul(
                        self.ps_t[:, 16:17], V("onesr", 0, 1),
                        V("bcol", 0, 1, 4 + hh % 2, 5 + hh % 2),
                        start=True, stop=True), deps=[lnk_b, tok["onesr"]])
                    st["lnk_bc"] = pb.add("vector", lambda e: e.tensor_copy(
                        V("stat", c0=34, c1=35), self.ps_t[:, 16:17]), deps=[mmk])
                steps.append(s_lnk)

                for qh in range(2):
                    def s_bias(qh=qh):
                        pr = 0 if qh == 0 else 32
                        colc = (hh % 2) * 2 + qh
                        prev2 = kn_toks.get(hh - 2, {})
                        t1 = pb.add("vector", lambda e, pr=pr, hh=hh: e.tensor_scalar(
                            V("stat", pr, pr + 1, 37, 38), V("stat", pr, pr + 1, 40 + hh, 41 + hh),
                            V("stat", pr, pr + 1, 34, 35), None, ALU.add),
                            deps=[tok["lnq_all"], st["lnk_bc"]])
                        t2 = pb.add("scalar", lambda e, pr=pr: e.activation(
                            V("stat", pr, pr + 1, 38, 39), V("stat", pr, pr + 1, 37, 38),
                            AF.Exp, scale=np.float32(0.5)), deps=[t1])
                        t3 = pb.add("vector", lambda e, pr=pr: e.tensor_scalar(
                            V("stat", pr, pr + 1, 39, 40), V("stat", pr, pr + 1, 38, 39),
                            -1.0, MARGIN, ALU.mult, op1=ALU.add), deps=[t2])
                        cast = pb.add("vector", lambda e, pr=pr, colc=colc: e.tensor_copy(
                            V("bcol", pr, pr + 1, colc, colc + 1), V("stat", pr, pr + 1, 39, 40)),
                            deps=[t3, prev2.get("bias_mm", {}).get(qh)])
                        bmm = pb.add("tensor", lambda e, pr=pr, colc=colc: e.matmul(
                            self.ps_t[:, 20 + colc:21 + colc],
                            V("onesr", 0, 1) if pr == 0 else V("onesF", 32, 33),
                            V("bcol", pr, pr + 1, colc, colc + 1),
                            start=True, stop=True),
                            deps=[cast, tok["onesF"], tok["onesr"],
                                  prev2.get("bias", {}).get(qh)])
                        st["bias_mm"][qh] = bmm
                        st["bias"][qh] = pb.add("vector", lambda e, colc=colc: e.tensor_copy(
                            V("sbias", 0, 128, colc, colc + 1), self.ps_t[:, 20 + colc:21 + colc]),
                            deps=[bmm])
                    steps.append(s_bias)
                return steps

            for s in make_prep(0):
                s()

        for h in range(NH):
            head_gate = [tok.get(("rz_done", h - 1)), tok.get(("onT_ev", h - 1, 1))]
            if h == 0:
                head_gate.append(tok.get("maxq_done"))
            if causal:
                g = h // 4
                new_kv = (h % 4 == 0)
                ksl = g % 2
                if new_kv:
                    kd = []
                    for r in range(4):
                        kd.append(pb.add("sync", lambda e, g=g, r=r, ksl=ksl: e.dma_start(
                            out=V("kth", c0=ksl * 4096 + r * 1024, c1=ksl * 4096 + (r + 1) * 1024),
                            in_=t["KT_full"][r * dkv + g * 128: r * dkv + g * 128 + 128, :]),
                            deps=[cons_kv.get(g - 2)], dma=True, phased=False,
                            sem=f"kt{ksl}"))
                        kd.append(pb.add("sync", lambda e, g=g, r=r, ksl=ksl: e.dma_start(
                            out=V("vth", c0=ksl * 4096 + r * 1024, c1=ksl * 4096 + (r + 1) * 1024)
                                .rearrange("p (j d) -> p j d", j=8),
                            in_=t["V_full"][r * TPC:(r + 1) * TPC, g * 128:(g + 1) * 128]
                                .rearrange("(j p) d -> p j d", p=128)),
                            deps=[cons_kv.get(g - 2)], dma=True, phased=False,
                            sem=f"vt{ksl}"))
                    tok[("kv", g)] = kd
                    kn_parts = []
                    prev = None
                    nseg = 4
                    for ppp in range(8):
                        ks = pb.add("vector", lambda e, ksl=ksl, ppp=ppp: e.tensor_tensor(
                            V("ksq"),
                            V("kth", c0=ksl * 4096 + ppp * 512, c1=ksl * 4096 + (ppp + 1) * 512),
                            V("kth", c0=ksl * 4096 + ppp * 512, c1=ksl * 4096 + (ppp + 1) * 512),
                            ALU.mult), deps=kd + head_gate + ([prev] if prev else []))
                        m1 = pb.add("tensor", lambda e: e.matmul(
                            self.ps_c[0:1, 0:512], V("onesc"), V("ksq"),
                            start=True, stop=True),
                            deps=[ks, tok["onesc"]] + head_gate + ([kn_parts[-1]] if kn_parts else []))
                        for sub in range(nseg):
                            cbase = 24 + ppp * nseg + sub
                            kn_parts.append(pb.add("vector", lambda e, sub=sub, cbase=cbase, nseg=nseg: e.tensor_reduce(
                                V("stat", 0, 1, cbase, cbase + 1),
                                self.ps_c[0:1, sub * (512 // nseg):(sub + 1) * (512 // nseg)],
                                AX.X, ALU.max), deps=[m1]))
                        prev = kn_parts[-1]
                    jt = []
                    for jq in range(8):
                        t1 = pb.add("vector", lambda e, jq=jq: e.tensor_tensor(
                            V("lnkr", 0, 1, 992, 1024),
                            V("stat", 0, 1, 24, 56),
                            V("kselb", 0, 1, jq * 32, (jq + 1) * 32), ALU.add),
                            deps=kn_parts + [tok["ksel"]])
                        jt.append(pb.add("vector", lambda e, jq=jq: e.tensor_reduce(
                            V("stat", 0, 1, 56 + jq, 57 + jq),
                            V("lnkr", 0, 1, 992, 1024), AX.X, ALU.max), deps=[t1]))
                    lnj = pb.add("scalar", lambda e: e.activation(
                        V("stat", 0, 1, 56, 64), V("stat", 0, 1, 56, 64), AF.Ln), deps=jt)
                    bcj = []
                    for jq in range(8):
                        bcj.append(pb.add("vector", lambda e, jq=jq: e.tensor_scalar(
                            V("lnkr", 0, 1, jq * 128, (jq + 1) * 128),
                            V("kselb", 0, 1, 0, 128),
                            0.0, V("stat", 0, 1, 56 + jq, 57 + jq),
                            ALU.mult, op1=ALU.add), deps=[lnj, tok["ksel"]]))
                    cpL = pb.add("vector", lambda e: e.tensor_copy(
                        V("hilo", 0, 1, 0, 512), V("lnkr", 0, 1, 512, 1024)), deps=bcj)
                    mm2 = pb.add("tensor", lambda e: e.matmul(
                        self.ps_a[:, 0:512],
                        V("onesr", 0, 1), V("hilo", 0, 1, 0, 512),
                        start=True, stop=True), deps=[cpL, tok["onesr"]] + head_gate)
                    tok[("lnkr2", g)] = pb.add("vector", lambda e: e.tensor_copy(
                        V("lnkr2", 32, 33, 0, 512), self.ps_a[32:33, 0:512]), deps=[mm2])
                    tok[("lnk_bc", g)] = bcj[-1]
                kv_deps = tok[("kv", g)]
                lnk_bc = tok[("lnk_bc", g)]

                negc = {}
                qn_ev = {}
                qsq = pb.add("vector", lambda e, h=h: e.tensor_tensor(
                    V("at0", c0=0, c1=1024), V("QT", c0=h * TPC, c1=(h + 1) * TPC),
                    V("QT", c0=h * TPC, c1=(h + 1) * TPC), ALU.mult),
                    deps=[tok["QT_ld"]] + head_gate + [tok.get(("avz", h - 1))])
                mqs = {}
                for qh in range(2):
                    pr = 0 if qh == 0 else 32
                    mqs[qh] = pb.add("tensor", lambda e, qh=qh, pr=pr: e.matmul(
                        self.ps_c[pr:pr + 1, 0:512], V("onesc"),
                        V("at0", c0=qh * 512, c1=(qh + 1) * 512),
                        start=True, stop=True),
                        deps=[qsq, tok["onesc"], lnk_bc] + head_gate)
                for qh in range(2):
                    pr = 0 if qh == 0 else 32
                    qn_ev[qh] = pb.add("vector", lambda e, pr=pr: e.tensor_copy(
                        V("rows", pr, pr + 1, 0, 512), self.ps_c[pr:pr + 1, 0:512]),
                        deps=[mqs[0], mqs[1]])
                for qh in range(2):
                    pr = 0 if qh == 0 else 32
                    l1 = pb.add("scalar", lambda e, pr=pr: e.activation(
                        V("rows", pr, pr + 1, 0, 512), V("rows", pr, pr + 1, 0, 512), AF.Ln),
                        deps=[qn_ev[qh]])
                    s1 = pb.add("vector", lambda e, qh=qh, pr=pr: e.tensor_tensor(
                        V("rows", pr, pr + 1, 0, 512), V("rows", pr, pr + 1, 0, 512),
                        V("lnkr", 0, 1, qh * 512, (qh + 1) * 512) if pr == 0
                        else V("lnkr2", 32, 33, 0, 512), ALU.add),
                        deps=[l1, lnk_bc, tok.get(("lnkr2", g))])
                    e1 = pb.add("scalar", lambda e, pr=pr: e.activation(
                        V("rows", pr, pr + 1, 0, 512), V("rows", pr, pr + 1, 0, 512),
                        AF.Exp, scale=np.float32(0.5)), deps=[s1])
                    negc[qh] = pb.add("vector", lambda e, pr=pr: e.tensor_scalar(
                        V("shiftr", pr, pr + 1, 0, 512), V("rows", pr, pr + 1, 0, 512),
                        -MARGIN, -1.0, ALU.add, op1=ALU.mult), deps=[e1])

                def c0_of(kt):
                    gk = (kt // 8) + 4 * (kt % 8)
                    return max(0, cdiv(gk - 3, 4)) * 128
                kts = [kt for kt in range(32) if c0_of(kt) < 1024]
                half_kts = {qh: [kt for kt in kts if c0_of(kt) < (qh + 1) * 512] for qh in range(2)}

                exp_toks = {}
                av_last = {}
                pend = {}
                av_all = []

                def emit_zav(idx):
                    if idx < 0 or idx not in pend:
                        return
                    kt, c0, asl = pend.pop(idx)
                    for qh in range(2):
                        cc, ce = qh * 512, (qh + 1) * 512
                        if ce <= c0:
                            continue
                        ccc = max(cc, c0)
                        first = (kt == half_kts[qh][0])
                        lastf = (kt == half_kts[qh][-1])
                        zdeps = [exp_toks[idx]] + (([qn_ev[0], qn_ev[1]] + head_gate) if first else [])
                        if qh == 0:
                            pb.add("tensor", lambda e, ccc=ccc, ce=ce, cc=cc, first=first, lastf=lastf, asl=asl: e.matmul(
                                self.ps_c[0:1, ccc - cc:512], V("onesc"),
                                V("at0", c0=asl * 1024 + ccc, c1=asl * 1024 + ce),
                                start=first, stop=lastf), deps=zdeps)
                        else:
                            pb.add("tensor", lambda e, ccc=ccc, ce=ce, cc=cc, first=first, lastf=lastf, asl=asl: e.matmul(
                                self.ps_c[32:33, ccc - cc:512], V("onesc"),
                                V("at0", c0=asl * 1024 + ccc, c1=asl * 1024 + ce),
                                start=first, stop=lastf), deps=zdeps)
                        lav = pb.add("tensor", lambda e, ccc=ccc, ce=ce, kt=kt, ksl=ksl, first=first, lastf=lastf, asl=asl: e.matmul(
                            self.ps_b[:, ccc:ce],
                            V("vth", c0=ksl * 4096 + kt * 128, c1=ksl * 4096 + (kt + 1) * 128),
                            V("at0", c0=asl * 1024 + ccc, c1=asl * 1024 + ce),
                            start=first, stop=lastf), deps=zdeps)
                        av_last[idx] = lav
                        av_all.append(lav)

                for idx, kt in enumerate(kts):
                    c0 = c0_of(kt)
                    slot = idx % 2
                    pd = [negc[0], negc[1], tok["onesr"], tok["onesF"], exp_toks.get(idx - 2)]
                    mm_last = None
                    for cc in range(c0 - (c0 % 512), 1024, 512):
                        ccc = max(cc, c0)
                        ce = cc + 512
                        qh = cc // 512
                        pr = 0 if qh == 0 else 32
                        pb.add("tensor", lambda e, slot=slot, ccc=ccc, ce=ce, pr=pr, cc=cc: e.matmul(
                            self.ps_a[:, slot * 1024 + ccc: slot * 1024 + ce],
                            V("onesr", 0, 1) if pr == 0 else V("onesF", pr, pr + 1),
                            V("shiftr", pr, pr + 1, ccc - cc, 512),
                            start=True, stop=False), deps=pd)
                        mm_last = pb.add("tensor", lambda e, slot=slot, ccc=ccc, ce=ce, kt=kt, ksl=ksl, h=h: e.matmul(
                            self.ps_a[:, slot * 1024 + ccc: slot * 1024 + ce],
                            V("kth", c0=ksl * 4096 + kt * 128, c1=ksl * 4096 + (kt + 1) * 128),
                            V("QT", c0=h * TPC + ccc, c1=h * TPC + ce),
                            start=False, stop=True), deps=(kv_deps + [tok["QT_ld"]]) if cc == c0 - (c0 % 512) else [])
                    emit_zav(idx - 2)
                    mask_tok = None
                    gk = (kt // 8) + 4 * (kt % 8)
                    jb = gk // 4
                    if jb * 128 >= c0:
                        mask_tok = pb.add("vector", lambda e, slot=slot, jb=jb, kt=kt: e.tensor_tensor(
                            self.ps_a[:, slot * 1024 + jb * 128: slot * 1024 + (jb + 1) * 128],
                            self.ps_a[:, slot * 1024 + jb * 128: slot * 1024 + (jb + 1) * 128],
                            V("cm", c0=kt * 128, c1=(kt + 1) * 128), ALU.add),
                            deps=[mm_last, tok["cm_ld"]])
                    asl = idx % 2
                    ed = [mm_last, mask_tok, av_last.get(idx - 2)]
                    if idx < 2:
                        ed.append(tok.get(("avz", h - 1)))
                    exp_toks[idx] = pb.add("scalar", lambda e, slot=slot, c0=c0, asl=asl: e.activation(
                        V("at0", c0=asl * 1024 + c0, c1=asl * 1024 + 1024),
                        self.ps_a[:, slot * 1024 + c0: slot * 1024 + 1024], AF.Exp), deps=ed)
                    pend[idx] = (kt, c0, asl)
                for tail in (2, 1):
                    emit_zav(len(kts) - tail)
                tok[("avz", h)] = av_all[-1]
                emd = [exp_toks[len(kts) - 1], exp_toks.get(len(kts) - 2)]
            else:
                st = kn_toks[h]
                kv_deps = st["kd"]
                ksl = h % 2
                steps_next = make_prep(h + 1)
                kts = list(range(32))
                exp_toks = {}
                av_last = {}
                sc_tok = {}
                pend = {}
                av_all = []

                def emit_zav(idx):
                    if idx < 0 or idx not in pend:
                        return
                    kt, asl = pend.pop(idx)
                    for qh in range(2):
                        cc, ce = qh * 512, (qh + 1) * 512
                        first = (kt == kts[0])
                        lastf = (kt == kts[-1])
                        zdeps = [exp_toks[(idx, qh)]] + (head_gate if first else [])
                        pb.add("tensor", lambda e, cc=cc, ce=ce, first=first, lastf=lastf, asl=asl, qh=qh: e.matmul(
                            self.ps_c[0:1, 0:512] if qh == 0 else self.ps_c[32:33, 0:512],
                            V("onesc"),
                            V("at0", c0=asl * 1024 + cc, c1=asl * 1024 + ce),
                            start=first, stop=lastf), deps=zdeps)
                        lav = pb.add("tensor", lambda e, cc=cc, ce=ce, kt=kt, ksl=ksl, first=first, lastf=lastf, asl=asl: e.matmul(
                            self.ps_b[:, cc:ce],
                            V("vth", c0=ksl * 4096 + kt * 128, c1=ksl * 4096 + (kt + 1) * 128),
                            V("at0", c0=asl * 1024 + cc, c1=asl * 1024 + ce),
                            start=first, stop=lastf), deps=zdeps)
                        av_last[(idx, qh)] = lav
                        av_all.append(lav)

                for idx, kt in enumerate(kts):
                    slot = idx % 2
                    asl = idx % 3
                    for qh in range(2):
                        cc, ce = qh * 512, (qh + 1) * 512
                        d0 = [exp_toks.get((idx - 2, qh))]
                        if idx == 0:
                            d0 = d0 + head_gate
                        if qh == 0:
                            d0 = d0 + kv_deps + [tok["QT_ld"]]
                        sc_tok[(idx, qh)] = pb.add("tensor", lambda e, slot=slot, cc=cc, ce=ce, kt=kt, ksl=ksl, h=h: e.matmul(
                            self.ps_a[:, slot * 1024 + cc: slot * 1024 + ce],
                            V("kth", c0=ksl * 4096 + kt * 128, c1=ksl * 4096 + (kt + 1) * 128),
                            V("QT", c0=h * TPC + cc, c1=h * TPC + ce),
                            start=True, stop=True), deps=d0)
                    emit_zav(idx - 2)
                    if steps_next:
                        steps_next.pop(0)()
                    for qh in range(2):
                        colc = (h % 2) * 2 + qh
                        ed = [sc_tok[(idx, qh)], av_last.get((idx - 3, qh)), st["bias"][qh]]
                        if idx < 3:
                            ed.append(tok.get(("avz", h - 1)))
                        exp_toks[(idx, qh)] = pb.add("scalar", lambda e, slot=slot, asl=asl, qh=qh, colc=colc: e.activation(
                            V("at0", c0=asl * 1024 + qh * 512, c1=asl * 1024 + (qh + 1) * 512),
                            self.ps_a[:, slot * 1024 + qh * 512: slot * 1024 + (qh + 1) * 512],
                            AF.Exp, bias=V("sbias", 0, 128, colc, colc + 1)), deps=ed)
                    pend[idx] = (kt, asl)
                for tail in (2, 1):
                    emit_zav(len(kts) - tail)
                while steps_next:
                    steps_next.pop(0)()
                tok[("avz", h)] = av_all[-1]
                nk = len(kts)
                emd = [exp_toks[(nk - 1, 0)], exp_toks[(nk - 1, 1)],
                       exp_toks[(nk - 2, 0)], exp_toks[(nk - 2, 1)]]

            rzx = {}
            for qh in range(2):
                pr = 0 if qh == 0 else 32
                src = (lambda: self.ps_c[0:1, 0:512]) if qh == 0 else (lambda: self.ps_c[32:33, 0:512])
                zf = pb.add("vector", lambda e, pr=pr, src=src: e.tensor_scalar(
                    V("rows", pr, pr + 1, 0, 512), src(), 1e-30, None, ALU.add),
                    deps=av_all)
                rr = pb.add("vector", lambda e, pr=pr: e.reciprocal(
                    V("rows", pr, pr + 1, 0, 512), V("rows", pr, pr + 1, 0, 512)), deps=[zf])
                rzx[qh] = pb.add("vector", lambda e, pr=pr: e.tensor_scalar(
                    V("rows", pr, pr + 1, 0, 512), V("rows", pr, pr + 1, 0, 512),
                    post, None, ALU.mult), deps=[rr])
            rzbc0 = self.bcast_hilo("rows", 0, 0, 512, "rzbc", 0, self.ps_a, [rzx[0]],
                                    extra_mm_deps=emd)
            rzbc1 = self.bcast_hilo("rows", 32, 0, 512, "rzbc", 512, self.ps_a[:, 512:1024],
                                    [rzx[1]], extra_mm_deps=emd)
            tok[("rz_done", h)] = rzbc1
            for qh in range(2):
                bc = rzbc0 if qh == 0 else rzbc1
                tok[("onT_ev", h, qh)] = pb.add("vector", lambda e, h=h, qh=qh: e.tensor_tensor(
                    V("onT", c0=h * TPC + qh * 512, c1=h * TPC + (qh + 1) * 512),
                    self.ps_b[:, qh * 512:(qh + 1) * 512],
                    V("rzbc", c0=qh * 512, c1=(qh + 1) * 512), ALU.mult),
                    deps=[bc] + av_all)
            cons_kv[g if causal else h] = tok[("onT_ev", h, 1)]
        pb.barrier()

    def dump(self, name, src_name, cols, bf=False):
        if not self.dbg:
            return
        self.dout(name, (128, cols), BF16 if bf else F32)
        tk = self.pb.add("sync", lambda e: e.dma_start(
            out=self.t[name], in_=self.V(src_name, c0=0, c1=cols)),
            deps=list(self.pb.phase_deps), dma=True, phased=False, sem="dbg")
        self.pb.add("sync", lambda e: None, deps=[tk], phased=False)

    def store_x(self):
        stx = self.pb.add("sync", lambda e: e.dma_start(
            out=self.t["x_out"].rearrange("(t p) d -> p t d", p=128),
            in_=self.V("x").rearrange("p (t d) -> p t d", t=NT)),
            deps=list(self.pb.phase_deps), dma=True, phased=False, sem="sx")
        self.pb.add("sync", lambda e: None, deps=[stx], phased=False)


    def build(self):
        leg = self.leg
        self.din("x_in", (TPC, D))
        self.din("ones_row", (1, 128), BF16)
        self.din("half_row", (1, 128), BF16)
        self.din("ones_col", (128, 1), BF16)
        self.din("ones_full", (128, 128), BF16)
        self.din("ident", (128, 128), BF16)
        if leg in ("A", "B"):
            self.din("guA", (NIC, 128, 2 * D))
            self.din("dA", (4, 22, 128, 1024))
            self.din("gs", (128, NIC))
            self.din("us", (128, NIC))
            self.din("ds_row", (1, D))
        if leg == "B":
            self.t["guB"] = self.nc.dram_tensor("guB", [NIC, 128, 2 * D], BF16, kind="Internal").ap()
            self.t["dB"] = self.nc.dram_tensor("dB", [4, 22, 128, 1024], BF16, kind="Internal").ap()
        if leg in ("B", "C"):
            self.t["woB"] = self.nc.dram_tensor("woB", [4, 128, NKC * 512], BF16, kind="Internal").ap()
        if leg == "B":
            self.t["wq1B"] = self.nc.dram_tensor("wq1B", [NKC, 128, D], BF16, kind="Internal").ap()
            self.t["wk1B"] = self.nc.dram_tensor("wk1B", [4, 128, D], BF16, kind="Internal").ap()
            self.t["wv1B"] = self.nc.dram_tensor("wv1B", [1, 128, NKC * 512], BF16, kind="Internal").ap()
        if leg == "A":
            self.din("wq", (NKC, 128, D))
            self.din("wk", (NKC, 128, D))
            self.din("wv", (4, 128, NKC * 512))
            self.din("bq", (128, NKC))
            self.din("bk", (128, NKC))
            self.din("bv_row", (1, D))
            self.dout("x_out", (TPC, D))
            self.dout("KT_out", (D, TPC), BF16)
            self.dout("V_out", (TPC, D), BF16)
            self.dout("QT_out", (D, TPC), BF16)
        if leg in ("B", "C"):
            dkv = D if leg == "B" else DKV1
            self.din("QT_in", (D, TPC), BF16)
            self.din("KT_full", (4 * dkv, TPC), BF16)
            self.din("V_full", (4 * TPC, dkv), BF16)
            self.din("wo", (4, 128, NKC * 512))
            self.din("bo_row", (1, D))
            self.din("w1", (4, 128, NKC * 512))
            self.din("b1_row", (1, D))
            self.dout("x_out", (TPC, D))
        if leg == "B":
            self.din("wq1", (NKC, 128, D))
            self.din("wk1", (4, 128, D))
            self.din("wv1", (1, 128, NKC * 512))
            self.din("bq1", (128, NKC))
            self.din("bk1", (128, 4))
            self.din("bv1_row", (1, DKV1))
            self.din("cos_t", (128, TPC))
            self.din("sin_t", (128, TPC))
            self.dout("QT_out", (D, TPC), BF16)
            self.dout("KT_out", (DKV1, TPC), BF16)
            self.dout("V_out", (TPC, DKV1), BF16)
        if leg == "C":
            self.din("cmask", (32, 128, 128), BF16)
            self.din("ksel", (1, 256))

        r = self.res
        r("f", "x", NT * D)
        r("f", "stat", 64)
        r("f", "bstat", 32)
        r("f", "scrF", 2048)
        r("f", "scr0", 512, at="scrF")
        r("f", "scr1", 512, at="scrF", at_off=512)
        r("f", "sq", 2048, at="scrF")
        r("f", "qf", 1024, at="scrF", at_off=1024)
        r("f", "hiloF", 1024, at="scrF")
        r("f", "rows", 512)
        r("b", "onesr", 128)
        r("b", "halfr", 128)
        r("b", "onesc", 1)
        r("b", "onesF", 128)
        r("b", "ident", 128)
        r("b", "browB", 2048)
        r("b", "hilo", 1024)
        r("b", "shiftr", 1024, at="hilo")
        r("b", "sbf", 512)
        if leg in ("A", "B"):
            r("f", "ds_bc", 2048)
            r("f", "ds_bc2", 2048, at="ds_bc")
            r("f", "gsc", NIC)
            r("f", "usc", NIC)
        if leg == "A":
            r("b", "xnT", NKC * TPC)
            r("b", "wgu", 2 * 2 * D)
            r("b", "hdn", NIC * 512)
            r("b", "dwn", 2 * 2048)
            r("b", "stage", 2 * D, at="hdn", at_off=8192)
            r("b", "xn", 2048, at="stage")
            r("b", "wproj", 2 * D, at="wgu")
            r("b", "wvsl", NKC * 512, at="hdn")
            self.tok_gates = {"dwn_gate": None}
        if leg == "B":
            r("f", "cs_c", 1024, at="ds_bc")
            r("f", "cs_s", 1024, at="ds_bc", at_off=1024)
            r("f", "rzbc", 1536)
            r("f", "qswap", 1024, at="rzbc")
            r("f", "w1f", 1024, at="rzbc")
            r("f", "abc", 512, at="rzbc", at_off=1024)
            r("b", "QT", NKC * TPC)
            r("b", "xnT", NKC * TPC, at="QT")
            r("b", "onT", NKC * TPC)
            r("b", "at0", 3 * 1024)
            r("b", "ksq", 512)
            r("b", "kth", 2 * 4096)
            r("b", "vth", 2 * 4096)
            r("b", "stage", 2 * D, at="onT")
            r("b", "xn", 2048, at="stage")
            self.m["hdn"] = ("b", self.m["onT"][1], NIC * 512)
            assert self.m["at0"][1] == self.m["onT"][1] + 16384
            assert self.m["ksq"][1] == self.m["onT"][1] + 19456
            assert self.m["kth"][1] == self.m["onT"][1] + 19968
            self.m["wgu"] = ("b", self.m["kth"][1] + 4096, 2 * 2 * D)
            assert self.m["vth"][1] == self.m["kth"][1] + 8192
            self.m["dwn"] = ("b", self.m["vth"][1] + 4096, 2 * 2048)
            r("b", "w1sl", NKC * 512, at="kth")
            r("b", "wosl", NKC * 512, at="kth")
            r("b", "wproj", 2 * D, at="kth")
            r("b", "wvsl", NKC * 512, at="kth")
            r("b", "bcol", 6)
            r("f", "sbias", 4)
        if leg == "C":
            r("f", "kselb", 256)
            r("f", "lnkr", 1024, at="scrF", at_off=1024)
            r("f", "lnkr2", 512, at="scrF", at_off=1024)
            r("f", "rzbc", 1536)
            r("f", "w1f", 1024, at="rzbc")
            r("f", "abc", 512, at="rzbc", at_off=1024)
            r("b", "QT", NKC * TPC)
            r("b", "xnT", NKC * TPC, at="QT")
            r("b", "onT", NKC * TPC)
            r("b", "at0", 2 * 1024)
            r("b", "ksq", 512)
            r("b", "kth", 2 * 4096)
            r("b", "vth", 2 * 4096)
            r("b", "cm", 32 * 128)
            r("b", "stage", 2 * D, at="onT")
            r("b", "xn", 2048, at="stage")
            r("b", "w1sl", NKC * 512, at="kth")
            r("b", "wosl", NKC * 512, at="kth")
            r("b", "wosl2", NKC * 512, at="vth")

        with ExitStack() as st:
            nfc = max(1, self.off["f"])
            nbc = max(1, self.off["b"])
            assert nfc * 4 + nbc * 2 <= 219000, (leg, nfc * 4, nbc * 2)
            self.f32a = st.enter_context(self.nc.sbuf_tensor("f32a", [128, nfc], F32))
            self.bfa = st.enter_context(self.nc.sbuf_tensor("bfa", [128, nbc], BF16))
            self.ps_a = st.enter_context(self.nc.psum_tensor("ps_a", [128, 2048], F32))
            self.ps_b = st.enter_context(self.nc.psum_tensor("ps_b", [128, 1024], F32))
            self.ps_c = st.enter_context(self.nc.psum_tensor("ps_c", [128, 512], F32))
            self.ps_t = st.enter_context(self.nc.psum_tensor("ps_t", [128, 512], F32))
            self.ps_tb = self.ps_t.ap().bitcast(BF16)
            sems = {e: st.enter_context(self.nc.semaphore(f"s_{e}")) for e in ENGINES}
            block = st.enter_context(self.nc.Block())
            self._sem_stack = st

            self.consts()
            if self.leg == "A":
                self.emit_ln("ln1")
                self.dump("d_xnT", "xnT", NKC * TPC, bf=True)
                self.dump("d_stat", "stat", 64)
                self.emit_ffn0()
                self.dump("d_ds", "ds_bc2", 2048)
                self.dump("d_dwn", "dwn", 2 * 2048, bf=True)
                self.dump("d_hdn", "hdn", NIC * 512, bf=True)
                self.dump("d_x", "x", NT * D)
                self.emit_ln("ln2")
                self.tok["wproj_gate"] = self.tok["gu_last"]
                self.emit_proj_fm("wk", "bk", NKC, "KT_out", 0)
                self.tok["wproj_gate"] = self.tok["fm_last"]
                self.emit_proj_fm("wq", "bq", NKC, "QT_out", 16, scale=1.0 / math.sqrt(HD))
                self.tok["wvsl_gate"] = self.tok["dw_last"]
                bv = self.load_row_bf("bv_row", 0, D, deps=[])
                self.emit_proj_tm("wv", "wvsl", bv, "V_out", D)
                self.store_x()
            elif self.leg == "B":
                if PRECAST:
                    pc_last = {}
                    for n4 in range(4):
                        pc_last[n4] = self.pb.add("gpsimd", lambda e, n4=n4: e.dma_start(
                            out=self.t["woB"][n4], in_=self.t["wo"][n4]),
                            dma=True, phased=False, sem=f"pc{n4}")
                    self.tok["precast_wo"] = [pc_last[k] for k in range(4)]
                    for ic in range(NIC):
                        pc_last[ic % 4] = self.pb.add("gpsimd", lambda e, ic=ic: e.dma_start(
                            out=self.t["guB"][ic], in_=self.t["guA"][ic]),
                            dma=True, phased=False, sem=f"pc{ic % 4}")
                    for n4 in range(4):
                        for g11 in range(22):
                            k4 = (n4 * 22 + g11) % 4
                            pc_last[k4] = self.pb.add("gpsimd", lambda e, n4=n4, g11=g11, k4=k4: e.dma_start(
                                out=self.t["dB"][n4, g11], in_=self.t["dA"][n4, g11]),
                                dma=True, phased=False, sem=f"pc{k4}")
                    self.tok["precast"] = [pc_last[k] for k in range(4)]
                self.emit_attention(0)
                self.tok["wgu_gate"] = self.tok[("onT_ev", NH - 1, 1)]
                self.tok["dwn_gate"] = self.tok[("onT_ev", NH - 1, 1)]
                self.tok["wosl_gate"] = self.tok[("onT_ev", NH - 1, 1)]
                self.tok["w1f_gate"] = self.tok[("onT_ev", NH - 1, 1)]
                self.dump("d_xat", "x", NT * D)
                self.emit_out_proj("wo", "bo_row")
                self.dump("d_xop", "x", NT * D)
                self.emit_ln("ln3")
                self.emit_ffn0()
                self.dump("d_wgu", "wgu", 2 * 2 * D, bf=True)
                self.dump("d_dwn", "dwn", 2 * 2048, bf=True)
                self.dump("d_xf0", "x", NT * D)
                self.emit_ln("ln4")
                self.emit_ffn1()
                self.emit_ln("ln5")
                csc = self.pb.add("sync", lambda e: e.dma_start(
                    out=self.V("cs_c"), in_=self.t["cos_t"]), dma=True, phased=False,
                    deps=list(self.pb.phase_deps), sem="cs")
                css = self.pb.add("sync", lambda e: e.dma_start(
                    out=self.V("cs_s"), in_=self.t["sin_t"]), dma=True, phased=False,
                    deps=[csc], sem="cs")
                self.tok["cs"] = css
                self.tok["wproj_gate"] = self.tok["b1_last"]
                self.emit_proj_fm("wk1", "bk1", 4, "KT_out", 0, rope=True)
                self.tok["wproj_gate"] = self.tok["fm_last"]
                self.emit_proj_fm("wq1", "bq1", NKC, "QT_out", 16, rope=True)
                self.tok["wvsl_gate"] = self.tok["fm_last"]
                bv = self.load_row_bf("bv1_row", 0, DKV1, deps=[self.tok.get("b1_last")])
                self.emit_proj_tm("wv1", "wvsl", bv, "V_out", DKV1)
                self.store_x()
            else:
                if PRECAST and PRECAST_CONSUME:
                    pcw = []
                    for n4 in range(4):
                        pcw.append(self.pb.add("gpsimd", lambda e, n4=n4: e.dma_start(
                            out=self.t["woB"][n4], in_=self.t["wo"][n4]),
                            dma=True, phased=False, sem=f"pc{n4}"))
                    self.tok["precast_wo"] = pcw
                self.emit_attention(1)
                self.tok["wosl_gate"] = self.tok[("onT_ev", NH - 1, 1)]
                self.tok["w1f_gate"] = self.tok[("onT_ev", NH - 1, 1)]
                self.emit_out_proj("wo", "bo_row")
                self.emit_ln("ln6")
                self.emit_ffn1()
                self.store_x()

            lsems = {lbl: st.enter_context(self.nc.semaphore(f"d_{lbl}"))
                     for lbl in self.pb.label_count}
            self.pb.render(self.nc, block, sems, lsems)
        return self.nc



_cache = {}
_nc_cache = {}


def rope_perm(nh):
    p = []
    for h in range(nh):
        base = h * HD
        p += [base + i for i in range(0, HD, 2)]
        p += [base + i for i in range(1, HD, 2)]
    return np.array(p, dtype=np.int64)


def _host_prep(inputs):
    if "done" in _cache:
        return _cache
    c = _cache
    f32 = lambda a: np.ascontiguousarray(np.asarray(a, dtype=np.float32))

    gate_w = f32(inputs["L0_ffn_gate_w"])
    up_w = f32(inputs["L0_ffn_up_w"])
    down_w = f32(inputs["L0_ffn_down_w"])

    def lhsT_ic(w):
        a = w.reshape(NIC, 128, NKC, 128)
        return np.ascontiguousarray(a.transpose(0, 3, 2, 1)).reshape(NIC, 128, D)
    c["guA"] = np.ascontiguousarray(np.concatenate([lhsT_ic(gate_w), lhsT_ic(up_w)], axis=2))
    dT = np.ascontiguousarray(down_w.T)
    a = dT.reshape(22, 2, 128, 4, 512)
    c["dA"] = np.ascontiguousarray(a.transpose(3, 0, 2, 1, 4)).reshape(4, 22, 128, 1024)
    c["gs"] = np.ascontiguousarray(f32(inputs["L0_ffn_gate_s"]).reshape(NIC, 128).T)
    c["us"] = np.ascontiguousarray(f32(inputs["L0_ffn_up_s"]).reshape(NIC, 128).T)
    c["ds_row"] = f32(inputs["L0_ffn_down_s"]).reshape(1, D)

    def lhsT_m(w):
        K, M = w.shape
        a = w.reshape(K // 128, 128, M // 128, 128)
        return np.ascontiguousarray(a.transpose(2, 1, 0, 3)).reshape(M // 128, 128, K)

    def rhs_nsl(w, nsl):
        K, N = w.shape
        ns = N // nsl
        a = w.reshape(K // 128, 128, nsl, ns)
        return np.ascontiguousarray(a.transpose(2, 1, 0, 3)).reshape(nsl, 128, (K // 128) * ns)

    c["wq"] = lhsT_m(f32(inputs["L0_attn_Wq"]))
    c["wk"] = lhsT_m(f32(inputs["L0_attn_Wk"]))
    c["wv"] = rhs_nsl(f32(inputs["L0_attn_Wv"]), 4)
    c["bq"] = np.ascontiguousarray(f32(inputs["L0_attn_bq"]).reshape(NKC, 128).T)
    c["bk"] = np.ascontiguousarray(f32(inputs["L0_attn_bk"]).reshape(NKC, 128).T)
    c["bv_row"] = f32(inputs["L0_attn_bv"]).reshape(1, D)
    c["wo"] = rhs_nsl(f32(inputs["L0_attn_Wo"]), 4)
    c["bo_row"] = f32(inputs["L0_attn_bo"]).reshape(1, D)

    c["w1"] = rhs_nsl(np.ascontiguousarray(f32(inputs["L1_ffn_weight"]).T), 4)
    c["b1_row"] = f32(inputs["L1_ffn_bias"]).reshape(1, D)

    pq = rope_perm(NH)
    pk = rope_perm(4)
    c["wq1"] = lhsT_m(f32(inputs["L1_attn_Wq"])[:, pq])
    c["bq1"] = np.ascontiguousarray(f32(inputs["L1_attn_bq"])[pq].reshape(NKC, 128).T)
    c["wk1"] = lhsT_m(f32(inputs["L1_attn_Wk"])[:, pk])
    c["bk1"] = np.ascontiguousarray(f32(inputs["L1_attn_bk"])[pk].reshape(4, 128).T)
    c["wv1"] = rhs_nsl(f32(inputs["L1_attn_Wv"]), 1)
    c["bv1_row"] = f32(inputs["L1_attn_bv"]).reshape(1, DKV1)
    c["wo1"] = rhs_nsl(f32(inputs["L1_attn_Wo"]), 4)
    c["bo1_row"] = f32(inputs["L1_attn_bo"]).reshape(1, D)

    c["ones_row"] = np.ones((1, 128), dtype=BF)
    c["half_row"] = np.full((1, 128), 0.5, dtype=BF)
    c["ones_col"] = np.ones((128, 1), dtype=BF)
    c["ones_full"] = np.ones((128, 128), dtype=BF)
    c["ident"] = np.eye(128, dtype=np.float32).astype(BF)

    freqs = 10000.0 ** (-(np.arange(0, 64, dtype=np.float64) * 2.0 / D))
    cos_c, sin_c, cmask_c, ksel_c = [], [], [], []
    for p in range(4):
        pos = np.concatenate([np.arange(128 * (p + 4 * j), 128 * (p + 4 * j) + 128) for j in range(8)])
        ang = np.outer(freqs, pos.astype(np.float64))
        cc = np.cos(ang).astype(np.float32)
        ss = np.sin(ang).astype(np.float32)
        cos_c.append(np.ascontiguousarray(np.concatenate([cc, cc], axis=0)))
        sin_c.append(np.ascontiguousarray(np.concatenate([ss, ss], axis=0)))
        ks = np.full((8, 32), -1e9, dtype=np.float32)
        for jq in range(8):
            for kt in range(32):
                gk = (kt // 8) + 4 * (kt % 8)
                if gk <= p + 4 * jq:
                    ks[jq, kt] = 0.0
        ksel_c.append(np.ascontiguousarray(ks.reshape(1, 256)))
        msk = np.zeros((32, 128, 128), dtype=np.float32)
        for kt in range(32):
            rr, jk = kt // 8, kt % 8
            gk = rr + 4 * jk
            jb = gk // 4
            gq = p + 4 * jb
            if gk > gq:
                msk[kt, :, :] = -1e30
            elif gk == gq:
                kk = np.arange(128)[:, None]
                qq = np.arange(128)[None, :]
                msk[kt][kk > qq] = -1e30
        cmask_c.append(np.ascontiguousarray(msk.astype(BF)))
    c["cos_t"] = cos_c
    c["sin_t"] = sin_c
    c["cmask"] = cmask_c
    c["ksel"] = ksel_c
    c["done"] = True
    return c


def _shard_x(x):
    out = []
    for cid in range(8):
        b, p = cid // 4, cid % 4
        rows = np.concatenate([x[b, 128 * (p + 4 * j):128 * (p + 4 * j) + 128] for j in range(8)])
        out.append(np.ascontiguousarray(rows))
    return out


def _gather_kv(KTs, Vs):
    ktf, vf = [], []
    for grp in range(2):
        kt = np.ascontiguousarray(np.concatenate([KTs[grp * 4 + r] for r in range(4)], axis=0))
        v = np.ascontiguousarray(np.concatenate([Vs[grp * 4 + r] for r in range(4)], axis=0))
        for r in range(4):
            ktf.append(kt)
            vf.append(v)
    return ktf, vf


def _get_nc(leg):
    if leg not in _nc_cache:
        _nc_cache[leg] = Leg(leg).build()
    return _nc_cache[leg]


def run_leg_a(c, xs):
    ncA = _get_nc("A")
    in_maps = []
    for cid in range(8):
        m = {k: c[k] for k in ("ones_row", "half_row", "ones_col", "ones_full", "ident",
                               "guA", "dA", "gs", "us", "ds_row",
                               "wq", "wk", "wv", "bq", "bk", "bv_row")}
        m["x_in"] = xs[cid]
        in_maps.append(m)
    return run_bass_kernel_spmd(ncA, in_maps, core_ids=list(range(8)))


def run_leg_b(c, x1, QT0, ktf, vf):
    ncB = _get_nc("B")
    in_maps = []
    for cid in range(8):
        m = {k: c[k] for k in ("ones_row", "half_row", "ones_col", "ones_full", "ident",
                               "guA", "dA", "gs", "us", "ds_row",
                               "wo", "bo_row", "w1", "b1_row",
                               "wq1", "bq1", "wk1", "bk1", "wv1", "bv1_row")}
        m["x_in"] = x1[cid]
        m["QT_in"] = QT0[cid]
        m["KT_full"] = ktf[cid]
        m["V_full"] = vf[cid]
        m["cos_t"] = c["cos_t"][cid % 4]
        m["sin_t"] = c["sin_t"][cid % 4]
        in_maps.append(m)
    return run_bass_kernel_spmd(ncB, in_maps, core_ids=list(range(8)))


def run_leg_c(c, x2, QT1, ktf1, vf1):
    ncC = _get_nc("C")
    in_maps = []
    for cid in range(8):
        m = {k: c[k] for k in ("ones_row", "half_row", "ones_col", "ones_full", "ident", "b1_row", "w1")}
        m["wo"] = c["wo1"]
        m["bo_row"] = c["bo1_row"]
        m["x_in"] = x2[cid]
        m["QT_in"] = QT1[cid]
        m["KT_full"] = ktf1[cid]
        m["V_full"] = vf1[cid]
        m["cmask"] = c["cmask"][cid % 4]
        m["ksel"] = c["ksel"][cid % 4]
        in_maps.append(m)
    return run_bass_kernel_spmd(ncC, in_maps, core_ids=list(range(8)))


def kernel(**inputs):
    c = _host_prep(inputs)
    x = np.asarray(inputs["L0_attn_X"], dtype=np.float32)
    xs = _shard_x(x)

    resA = run_leg_a(c, xs)
    x1 = [np.asarray(resA.results[i]["x_out"]) for i in range(8)]
    KT0 = [np.asarray(resA.results[i]["KT_out"]) for i in range(8)]
    V0 = [np.asarray(resA.results[i]["V_out"]) for i in range(8)]
    QT0 = [np.asarray(resA.results[i]["QT_out"]) for i in range(8)]
    ktf, vf = _gather_kv(KT0, V0)

    resB = run_leg_b(c, x1, QT0, ktf, vf)
    x2 = [np.asarray(resB.results[i]["x_out"]) for i in range(8)]
    QT1 = [np.asarray(resB.results[i]["QT_out"]) for i in range(8)]
    KT1 = [np.asarray(resB.results[i]["KT_out"]) for i in range(8)]
    V1 = [np.asarray(resB.results[i]["V_out"]) for i in range(8)]
    ktf1, vf1 = _gather_kv(KT1, V1)

    resC = run_leg_c(c, x2, QT1, ktf1, vf1)
    y = [np.asarray(resC.results[i]["x_out"]) for i in range(8)]

    out = np.empty((B, S, D), dtype=np.float32)
    for cid in range(8):
        b, p = cid // 4, cid % 4
        for j in range(8):
            out[b, 128 * (p + 4 * j):128 * (p + 4 * j) + 128] = y[cid][j * 128:(j + 1) * 128]
    return out
